# version 31
# baseline (speedup 1.0000x reference)
import os
from contextlib import ExitStack

import numpy as np
import ml_dtypes

import concourse.bass as bass
import concourse.mybir as mybir
from concourse.bass_utils import run_bass_kernel_spmd
from concourse.alu_op_type import AluOpType as ALU

F32 = mybir.dt.float32
BF16 = mybir.dt.bfloat16
F8E5 = mybir.dt.float8e5
AF = mybir.ActivationFunctionType
AX = mybir.AxisListType.X

D = 1024
H = 16
NCB = D // 128
HID = 4096
NHB = HID // 128
C_SHIFT = 72.0
SCALE = 0.125


class Sem:
    def __init__(self, stack, nc, name):
        self.h = stack.enter_context(nc.semaphore(name))
        self.n = 0

    def inc(self, instr, k=1):
        instr.then_inc(self.h, k)
        self.n += k
        return instr

    def dinc(self, instr):
        if self.n > 0:
            instr.wait_op(self.h, self.n, "sem-ge")
        instr.then_inc(self.h, 16)
        self.n += 16
        return instr


def build_nc(T=2048):
    assert T % 512 == 0
    NTB = T // 128
    NQS = T // 512
    NKC = NTB
    nc = bass.Bass("TRN2", target_bir_lowering=False)

    x_h = nc.dram_tensor("x", [T, D], F32, kind="ExternalInput").ap()
    mask_h = nc.dram_tensor("mask", [T, D], BF16, kind="ExternalInput").ap()
    wqk_h = nc.dram_tensor("wqk", [H, 128, D], F32, kind="ExternalInput").ap()
    wv_h = nc.dram_tensor("wv", [NCB, 128, D], F32, kind="ExternalInput").ap()
    wpj_h = nc.dram_tensor("wpj", [NCB, 128, D], F32, kind="ExternalInput").ap()
    w1_h = nc.dram_tensor("w1", [NHB, 128, D], F32, kind="ExternalInput").ap()
    w2_h = nc.dram_tensor("w2", [NHB, 128, D], F32, kind="ExternalInput").ap()
    b1_h = nc.dram_tensor("b1t", [128, NHB], F32, kind="ExternalInput").ap()
    b2_h = nc.dram_tensor("b2t", [128, D], F32, kind="ExternalInput").ap()
    bp_h = nc.dram_tensor("bpt", [128, D], F32, kind="ExternalInput").ap()
    out_h = nc.dram_tensor("out", [T, D], F32, kind="ExternalOutput").ap()
    ident_h = nc.inline_tensor(np.eye(128, dtype=np.float32), name="ident_h").ap()
    ones_h = nc.inline_tensor(np.ones((128, 128), dtype=np.float32), name="ones_h").ap()

    es = ExitStack()
    sems = ExitStack()

    def sb(stack, name, shape, dt, side="left"):
        return stack.enter_context(nc.sbuf_tensor(name, shape, dt, side=side))

    consts = sb(es, "consts", [128, 8], F32)
    CONST_COLS = {}
    for v in (1.0 / D, -1.0 / D, 1e-5, SCALE, -C_SHIFT, 1.0):
        CONST_COLS[v] = len(CONST_COLS)
        nc.const_aps.register(F32, v, consts[:, CONST_COLS[v]:CONST_COLS[v] + 1])

    ident = sb(es, "ident", [128, 128], F32)
    ones_sb = sb(es, "ones_sb", [128, 128], F32)
    b1q = sb(es, "b1q", [128, NHB], F32)
    b1q_e5 = sb(es, "b1q_e5", [128, NHB], F8E5)
    b2q = sb(es, "b2q", [128, D], F32)
    b2q_e5 = sb(es, "b2q_e5", [128, D], F8E5)
    bp_sb = sb(es, "bp_sb", [128, D], F32)
    XST_LO = -(-nc.sbuf_base // 32) * 32
    x_st = sb(es, "x_st", [128, 3 * D], F32)
    wst = sb(es, "wst", [128, 2 * D], F32)
    stat = sb(es, "stat", [128, 18], F32)

    psA = es.enter_context(nc.psum_tensor("psA", [128, 2048], F32))
    psB = es.enter_context(nc.psum_tensor("psB", [128, 2048], F32))

    s_x = Sem(sems, nc, "s_x")
    s_w = Sem(sems, nc, "s_w")
    s_v = Sem(sems, nc, "s_v")
    s_s = Sem(sems, nc, "s_s")
    s_t = Sem(sems, nc, "s_t")
    s_o = Sem(sems, nc, "s_o")
    s_f = Sem(sems, nc, "s_f")

    block = es.enter_context(nc.Block())

    def em(engname, f):
        getattr(block, engname)(f)

    wst_chain = []

    def wst_wait(e, idx):
        if idx >= 2:
            sem, val = wst_chain[idx - 2]
            e.wait_ge(sem.h, val)

    def _setup_sync(e):
        s_w.dinc(e.dma_start(out=ident[:], in_=ident_h[:]))
        s_w.dinc(e.dma_start(out=ones_sb[:], in_=ones_h[:]))
        s_w.dinc(e.dma_start(out=b1q[:, 0:NHB], in_=b1_h[:]))
        s_w.dinc(e.dma_start(out=b2q[:], in_=b2_h[:]))
        s_w.dinc(e.dma_start(out=bp_sb[:], in_=bp_h[:]))
    em("sync", _setup_sync)

    def _setup_vec(e):
        for v, j in CONST_COLS.items():
            e.memset(consts[:, j:j + 1], v)
        e.wait_ge(s_w.h, s_w.n)
        e.tensor_copy(b1q_e5[:, 0:NHB], b1q[:, 0:NHB])
        e.tensor_copy(b2q_e5[:], b2q[:])
        e.drain()
        e.tensor_copy(b1q[:, 0:NHB], b1q_e5[:, 0:NHB])
        s_v.inc(e.tensor_copy(b2q[:], b2q_e5[:]))
    em("vector", _setup_vec)
    SETUP_V = s_v.n

    esB = ExitStack()
    xn_e5 = sb(esB, "xn_e5", [128, NCB * T], F8E5)
    esC = ExitStack()
    addr = {}

    def sb_rec(stack, name, shape, dt, side):
        hi = nc.sbuf_top
        t_ = sb(stack, name, shape, dt, side=side)
        addr[name] = (nc.sbuf_top, hi)
        return t_
    vt_bf = sb_rec(esC, "vt_bf", [128, NTB * (H * 65)], BF16, side="right")
    k_bf = sb_rec(esC, "k_bf", [128, NCB * T], BF16, side="right")
    q_bf = sb_rec(esC, "q_bf", [128, NCB * T], BF16, side="right")
    vt_v = vt_bf[:].rearrange("p (tb h d) -> p tb h d", h=H, d=65)
    W1_BYTES = NHB * D
    W1_OVERLAP = (addr["k_bf"][1] - addr["k_bf"][0] == W1_BYTES and
                  addr["vt_bf"][1] - addr["vt_bf"][0] >= W1_BYTES and
                  NQS >= 1 and (H // 2) > 0 and NHB % (H // 2) == 0)
    if W1_OVERLAP:
        w1_e5 = nc.alloc_sbuf_tensor_at(
            "w1_e5", [128, NHB * D], F8E5, offset=addr["k_bf"][0])
        w2_e5 = nc.alloc_sbuf_tensor_at(
            "w2_e5", [128, NHB * D], F8E5, offset=addr["vt_bf"][0])
    esA = ExitStack()
    xn_f32 = sb(esA, "xn_f32", [128, 3 * D], F32, side="right")
    xn_bf = sb(esA, "xn_bf", [128, NCB * T], BF16, side="right")
    wv_bf = sb(esA, "wv_bf", [128, NCB * D], BF16, side="right")
    wqk_b = sb(esA, "wqk_b", [128, 2 * D], BF16, side="right")
    xn_bf_v = xn_bf[:].rearrange("p (c t) -> p c t", t=T)
    xn_e5_v = xn_e5[:].rearrange("p (c t) -> p c t", t=T)

    evac_done = []
    p0_vec_mark = []
    xn_mark = []
    tr_marks = []
    P0_S0 = s_s.n

    def p0_smark(tb):
        return P0_S0 + tb + 1

    wv_cast = []
    v_free = []
    ONES_V_BOX = []
    KWV = max(1, (NCB + NTB - 1) // NTB)
    wv_sched = {}
    for cb_ in range(NCB):
        wv_sched.setdefault(cb_ // KWV, []).append(cb_)
    last_wv_tb = max(wv_sched)
    v_delay = min(last_wv_tb + 2, NTB - 1) if NTB > 1 else 1
    wv_dma_mark = {}
    wv_chain_idx = {}

    def emit_v(j):
        pb = j % 2

        def _tensV(e, j=j, pb=pb):
            if j < 2:
                e.wait_ge(s_f.h, wv_cast[-1])
            else:
                e.wait_ge(s_f.h, v_free[j - 2])
            e.wait_ge(s_f.h, evac_done[j])
            for cb in range(NCB):
                for vs in range(2):
                    i = nc.tensor.matmul(
                        psA[:, pb * 1024 + vs * 512: pb * 1024 + (vs + 1) * 512],
                        xn_bf[:, cb * T + j * 128: cb * T + (j + 1) * 128],
                        wv_bf[:, cb * D + vs * 512: cb * D + (vs + 1) * 512],
                        start=(cb == 0), stop=(cb == NCB - 1))
                    if cb == NCB - 1 and vs == 1:
                        s_t.inc(i)
        em("tensor", _tensV)

        def _vecV(e, j=j, pb=pb):
            e.wait_ge(s_t.h, s_t.n)
            i = e.tensor_copy(
                vt_v[:, j, :, 0:64],
                psA[:, pb * 1024: pb * 1024 + 1024].rearrange("p (h d) -> p h d", d=64))
            s_f.inc(i)
            v_free.append(s_f.n)
        em("vector", _vecV)

    for tb in range(NTB):
        buf = tb % 3
        nbuf = tb % 3
        xs = x_st[:, buf * D:(buf + 1) * D]
        xn = xn_f32[:, nbuf * D:(nbuf + 1) * D]
        sc = (tb % 3) * 6

        def _sync(e, tb=tb, xs=xs):
            if tb >= 3:
                e.wait_ge(s_v.h, xn_mark[tb - 3])
                e.wait_ge(s_s.h, p0_smark(tb - 3))
            s_x.dinc(e.dma_start(out=xs[:], in_=x_h[tb * 128:(tb + 1) * 128, :]))
            for cb in wv_sched.get(tb, ()):
                sidx = len(wst_chain)
                wst_wait(e, sidx)
                s_w.dinc(e.dma_start(out=wst[:, (cb % 2) * D:(cb % 2 + 1) * D],
                                     in_=wv_h[cb]))
                wv_dma_mark[cb] = s_w.n
                wv_chain_idx[cb] = sidx
                wst_chain.append((s_f, None))
        em("sync", _sync)

        if wv_sched.get(tb):
            def _vec_wv(e, cbs=wv_sched[tb]):
                for cb in cbs:
                    e.wait_ge(s_w.h, wv_dma_mark[cb])
                    i = e.tensor_copy(wv_bf[:, cb * D:(cb + 1) * D],
                                      wst[:, (cb % 2) * D:(cb % 2 + 1) * D])
                    s_f.inc(i)
                    wv_cast.append(s_f.n)
                    wst_chain[wv_chain_idx[cb]] = (s_f, s_f.n)
            em("vector", _vec_wv)
        if tb == last_wv_tb + 1 or (tb == last_wv_tb and tb == NTB - 1):
            def _vec_ones(e):
                s_v.inc(e.memset(vt_v[:, :, :, 64:65], 1.0))
            em("vector", _vec_ones)
            ONES_V_BOX.append(s_v.n)

        def emit_ln_stats(tb_):
            sbuf = tb_ % 3
            sxs = x_st[:, sbuf * D:(sbuf + 1) * D]
            ssc = (tb_ % 3) * 6

            def _vec(e, tb_=tb_, sxs=sxs, ssc=ssc):
                if tb_ == 0:
                    e.wait_ge(s_v.h, SETUP_V)
                if tb_ >= 3:
                    e.wait_ge(s_s.h, p0_smark(tb_ - 3))
                e.wait_ge(s_x.h, 16 * (tb_ + 1))
                e.reduce_sum(stat[:, ssc:ssc + 1], sxs[:], axis=AX)
                e.drain()
                i = e.tensor_scalar(stat[:, ssc + 1:ssc + 2], stat[:, ssc:ssc + 1],
                                    -1.0 / D, None, ALU.mult)
                s_v.inc(i)
                p0_vec_mark.append(s_v.n)
            em("vector", _vec)

        if tb == 0:
            emit_ln_stats(0)

        def _vec2(e, tb=tb, xs=xs, xn=xn, sc=sc):
            e.wait_ge(s_s.h, p0_smark(tb))
            e.reciprocal(stat[:, sc + 4:sc + 5], stat[:, sc + 3:sc + 4])
            e.drain()
            s_v.inc(e.tensor_scalar(xn[:], xs[:], stat[:, sc + 1:sc + 2],
                                    stat[:, sc + 4:sc + 5], ALU.add, ALU.mult))
            xn_mark.append(s_v.n)
        em("vector", _vec2)
        if tb + 1 < NTB:
            emit_ln_stats(tb + 1)

        def _tens(e, tb=tb, xn=xn, xm=xn_mark[tb]):
            e.wait_ge(s_v.h, xm)
            half = tb % 2
            if tb >= 2:
                e.wait_ge(s_f.h, evac_done[tb - 2])
            for cb in range(NCB):
                i = nc.tensor.transpose(
                    psB[:, half * 1024 + cb * 128: half * 1024 + (cb + 1) * 128],
                    xn[:, cb * 128:(cb + 1) * 128], ident[:])
                if cb == NCB - 1:
                    s_t.inc(i)
        em("tensor", _tens)
        tr_marks.append(s_t.n)

        def _vec3(e, tb=tb):
            half = tb % 2
            e.wait_ge(s_t.h, s_t.n)
            src = psB[:, half * 1024: half * 1024 + 1024].rearrange(
                "p (c t) -> p c t", t=128)
            e.tensor_copy(xn_bf_v[:, 0:NCB, tb * 128:(tb + 1) * 128], src)
            e.drain()
            i = e.tensor_copy(xn_e5_v[:, 0:NCB, tb * 128:(tb + 1) * 128], src)
            s_f.inc(i)
            evac_done.append(s_f.n)
        em("vector", _vec3)

        if tb >= v_delay:
            emit_v(tb - v_delay)

    for j in range(max(0, NTB - v_delay), NTB):
        emit_v(j)

    def _p0_scal(e):
        for tb in range(NTB):
            buf = tb % 3
            nbuf = tb % 3
            xs = x_st[:, buf * D:(buf + 1) * D]
            xn = xn_f32[:, nbuf * D:(nbuf + 1) * D]
            sc = (tb % 3) * 6
            e.wait_ge(s_v.h, p0_vec_mark[tb])
            if tb >= 3:
                e.wait_ge(s_t.h, tr_marks[tb - 3])
            e.activation(xn[:], xs[:], AF.Square, bias=stat[:, sc + 1:sc + 2],
                         accum_out=stat[:, sc + 2:sc + 3])
            e.drain()
            s_s.inc(e.activation(stat[:, sc + 3:sc + 4], stat[:, sc + 2:sc + 3],
                                 AF.Sqrt, scale=1.0 / D, bias=1e-5))
            assert s_s.n == p0_smark(tb)
    em("scalar", _p0_scal)

    P0_V = s_v.n
    P0_F = s_f.n

    qk_free = []
    fb_last_evac = {}
    for fb in range(H):
        buf = fb % 2
        sidx = len(wst_chain)

        def _sync(e, fb=fb, buf=buf, sidx=sidx):
            wst_wait(e, sidx)
            if fb >= 2:
                e.wait_ge(s_f.h, fb_last_evac[fb - 2])
            s_w.dinc(e.dma_start(out=wst[:, buf * D:(buf + 1) * D], in_=wqk_h[fb]))
        em("sync", _sync)

        def _vec(e, buf=buf):
            e.wait_ge(s_w.h, s_w.n)
            s_v.inc(e.tensor_copy(wqk_b[:, buf * D:(buf + 1) * D],
                                  wst[:, buf * D:(buf + 1) * D]))
        em("vector", _vec)
        cast_mark = s_v.n
        wst_chain.append((s_v, s_v.n))

        for qs in range(NQS):
            g = NQS * fb + qs

            def _tens(e, fb=fb, buf=buf, qs=qs, g=g, cast_mark=cast_mark):
                if qs == 0:
                    e.wait_ge(s_v.h, cast_mark)
                if fb == 0 and qs == 0:
                    e.wait_ge(s_f.h, P0_F)
                if g >= 4:
                    e.wait_ge(s_f.h, qk_free[g - 4])
                for cb in range(NCB):
                    i = nc.tensor.matmul(
                        psB[:, (g % 4) * 512:(g % 4 + 1) * 512],
                        wqk_b[:, buf * D + cb * 128: buf * D + (cb + 1) * 128],
                        xn_bf[:, cb * T + qs * 512: cb * T + (qs + 1) * 512],
                        start=(cb == 0), stop=(cb == NCB - 1))
                s_t.inc(i)
            em("tensor", _tens)

            def _vec2(e, fb=fb, qs=qs, g=g):
                e.wait_ge(s_t.h, s_t.n)
                dst = q_bf if fb < 8 else k_bf
                c0 = (fb % 8) * T + qs * 512
                i = e.tensor_copy(dst[:, c0:c0 + 512],
                                  psB[:, (g % 4) * 512:(g % 4) * 512 + 512])
                s_f.inc(i)
                qk_free.append(s_f.n)
            em("vector", _vec2)
        fb_last_evac[fb] = s_f.n

    ONES_V = ONES_V_BOX[0]
    P1_V = s_v.n
    P1_F = s_f.n

    esA.close()

    esC2 = ExitStack()
    probs = sb(esC2, "probs", [128, 6 * 512], BF16, side="right")
    x_att_tm = nc.alloc_sbuf_tensor_at("x_att_tm", [128, 1024], F32,
                                       offset=XST_LO)
    rcp = nc.alloc_sbuf_tensor_at("rcp", [128, 16], F32, offset=XST_LO + 4096)
    esD = ExitStack()
    attn_bf = sb(esD, "attn_bf", [128, NCB * T], BF16)
    wpj_bf = sb(esD, "wpj_bf", [128, NCB * D], BF16)

    pv_pair_mark = {}
    sc_marks = {}
    pv_done = {}
    norm_mark = {}
    trans_mark = {}
    evac_mark = {}

    def pvoff(rot, t):
        return (2 * rot + t // 4) * 512 + (t % 4) * 65

    def emit_norm_vec(j):
        rot = j % 2

        def _vec(e, j=j, rot=rot):
            e.wait_ge(s_t.h, pv_done[j])
            for b in range(2):
                bank = (2 * rot + b) * 512
                e.reciprocal(
                    rcp[:, rot * 8 + b * 4: rot * 8 + b * 4 + 4].rearrange(
                        "p (t c) -> p t c", c=1),
                    psB[:, bank: bank + 260].rearrange(
                        "p (t c) -> p t c", c=65)[:, :, 64:65])
            e.drain()
            for t in range(8):
                h2, qb = t // 4, t % 4
                i = e.tensor_scalar(
                    x_att_tm[:, rot * 512 + qb * 128 + h2 * 64:
                             rot * 512 + qb * 128 + h2 * 64 + 64],
                    psB[:, pvoff(rot, t): pvoff(rot, t) + 64],
                    rcp[:, rot * 8 + t: rot * 8 + t + 1], None, ALU.mult)
            s_v.inc(i)
            norm_mark[j] = s_v.n
        em("vector", _vec)

    def emit_trans(j):
        rot = j % 2

        def _tens(e, j=j, rot=rot):
            e.wait_ge(s_v.h, norm_mark[j])
            for qb in range(4):
                dst = (2 * rot + qb // 2) * 512 + (qb % 2) * 128
                i = nc.tensor.transpose(
                    psB[:, dst: dst + 128],
                    x_att_tm[:, rot * 512 + qb * 128: rot * 512 + qb * 128 + 128],
                    ident[:])
            s_t.inc(i)
            trans_mark[j] = s_t.n
        em("tensor", _tens)

    def emit_evac(j):
        rot = j % 2
        pj, qsj = j // NQS, j % NQS

        def _vec(e, j=j, rot=rot, pj=pj, qsj=qsj):
            e.wait_ge(s_t.h, trans_mark[j])
            c0 = pj * T + qsj * 512
            i = e.tensor_copy(
                attn_bf[:, c0:c0 + 512].rearrange("p (b c) -> p b c", c=256),
                psB[:, 2 * rot * 512: 2 * rot * 512 + 1024].rearrange(
                    "p (b c) -> p b c", c=512)[:, :, 0:256])
            s_v.inc(i)
            evac_mark[j] = s_v.n
        em("vector", _vec)

    wpj_cast = []
    for cb in range(NCB):
        buf = cb % 2
        sidx = len(wst_chain)

        def _sync(e, cb=cb, buf=buf, sidx=sidx):
            wst_wait(e, sidx)
            s_w.dinc(e.dma_start(out=wst[:, buf * D:(buf + 1) * D], in_=wpj_h[cb]))
        em("sync", _sync)

        def _vec(e, cb=cb, buf=buf):
            e.wait_ge(s_w.h, s_w.n)
            i = e.tensor_copy(wpj_bf[:, cb * D:(cb + 1) * D],
                              wst[:, buf * D:(buf + 1) * D])
            s_f.inc(i)
            wpj_cast.append(s_f.n)
        em("vector", _vec)
        wst_chain.append((s_f, s_f.n))
    WPJ_F = s_f.n

    wcast_mark = {}
    w1_dma_mark = {}
    W1_PER_PAIR = NHB // (H // 2) if W1_OVERLAP else 0

    w1_chain_idx = {}

    def emit_w1dma_p2(j):
        sidx = len(wst_chain)
        w1_chain_idx[j] = sidx

        def _sync(e, j=j, sidx=sidx):
            wst_wait(e, sidx)
            s_w.dinc(e.dma_start(out=wst[:, (j % 2) * D:(j % 2 + 1) * D],
                                 in_=w1_h[j]))
            w1_dma_mark[j] = s_w.n
        em("sync", _sync)
        wst_chain.append(None)

    def emit_w1cast_p2(j, pair_gate):
        def _vec(e, j=j, pair_gate=pair_gate):
            e.wait_ge(s_w.h, w1_dma_mark[j])
            e.wait_ge(s_t.h, pair_gate)
            i = e.tensor_copy(w1_e5[:, j * D:(j + 1) * D],
                              wst[:, (j % 2) * D:(j % 2 + 1) * D])
            s_v.inc(i)
            wcast_mark[('w1', j)] = s_v.n
            wst_chain[w1_chain_idx[j]] = (s_v, s_v.n)
        em("vector", _vec)

    P2_S0 = s_s.n

    def exp_mark(it, kc):
        return P2_S0 + it * NKC + kc + 1
    if W1_OVERLAP:
        emit_w1dma_p2(0)
        emit_w1dma_p2(1)
    w1_pending = []

    def emit_w1block(base, gate):
        emit_w1cast_p2(base, gate)
        emit_w1cast_p2(base + 1, gate)
        for j in (base + 2, base + 3):
            if j < NHB:
                emit_w1dma_p2(j)
        if base + 2 < NHB:
            emit_w1cast_p2(base + 2, gate)
        if base + 3 < NHB:
            emit_w1cast_p2(base + 3, gate)
        for j in (base + 4, base + 5):
            if j < NHB:
                emit_w1dma_p2(j)
    NG = (H // 2) * NQS * NKC

    def emit_scores_g(g):
        it_, kc = divmod(g, NKC)
        p_, qs_ = divmod(it_, NQS)
        rot = g % 2
        scA = psA[:, rot * 1024: rot * 1024 + 512]
        scB = psA[:, rot * 1024 + 512: rot * 1024 + 1024]

        def _tens(e, g=g, it_=it_, kc=kc, p_=p_, qs_=qs_, scA=scA, scB=scB):
            if g == 0:
                e.wait_ge(s_v.h, ONES_V)
                e.wait_ge(s_f.h, P1_F)
            if g >= 2:
                e.wait_ge(s_s.h, P2_S0 + g - 1)
            nc.tensor.matmul(
                scA, k_bf[0:64, p_ * T + kc * 128: p_ * T + (kc + 1) * 128],
                q_bf[0:64, p_ * T + qs_ * 512: p_ * T + (qs_ + 1) * 512],
                start=True, stop=True, tile_position=(0, 0))
            i = nc.tensor.matmul(
                scB, k_bf[64:128, p_ * T + kc * 128: p_ * T + (kc + 1) * 128],
                q_bf[64:128, p_ * T + qs_ * 512: p_ * T + (qs_ + 1) * 512],
                start=True, stop=True, tile_position=(64, 0))
            s_t.inc(i)
        em("tensor", _tens)
        it_, kc = divmod(g, NKC)
        sc_marks[(it_, kc)] = s_t.n

    emit_scores_g(0)
    emit_scores_g(1)
    for p in range(H // 2):
        for qs in range(NQS):
            it = p * NQS + qs

            def emit_pv(kc, p=p, it=it):
                rot3 = (it * NKC + kc) % 3
                rot = it % 2

                def _tens2(e, kc=kc, rot3=rot3, rot=rot):
                    e.wait_ge(s_s.h, exp_mark(it, kc))
                    if kc == 0 and it >= 2:
                        e.wait_ge(s_v.h, evac_mark[it - 2])
                    for t in range(8):
                        h2, qb = t // 4, t % 4
                        i = nc.tensor.matmul(
                            psB[:, pvoff(rot, t): pvoff(rot, t) + 65],
                            probs[:, rot3 * 1024 + h2 * 512 + qb * 128:
                                  rot3 * 1024 + h2 * 512 + qb * 128 + 128],
                            vt_bf[:, kc * (65 * H) + (2 * p + h2) * 65:
                                  kc * (65 * H) + (2 * p + h2) * 65 + 65],
                            start=(kc == 0 and t % 4 == 0),
                            stop=(kc == NKC - 1 and t % 4 == 3))
                    s_t.inc(i)
                    pv_pair_mark[(it, kc)] = s_t.n
                em("tensor", _tens2)

            if it >= 1:
                emit_norm_vec(it - 1)
            for kc in range(NKC):
                if kc == 2 and it >= 1:
                    emit_trans(it - 1)
                    emit_evac(it - 1)
                if kc == 3 and w1_pending:
                    emit_w1block(*w1_pending.pop(0))
                g = it * NKC + kc + 2
                if g < NG:
                    emit_scores_g(g)
                emit_pv(kc)
            pv_done[it] = s_t.n

            if W1_OVERLAP and qs == NQS - 1:
                w1_pending.append((W1_PER_PAIR * p, sc_marks[(it, NKC - 1)]))
    li = (H // 2) * NQS - 1
    emit_norm_vec(li)
    emit_trans(li)
    emit_evac(li)
    P2_V_CORE = s_v.n
    while w1_pending:
        emit_w1block(*w1_pending.pop(0))

    def _p2_scal(e):
        for it in range((H // 2) * NQS):
            for kc in range(NKC):
                rot2 = kc % 2
                rot3 = (it * NKC + kc) % 3
                e.wait_ge(s_t.h, sc_marks[(it, kc)])
                pit, pkc = (it, kc - 3) if kc >= 3 else (it - 1, NKC - 3 + kc)
                if (pit, pkc) in pv_pair_mark:
                    e.wait_ge(s_t.h, pv_pair_mark[(pit, pkc)])
                i = e.activation(probs[:, rot3 * 1024:(rot3 + 1) * 1024],
                                 psA[:, rot2 * 1024:(rot2 + 1) * 1024],
                                 AF.Exp, bias=-C_SHIFT, scale=SCALE)
                s_s.inc(i)
                assert s_s.n == exp_mark(it, kc)
    em("scalar", _p2_scal)

    P2_V = s_v.n
    P2_T = s_t.n

    esC2.close()
    esC.close()
    esG = ExitStack()
    h_e5 = sb(esG, "h_e5", [128, NHB * 512], F8E5)
    h_bfst = sb(esG, "h_bfst", [128, 4 * 512], BF16)
    y2_bf = sb(esG, "y2_bf", [128, D], BF16)
    y2_f32 = sb(esG, "y2_f32", [128, D], F32)
    mask_st = sb(esG, "mask_st", [128, 2 * D], BF16)
    out_st = sb(esG, "out_st", [128, 2 * D], F32)
    esF = ExitStack()
    if W1_OVERLAP:
        wst2 = sb(esF, "wst2", [128, 2 * D], F32)
        wst2b = wst
    else:
        wst2 = sb(esF, "wst2", [128, 4 * D], F32)
        wst2b = None

    def wst2_buf(i_):
        r = i_ % 4
        if W1_OVERLAP:
            t_ = wst2 if r < 2 else wst2b
            return t_[:, (r % 2) * D:(r % 2 + 1) * D]
        return wst2[:, r * D:(r + 1) * D]
    if W1_OVERLAP:
        assert nc.sbuf_base <= addr["k_bf"][0], (nc.sbuf_base, addr)
    else:
        w1_e5 = sb(esF, "w1_e5", [128, NHB * D], F8E5)
        w2_e5 = sb(esF, "w2_e5", [128, NHB * D], F8E5)

    NSTR = T // 512
    h_cast = {}
    P45_S0 = s_s.n

    def cmark(idx):
        return P45_S0 + idx + 1
    out_done = []
    f2_t_mark = []
    f1_marks = {}
    P2_S_END = s_s.n

    s_gs = [Sem(sems, nc, "s_g%d" % i) for i in range(4)]
    s_g = s_gs[0]
    w_dma_mark = {}

    def emit_w1dma(i_):
        def _sync(e, i_=i_):
            if i_ < 2:
                sem, val = wst_chain[len(wst_chain) - 2 + i_]
                e.wait_ge(sem.h, val)
            else:
                e.wait_ge(s_v.h, wcast_mark[('w1', i_ - 2)])
            s_w.dinc(e.dma_start(out=wst[:, (i_ % 2) * D:(i_ % 2 + 1) * D],
                                 in_=w1_h[i_]))
            w_dma_mark[('w1', i_)] = (s_w, s_w.n)
        em("sync", _sync)

    def emit_w2dma(i_):
        def _gp(e, i_=i_):
            if i_ < 4:
                e.wait_ge(s_t.h, P2_T)
            else:
                e.wait_ge(s_v.h, wcast_mark[('w2', i_ - 4)])
            ch = s_gs[i_ % 4]
            ch.dinc(e.dma_start(out=wst2_buf(i_), in_=w2_h[i_]))
            w_dma_mark[('w2', i_)] = (ch, ch.n)
        em("gpsimd", _gp)

    def emit_w2dma_h(h):
        def _gp(e, h=h):
            if h < 8:
                e.wait_ge(s_t.h, P2_T)
                if h in (4, 5):
                    e.wait_ge(s_v.h, wcast_mark[('w1', NHB - 2)])
                elif h in (6, 7):
                    e.wait_ge(s_v.h, wcast_mark[('w1', NHB - 1)])
            else:
                e.wait_ge(s_v.h, wcast_mark[('w2h', h - 8)])
            r = h % 8
            bufs = wst2 if r < 4 else wst
            ch = s_gs[h % 4]
            ch.dinc(e.dma_start(
                out=bufs[:, (r % 4) * 512:(r % 4) * 512 + 512],
                in_=w2_h[h // 2][:, (h % 2) * 512:(h % 2) * 512 + 512]))
            w_dma_mark[('w2h', h)] = (ch, ch.n)
        em("gpsimd", _gp)

    def emit_wcast(e, kind, i_):
        sem, val = w_dma_mark[(kind, i_)]
        e.wait_ge(sem.h, val)
        if kind == 'w1':
            dst = w1_e5[:, i_ * D:(i_ + 1) * D]
            srcb = wst[:, (i_ % 2) * D:(i_ % 2 + 1) * D]
        elif kind == 'w2':
            dst = w2_e5[:, (i_) * D:(i_ + 1) * D]
            srcb = wst2_buf(i_)
        else:
            r = i_ % 8
            bufs = wst2 if r < 4 else wst
            dst = w2_e5[:, i_ * 512:(i_ + 1) * 512]
            srcb = bufs[:, (r % 4) * 512:(r % 4) * 512 + 512]
        s_v.inc(e.tensor_copy(dst, srcb))
        wcast_mark[(kind, i_)] = s_v.n

    for i_ in (0, 1):
        if not W1_OVERLAP:
            emit_w1dma(i_)
    if W1_OVERLAP:
        for h in range(8):
            emit_w2dma_h(h)
    else:
        for i_ in range(min(4, NHB)):
            emit_w2dma(i_)

    def _wvec0(e):
        e.wait_ge(s_t.h, P2_T)
        if W1_OVERLAP:
            emit_wcast(e, 'w2h', 0)
            emit_wcast(e, 'w2h', 1)
            emit_wcast(e, 'w2h', 2)
            emit_wcast(e, 'w2h', 3)
        else:
            emit_wcast(e, 'w1', 0)
            emit_wcast(e, 'w1', 1)
            emit_wcast(e, 'w2', 0)
            emit_wcast(e, 'w2', 1)
    em("vector", _wvec0)

    def _p45_scal(e):
        for ts_ in range(NSTR):
            for hb_ in range(NHB):
                rot_ = hb_ % 4
                idx_ = ts_ * NHB + hb_
                e.wait_ge(s_v.h, h_cast[idx_])
                i = e.activation(h_e5[:, hb_ * 512:(hb_ + 1) * 512],
                                 h_bfst[:, rot_ * 512:(rot_ + 1) * 512], AF.Copy)
                s_s.inc(i)
                assert s_s.n == cmark(idx_)

    last_relu_of_stripe = {}
    for ts in range(NSTR):
        for hb in range(NHB):
            rot = hb % 4
            idx = ts * NHB + hb

            if ts == 0:
                if not W1_OVERLAP and hb + 2 < NHB:
                    emit_w1dma(hb + 2)
                if W1_OVERLAP:
                    for h in (2 * hb + 8, 2 * hb + 9):
                        if h < 2 * NHB:
                            emit_w2dma_h(h)
                elif hb + 4 < NHB:
                    emit_w2dma(hb + 4)

            def _tens(e, ts=ts, hb=hb, rot=rot, idx=idx):
                if idx == 0:
                    e.wait_ge(s_s.h, P2_S_END)
                    e.wait_ge(s_v.h, P2_V_CORE)
                if ts == 0:
                    e.wait_ge(s_v.h, wcast_mark[('w1', hb)])
                if hb == 0 and ts >= 1:
                    e.wait_ge(s_v.h, out_done[-2])
                if hb == 2 and ts >= 1:
                    e.wait_ge(s_v.h, out_done[-1])
                if idx >= 4:
                    e.wait_ge(s_v.h, h_cast[idx - 4])
                for cp in range(NCB // 2):
                    i = nc.tensor.matmul(
                        psA[:, rot * 512:(rot + 1) * 512],
                        w1_e5[:, hb * D + 2 * cp * 128: hb * D + (2 * cp + 2) * 128]
                            .rearrange("p (k m) -> p k m", k=2),
                        xn_e5_v[:, 2 * cp:2 * cp + 2, ts * 512:(ts + 1) * 512],
                        start=(cp == 0), stop=(cp == NCB // 2 - 1),
                        perf_mode=mybir.MatmulPerfMode.DoubleRow)
                    if cp == NCB // 2 - 1:
                        s_t.inc(i)
            em("tensor", _tens)
            f1_marks[idx] = s_t.n

            def _vec(e, ts=ts, hb=hb, rot=rot, idx=idx):
                if ts == 0 and hb + 2 < NHB:
                    if not W1_OVERLAP:
                        emit_wcast(e, 'w1', hb + 2)
                        emit_wcast(e, 'w2', hb + 2)
                    else:
                        emit_wcast(e, 'w2h', 2 * hb + 4)
                        emit_wcast(e, 'w2h', 2 * hb + 5)
                e.wait_ge(s_t.h, f1_marks[idx])
                if ts >= 1 and hb == 0:
                    e.wait_ge(s_t.h, f2_t_mark[ts - 1])
                if idx >= 4:
                    e.wait_ge(s_s.h, cmark(idx - 4))
                i = e.tensor_scalar(h_bfst[:, rot * 512:(rot + 1) * 512],
                                    psA[:, rot * 512:(rot + 1) * 512],
                                    b1q[:, hb:hb + 1], 0.0, ALU.add, ALU.max)
                s_v.inc(i)
                h_cast[idx] = s_v.n
            em("vector", _vec)
        H_STRIPE_V = s_v.n
        last_relu_of_stripe[ts] = h_cast[ts * NHB + NHB - 1]

        for tj in range(4):
            tb = ts * 4 + tj
            buf = tb % 2
            pb = tb % 2

            def _sync(e, tb=tb, buf=buf):
                if tb < 2:
                    e.wait_ge(s_v.h, P2_V)
                else:
                    e.wait_ge(s_v.h, out_done[tb - 2])
                s_x.dinc(e.dma_start(out=x_st[:, buf * D:(buf + 1) * D],
                                    in_=x_h[tb * 128:(tb + 1) * 128, :]))
                s_x.dinc(e.dma_start(out=mask_st[:, buf * D:(buf + 1) * D],
                                    in_=mask_h[tb * 128:(tb + 1) * 128, :]))
            em("sync", _sync)

            def _tens(e, ts=ts, tj=tj, tb=tb, pb=pb):
                if tb < 2:
                    e.wait_ge(s_f.h, WPJ_F)
                if tj <= 1:
                    e.wait_ge(s_v.h, last_relu_of_stripe[ts])
                if tb >= 2:
                    e.wait_ge(s_v.h, out_done[tb - 2])
                for jb in range(2):
                    for cb in range(NCB):
                        i = nc.tensor.matmul(
                            psA[:, pb * 1024 + jb * 512: pb * 1024 + (jb + 1) * 512],
                            attn_bf[:, cb * T + tb * 128: cb * T + (tb + 1) * 128],
                            wpj_bf[:, cb * D + jb * 512: cb * D + (jb + 1) * 512],
                            start=(cb == 0), stop=(cb == NCB - 1))
                e.wait_ge(s_s.h, cmark(ts * NHB + NHB - 1))
                h_v = h_e5[:].rearrange("p (h t) -> p h t", t=512)
                w2_v = w2_e5[:].rearrange("p (h j) -> p h j", j=D)
                for ns in range(2):
                    for hp in range(NHB // 2):
                        i = nc.tensor.matmul(
                            psB[:, pb * 1024 + ns * 512: pb * 1024 + (ns + 1) * 512],
                            h_v[:, 2 * hp:2 * hp + 2, tj * 128:(tj + 1) * 128],
                            w2_v[:, 2 * hp:2 * hp + 2, ns * 512:(ns + 1) * 512],
                            start=(hp == 0), stop=(hp == NHB // 2 - 1),
                            perf_mode=mybir.MatmulPerfMode.DoubleRow)
                        if hp == NHB // 2 - 1 and ns == 1:
                            s_t.inc(i)
            em("tensor", _tens)
            f2_mark = s_t.n
            if tj == 3:
                f2_t_mark.append(f2_mark)

            def _vec(e, tb=tb, pb=pb, buf=buf, f2_mark=f2_mark):
                e.wait_ge(s_t.h, f2_mark)
                e.wait_ge(s_x.h, s_x.n)
                if tb >= 2:
                    e.wait_ge(s_o.h, 16 * (tb - 1))
                dst = out_st[:, buf * D:(buf + 1) * D]
                e.tensor_tensor(dst, psA[:, pb * 1024:(pb + 1) * 1024], bp_sb[:], ALU.add)
                e.drain()
                e.tensor_tensor(dst, dst, x_st[:, buf * D:(buf + 1) * D], ALU.add)
                e.tensor_tensor(y2_bf[:], psB[:, pb * 1024:(pb + 1) * 1024], b2q[:], ALU.add)
                e.drain()
                e.tensor_tensor(y2_f32[:], y2_bf[:], mask_st[:, buf * D:(buf + 1) * D],
                                ALU.mult)
                e.drain()
                i = e.tensor_tensor(dst, dst, y2_f32[:], ALU.add)
                s_v.inc(i)
                out_done.append(s_v.n)
            em("vector", _vec)
            st_mark = s_v.n

            def _gp(e, tb=tb, buf=buf, st_mark=st_mark):
                e.wait_ge(s_v.h, st_mark)
                s_o.dinc(e.dma_start(out=out_h[tb * 128:(tb + 1) * 128, :],
                                    in_=out_st[:, buf * D:(buf + 1) * D]))
            em("gpsimd", _gp)


    em("scalar", _p45_scal)

    def _gp_final(e):
        e.wait_ge(s_o.h, s_o.n)
    em("gpsimd", _gp_final)

    for st in (esF, esG, esD, esB):
        st.close()
    es.close()
    sems.close()
    return nc


def _host_mask(T, Dm):
    import jax
    if os.environ.get("BASS_SIM_FP64") == "1":
        jax.config.update("jax_enable_x64", True)
    cpu = jax.devices("cpu")[0]
    with jax.default_device(cpu):
        keep = jax.random.bernoulli(
            jax.random.key(42, impl="threefry2x32"), 0.8, (8, T, Dm))
    keep = np.asarray(keep)
    return np.where(keep, np.float32(1.25), np.float32(0.0)).astype(ml_dtypes.bfloat16)


def _prep_inputs(L0_attn_x, L0_attn_w_qkv, L0_attn_w_proj, L0_attn_b_proj,
                 L0_ffn_W1, L0_ffn_b1, L0_ffn_W2, L0_ffn_b2):
    B, T, _ = L0_attn_x.shape
    w_qkv = np.asarray(L0_attn_w_qkv, dtype=np.float32)
    w_proj = np.asarray(L0_attn_w_proj, dtype=np.float32)
    W1 = np.asarray(L0_ffn_W1, dtype=np.float32)
    W2 = np.asarray(L0_ffn_W2, dtype=np.float32)

    wqk = np.ascontiguousarray(
        w_qkv[:2 * D].reshape(H, 128, NCB, 128).transpose(0, 3, 2, 1)).reshape(H, 128, D)
    wv = np.ascontiguousarray(w_qkv[2 * D:].T).reshape(NCB, 128, D)
    wpj = np.ascontiguousarray(w_proj.T).reshape(NCB, 128, D)
    w1 = np.ascontiguousarray(
        np.ascontiguousarray(W1.T).reshape(NCB, 128, NHB, 128).transpose(2, 1, 0, 3)
    ).reshape(NHB, 128, D)
    w2 = np.ascontiguousarray(W2.T).reshape(NHB, 128, D)
    b1t = np.ascontiguousarray(np.asarray(L0_ffn_b1, dtype=np.float32).reshape(NHB, 128).T)
    b2t = np.ascontiguousarray(
        np.broadcast_to(np.asarray(L0_ffn_b2, dtype=np.float32), (128, D)))
    bpt = np.ascontiguousarray(
        np.broadcast_to(np.asarray(L0_attn_b_proj, dtype=np.float32), (128, D)))
    mask = _host_mask(T, D)

    in_maps = []
    for b in range(B):
        in_maps.append({
            "x": np.ascontiguousarray(np.asarray(L0_attn_x[b], dtype=np.float32)),
            "mask": np.ascontiguousarray(mask[b]),
            "wqk": wqk, "wv": wv, "wpj": wpj, "w1": w1, "w2": w2,
            "b1t": b1t, "b2t": b2t, "bpt": bpt,
        })
    return in_maps, T


def kernel(L0_attn_x, L0_attn_w_qkv, L0_attn_w_proj, L0_attn_b_proj,
           L0_ffn_W1, L0_ffn_b1, L0_ffn_W2, L0_ffn_b2):
    in_maps, T = _prep_inputs(L0_attn_x, L0_attn_w_qkv, L0_attn_w_proj,
                              L0_attn_b_proj, L0_ffn_W1, L0_ffn_b1,
                              L0_ffn_W2, L0_ffn_b2)
    nc = build_nc(T=T)
    res = run_bass_kernel_spmd(nc, in_maps, core_ids=list(range(len(in_maps))))
    return np.stack([np.asarray(r["out"], dtype=np.float32) for r in res.results], axis=0)


# version 32
# speedup vs baseline: 1.0092x; 1.0092x over previous
import os
from contextlib import ExitStack

import numpy as np
import ml_dtypes

import concourse.bass as bass
import concourse.mybir as mybir
from concourse.bass_utils import run_bass_kernel_spmd
from concourse.alu_op_type import AluOpType as ALU

F32 = mybir.dt.float32
BF16 = mybir.dt.bfloat16
F8E5 = mybir.dt.float8e5
AF = mybir.ActivationFunctionType
AX = mybir.AxisListType.X

D = 1024
H = 16
NCB = D // 128
HID = 4096
NHB = HID // 128
C_SHIFT = 72.0
SCALE = 0.125


class Sem:
    def __init__(self, stack, nc, name):
        self.h = stack.enter_context(nc.semaphore(name))
        self.n = 0

    def inc(self, instr, k=1):
        instr.then_inc(self.h, k)
        self.n += k
        return instr

    def dinc(self, instr):
        if self.n > 0:
            instr.wait_op(self.h, self.n, "sem-ge")
        instr.then_inc(self.h, 16)
        self.n += 16
        return instr


def build_nc(T=2048):
    assert T % 512 == 0
    NTB = T // 128
    NQS = T // 512
    NKC = NTB
    nc = bass.Bass("TRN2", target_bir_lowering=False)

    x_h = nc.dram_tensor("x", [T, D], F32, kind="ExternalInput").ap()
    mask_h = nc.dram_tensor("mask", [T, D], BF16, kind="ExternalInput").ap()
    wqk_h = nc.dram_tensor("wqk", [H, 128, D], F32, kind="ExternalInput").ap()
    wv_h = nc.dram_tensor("wv", [NCB, 128, D], F32, kind="ExternalInput").ap()
    wpj_h = nc.dram_tensor("wpj", [NCB, 128, D], F32, kind="ExternalInput").ap()
    w1_h = nc.dram_tensor("w1", [NHB, 128, D], F32, kind="ExternalInput").ap()
    w2_h = nc.dram_tensor("w2", [NHB, 128, D], F32, kind="ExternalInput").ap()
    b1_h = nc.dram_tensor("b1t", [128, NHB], F32, kind="ExternalInput").ap()
    b2_h = nc.dram_tensor("b2t", [128, D], F32, kind="ExternalInput").ap()
    bp_h = nc.dram_tensor("bpt", [128, D], F32, kind="ExternalInput").ap()
    out_h = nc.dram_tensor("out", [T, D], F32, kind="ExternalOutput").ap()
    ident_h = nc.inline_tensor(np.eye(128, dtype=np.float32), name="ident_h").ap()
    ones_h = nc.inline_tensor(np.ones((128, 128), dtype=np.float32), name="ones_h").ap()

    es = ExitStack()
    sems = ExitStack()

    def sb(stack, name, shape, dt, side="left"):
        return stack.enter_context(nc.sbuf_tensor(name, shape, dt, side=side))

    consts = sb(es, "consts", [128, 8], F32)
    CONST_COLS = {}
    for v in (1.0 / D, -1.0 / D, 1e-5, SCALE, -C_SHIFT, 1.0):
        CONST_COLS[v] = len(CONST_COLS)
        nc.const_aps.register(F32, v, consts[:, CONST_COLS[v]:CONST_COLS[v] + 1])

    ident = sb(es, "ident", [128, 128], F32)
    ones_sb = sb(es, "ones_sb", [128, 128], F32)
    b1q = sb(es, "b1q", [128, NHB], F32)
    b1q_e5 = sb(es, "b1q_e5", [128, NHB], F8E5)
    b2q = sb(es, "b2q", [128, D], F32)
    b2q_e5 = sb(es, "b2q_e5", [128, D], F8E5)
    bp_sb = sb(es, "bp_sb", [128, D], F32)
    XST_LO = -(-nc.sbuf_base // 32) * 32
    x_st = sb(es, "x_st", [128, 3 * D], F32)
    wst = sb(es, "wst", [128, 2 * D], F32)
    stat = sb(es, "stat", [128, 18], F32)

    psA = es.enter_context(nc.psum_tensor("psA", [128, 2048], F32))
    psB = es.enter_context(nc.psum_tensor("psB", [128, 2048], F32))

    s_x = Sem(sems, nc, "s_x")
    s_w = Sem(sems, nc, "s_w")
    s_v = Sem(sems, nc, "s_v")
    s_s = Sem(sems, nc, "s_s")
    s_t = Sem(sems, nc, "s_t")
    s_o = Sem(sems, nc, "s_o")
    s_f = Sem(sems, nc, "s_f")

    block = es.enter_context(nc.Block())

    def em(engname, f):
        getattr(block, engname)(f)

    wst_chain = []

    def wst_wait(e, idx):
        if idx >= 2:
            sem, val = wst_chain[idx - 2]
            e.wait_ge(sem.h, val)

    def _setup_sync(e):
        s_w.dinc(e.dma_start(out=ident[:], in_=ident_h[:]))
        s_w.dinc(e.dma_start(out=ones_sb[:], in_=ones_h[:]))
        s_w.dinc(e.dma_start(out=b1q[:, 0:NHB], in_=b1_h[:]))
        s_w.dinc(e.dma_start(out=b2q[:], in_=b2_h[:]))
        s_w.dinc(e.dma_start(out=bp_sb[:], in_=bp_h[:]))
    em("sync", _setup_sync)

    def _setup_vec(e):
        for v, j in CONST_COLS.items():
            e.memset(consts[:, j:j + 1], v)
        e.wait_ge(s_w.h, s_w.n)
        e.tensor_copy(b1q_e5[:, 0:NHB], b1q[:, 0:NHB])
        e.tensor_copy(b2q_e5[:], b2q[:])
        e.drain()
        e.tensor_copy(b1q[:, 0:NHB], b1q_e5[:, 0:NHB])
        s_v.inc(e.tensor_copy(b2q[:], b2q_e5[:]))
    em("vector", _setup_vec)
    SETUP_V = s_v.n

    esB = ExitStack()
    xn_e5 = sb(esB, "xn_e5", [128, NCB * T], F8E5)
    esC = ExitStack()
    addr = {}

    def sb_rec(stack, name, shape, dt, side):
        hi = nc.sbuf_top
        t_ = sb(stack, name, shape, dt, side=side)
        addr[name] = (nc.sbuf_top, hi)
        return t_
    vt_bf = sb_rec(esC, "vt_bf", [128, NTB * (H * 65)], BF16, side="right")
    k_bf = sb_rec(esC, "k_bf", [128, NCB * T], BF16, side="right")
    q_bf = sb_rec(esC, "q_bf", [128, NCB * T], BF16, side="right")
    vt_v = vt_bf[:].rearrange("p (tb h d) -> p tb h d", h=H, d=65)
    W1_BYTES = NHB * D
    W1_OVERLAP = (addr["k_bf"][1] - addr["k_bf"][0] == W1_BYTES and
                  addr["vt_bf"][1] - addr["vt_bf"][0] >= W1_BYTES and
                  NQS >= 1 and (H // 2) > 0 and NHB % (H // 2) == 0)
    if W1_OVERLAP:
        w1_e5 = nc.alloc_sbuf_tensor_at(
            "w1_e5", [128, NHB * D], F8E5, offset=addr["k_bf"][0])
        w2_e5 = nc.alloc_sbuf_tensor_at(
            "w2_e5", [128, NHB * D], F8E5, offset=addr["vt_bf"][0])
    esA = ExitStack()
    xn_f32 = sb(esA, "xn_f32", [128, 3 * D], F32, side="right")
    xn_bf = sb(esA, "xn_bf", [128, NCB * T], BF16, side="right")
    wv_bf = sb(esA, "wv_bf", [128, NCB * D], BF16, side="right")
    wqk_b = sb(esA, "wqk_b", [128, 2 * D], BF16, side="right")
    xn_bf_v = xn_bf[:].rearrange("p (c t) -> p c t", t=T)
    xn_e5_v = xn_e5[:].rearrange("p (c t) -> p c t", t=T)

    evac_done = []
    p0_vec_mark = []
    xn_mark = []
    tr_marks = []
    P0_S0 = s_s.n

    def p0_smark(tb):
        return P0_S0 + tb + 1

    wv_cast = []
    v_free = []
    ONES_V_BOX = []
    KWV = max(1, (NCB + NTB - 1) // NTB)
    wv_sched = {}
    for cb_ in range(NCB):
        wv_sched.setdefault(cb_ // KWV, []).append(cb_)
    last_wv_tb = max(wv_sched)
    v_delay = min(last_wv_tb + 2, NTB - 1) if NTB > 1 else 1
    wv_dma_mark = {}
    wv_chain_idx = {}

    def emit_v(j):
        pb = j % 2

        def _tensV(e, j=j, pb=pb):
            if j < 2:
                e.wait_ge(s_f.h, wv_cast[-1])
            else:
                e.wait_ge(s_f.h, v_free[j - 2])
            e.wait_ge(s_f.h, evac_done[j])
            for cb in range(NCB):
                for vs in range(2):
                    i = nc.tensor.matmul(
                        psA[:, pb * 1024 + vs * 512: pb * 1024 + (vs + 1) * 512],
                        xn_bf[:, cb * T + j * 128: cb * T + (j + 1) * 128],
                        wv_bf[:, cb * D + vs * 512: cb * D + (vs + 1) * 512],
                        start=(cb == 0), stop=(cb == NCB - 1))
                    if cb == NCB - 1 and vs == 1:
                        s_t.inc(i)
        em("tensor", _tensV)

        def _vecV(e, j=j, pb=pb):
            e.wait_ge(s_t.h, s_t.n)
            i = e.tensor_copy(
                vt_v[:, j, :, 0:64],
                psA[:, pb * 1024: pb * 1024 + 1024].rearrange("p (h d) -> p h d", d=64))
            s_f.inc(i)
            v_free.append(s_f.n)
        em("vector", _vecV)

    for tb in range(NTB):
        buf = tb % 3
        nbuf = tb % 3
        xs = x_st[:, buf * D:(buf + 1) * D]
        xn = xn_f32[:, nbuf * D:(nbuf + 1) * D]
        sc = (tb % 3) * 6

        def _sync(e, tb=tb, xs=xs):
            if tb >= 3:
                e.wait_ge(s_v.h, xn_mark[tb - 3])
                e.wait_ge(s_s.h, p0_smark(tb - 3))
            s_x.dinc(e.dma_start(out=xs[:], in_=x_h[tb * 128:(tb + 1) * 128, :]))
            for cb in wv_sched.get(tb, ()):
                sidx = len(wst_chain)
                wst_wait(e, sidx)
                s_w.dinc(e.dma_start(out=wst[:, (cb % 2) * D:(cb % 2 + 1) * D],
                                     in_=wv_h[cb]))
                wv_dma_mark[cb] = s_w.n
                wv_chain_idx[cb] = sidx
                wst_chain.append((s_f, None))
        em("sync", _sync)

        if wv_sched.get(tb):
            def _vec_wv(e, cbs=wv_sched[tb]):
                for cb in cbs:
                    e.wait_ge(s_w.h, wv_dma_mark[cb])
                    i = e.tensor_copy(wv_bf[:, cb * D:(cb + 1) * D],
                                      wst[:, (cb % 2) * D:(cb % 2 + 1) * D])
                    s_f.inc(i)
                    wv_cast.append(s_f.n)
                    wst_chain[wv_chain_idx[cb]] = (s_f, s_f.n)
            em("vector", _vec_wv)
        if tb == last_wv_tb + 1 or (tb == last_wv_tb and tb == NTB - 1):
            def _vec_ones(e):
                s_v.inc(e.memset(vt_v[:, :, :, 64:65], 1.0))
            em("vector", _vec_ones)
            ONES_V_BOX.append(s_v.n)

        def emit_ln_stats(tb_):
            sbuf = tb_ % 3
            sxs = x_st[:, sbuf * D:(sbuf + 1) * D]
            ssc = (tb_ % 3) * 6

            def _vec(e, tb_=tb_, sxs=sxs, ssc=ssc):
                if tb_ == 0:
                    e.wait_ge(s_v.h, SETUP_V)
                if tb_ >= 3:
                    e.wait_ge(s_s.h, p0_smark(tb_ - 3))
                e.wait_ge(s_x.h, 16 * (tb_ + 1))
                e.reduce_sum(stat[:, ssc:ssc + 1], sxs[:], axis=AX)
                e.drain()
                i = e.tensor_scalar(stat[:, ssc + 1:ssc + 2], stat[:, ssc:ssc + 1],
                                    -1.0 / D, None, ALU.mult)
                s_v.inc(i)
                p0_vec_mark.append(s_v.n)
            em("vector", _vec)

        if tb == 0:
            emit_ln_stats(0)

        def _vec2(e, tb=tb, xs=xs, xn=xn, sc=sc):
            e.wait_ge(s_s.h, p0_smark(tb))
            e.reciprocal(stat[:, sc + 4:sc + 5], stat[:, sc + 3:sc + 4])
            e.drain()
            s_v.inc(e.tensor_scalar(xn[:], xs[:], stat[:, sc + 1:sc + 2],
                                    stat[:, sc + 4:sc + 5], ALU.add, ALU.mult))
            xn_mark.append(s_v.n)
        em("vector", _vec2)
        if tb + 1 < NTB:
            emit_ln_stats(tb + 1)

        def _tens(e, tb=tb, xn=xn, xm=xn_mark[tb]):
            e.wait_ge(s_v.h, xm)
            half = tb % 2
            if tb >= 2:
                e.wait_ge(s_f.h, evac_done[tb - 2])
            for cb in range(NCB):
                i = nc.tensor.transpose(
                    psB[:, half * 1024 + cb * 128: half * 1024 + (cb + 1) * 128],
                    xn[:, cb * 128:(cb + 1) * 128], ident[:])
                if cb == NCB - 1:
                    s_t.inc(i)
        em("tensor", _tens)
        tr_marks.append(s_t.n)

        def _vec3(e, tb=tb):
            half = tb % 2
            e.wait_ge(s_t.h, s_t.n)
            src = psB[:, half * 1024: half * 1024 + 1024].rearrange(
                "p (c t) -> p c t", t=128)
            e.tensor_copy(xn_bf_v[:, 0:NCB, tb * 128:(tb + 1) * 128], src)
            e.drain()
            i = e.tensor_copy(xn_e5_v[:, 0:NCB, tb * 128:(tb + 1) * 128], src)
            s_f.inc(i)
            evac_done.append(s_f.n)
        em("vector", _vec3)

        if tb >= v_delay:
            emit_v(tb - v_delay)

    for j in range(max(0, NTB - v_delay), NTB):
        emit_v(j)

    def _p0_scal(e):
        for tb in range(NTB):
            buf = tb % 3
            nbuf = tb % 3
            xs = x_st[:, buf * D:(buf + 1) * D]
            xn = xn_f32[:, nbuf * D:(nbuf + 1) * D]
            sc = (tb % 3) * 6
            e.wait_ge(s_v.h, p0_vec_mark[tb])
            if tb >= 3:
                e.wait_ge(s_t.h, tr_marks[tb - 3])
            e.activation(xn[:], xs[:], AF.Square, bias=stat[:, sc + 1:sc + 2],
                         accum_out=stat[:, sc + 2:sc + 3])
            e.drain()
            s_s.inc(e.activation(stat[:, sc + 3:sc + 4], stat[:, sc + 2:sc + 3],
                                 AF.Sqrt, scale=1.0 / D, bias=1e-5))
            assert s_s.n == p0_smark(tb)
    em("scalar", _p0_scal)

    P0_V = s_v.n
    P0_F = s_f.n

    qk_free = []
    fb_last_evac = {}
    for fb in range(H):
        buf = fb % 2
        sidx = len(wst_chain)

        def _sync(e, fb=fb, buf=buf, sidx=sidx):
            wst_wait(e, sidx)
            if fb >= 2:
                e.wait_ge(s_f.h, fb_last_evac[fb - 2])
            s_w.dinc(e.dma_start(out=wst[:, buf * D:(buf + 1) * D], in_=wqk_h[fb]))
        em("sync", _sync)

        def _vec(e, buf=buf):
            e.wait_ge(s_w.h, s_w.n)
            s_v.inc(e.tensor_copy(wqk_b[:, buf * D:(buf + 1) * D],
                                  wst[:, buf * D:(buf + 1) * D]))
        em("vector", _vec)
        cast_mark = s_v.n
        wst_chain.append((s_v, s_v.n))

        for qs in range(NQS):
            g = NQS * fb + qs

            def _tens(e, fb=fb, buf=buf, qs=qs, g=g, cast_mark=cast_mark):
                if qs == 0:
                    e.wait_ge(s_v.h, cast_mark)
                if fb == 0 and qs == 0:
                    e.wait_ge(s_f.h, P0_F)
                if g >= 4:
                    e.wait_ge(s_f.h, qk_free[g - 4])
                for cb in range(NCB):
                    i = nc.tensor.matmul(
                        psB[:, (g % 4) * 512:(g % 4 + 1) * 512],
                        wqk_b[:, buf * D + cb * 128: buf * D + (cb + 1) * 128],
                        xn_bf[:, cb * T + qs * 512: cb * T + (qs + 1) * 512],
                        start=(cb == 0), stop=(cb == NCB - 1))
                s_t.inc(i)
            em("tensor", _tens)

            def _vec2(e, fb=fb, qs=qs, g=g):
                e.wait_ge(s_t.h, s_t.n)
                dst = q_bf if fb < 8 else k_bf
                c0 = (fb % 8) * T + qs * 512
                i = e.tensor_copy(dst[:, c0:c0 + 512],
                                  psB[:, (g % 4) * 512:(g % 4) * 512 + 512])
                s_f.inc(i)
                qk_free.append(s_f.n)
            em("vector", _vec2)
        fb_last_evac[fb] = s_f.n

    ONES_V = ONES_V_BOX[0]
    P1_V = s_v.n
    P1_F = s_f.n

    esA.close()

    esC2 = ExitStack()
    probs = sb(esC2, "probs", [128, 6 * 512], BF16, side="right")
    x_att_tm = nc.alloc_sbuf_tensor_at("x_att_tm", [128, 1024], F32,
                                       offset=XST_LO)
    rcp = nc.alloc_sbuf_tensor_at("rcp", [128, 16], F32, offset=XST_LO + 4096)
    esD = ExitStack()
    attn_bf = sb(esD, "attn_bf", [128, NCB * T], BF16)
    wpj_bf = sb(esD, "wpj_bf", [128, NCB * D], BF16)

    pv_pair_mark = {}
    sc_marks = {}
    pv_done = {}
    norm_mark = {}
    trans_mark = {}
    evac_mark = {}

    def pvoff(rot, t):
        return (2 * rot + t // 4) * 512 + (t % 4) * 65

    def emit_norm_vec(j):
        rot = j % 2

        def _vec(e, j=j, rot=rot):
            e.wait_ge(s_t.h, pv_done[j])
            for b in range(2):
                bank = (2 * rot + b) * 512
                e.reciprocal(
                    rcp[:, rot * 8 + b * 4: rot * 8 + b * 4 + 4].rearrange(
                        "p (t c) -> p t c", c=1),
                    psB[:, bank: bank + 260].rearrange(
                        "p (t c) -> p t c", c=65)[:, :, 64:65])
            e.drain()
            for t in range(8):
                h2, qb = t // 4, t % 4
                i = e.tensor_scalar(
                    x_att_tm[:, rot * 512 + qb * 128 + h2 * 64:
                             rot * 512 + qb * 128 + h2 * 64 + 64],
                    psB[:, pvoff(rot, t): pvoff(rot, t) + 64],
                    rcp[:, rot * 8 + t: rot * 8 + t + 1], None, ALU.mult)
            s_v.inc(i)
            norm_mark[j] = s_v.n
        em("vector", _vec)

    def emit_trans(j):
        rot = j % 2

        def _tens(e, j=j, rot=rot):
            e.wait_ge(s_v.h, norm_mark[j])
            for qb in range(4):
                dst = (2 * rot + qb // 2) * 512 + (qb % 2) * 128
                i = nc.tensor.transpose(
                    psB[:, dst: dst + 128],
                    x_att_tm[:, rot * 512 + qb * 128: rot * 512 + qb * 128 + 128],
                    ident[:])
            s_t.inc(i)
            trans_mark[j] = s_t.n
        em("tensor", _tens)

    def emit_evac(j):
        rot = j % 2
        pj, qsj = j // NQS, j % NQS

        def _vec(e, j=j, rot=rot, pj=pj, qsj=qsj):
            e.wait_ge(s_t.h, trans_mark[j])
            c0 = pj * T + qsj * 512
            i = e.tensor_copy(
                attn_bf[:, c0:c0 + 512].rearrange("p (b c) -> p b c", c=256),
                psB[:, 2 * rot * 512: 2 * rot * 512 + 1024].rearrange(
                    "p (b c) -> p b c", c=512)[:, :, 0:256])
            s_v.inc(i)
            evac_mark[j] = s_v.n
        em("vector", _vec)

    wpj_cast = []
    for cb in range(NCB):
        buf = cb % 2
        sidx = len(wst_chain)

        def _sync(e, cb=cb, buf=buf, sidx=sidx):
            wst_wait(e, sidx)
            s_w.dinc(e.dma_start(out=wst[:, buf * D:(buf + 1) * D], in_=wpj_h[cb]))
        em("sync", _sync)

        def _vec(e, cb=cb, buf=buf):
            e.wait_ge(s_w.h, s_w.n)
            i = e.tensor_copy(wpj_bf[:, cb * D:(cb + 1) * D],
                              wst[:, buf * D:(buf + 1) * D])
            s_f.inc(i)
            wpj_cast.append(s_f.n)
        em("vector", _vec)
        wst_chain.append((s_f, s_f.n))
    WPJ_F = s_f.n

    wcast_mark = {}
    w1_dma_mark = {}
    W1_PER_PAIR = NHB // (H // 2) if W1_OVERLAP else 0

    w1_chain_idx = {}

    def emit_w1dma_p2(j):
        sidx = len(wst_chain)
        w1_chain_idx[j] = sidx

        def _sync(e, j=j, sidx=sidx):
            wst_wait(e, sidx)
            s_w.dinc(e.dma_start(out=wst[:, (j % 2) * D:(j % 2 + 1) * D],
                                 in_=w1_h[j]))
            w1_dma_mark[j] = s_w.n
        em("sync", _sync)
        wst_chain.append(None)

    def emit_w1cast_p2(j, pair_gate):
        def _vec(e, j=j, pair_gate=pair_gate):
            e.wait_ge(s_w.h, w1_dma_mark[j])
            e.wait_ge(s_t.h, pair_gate)
            i = e.tensor_copy(w1_e5[:, j * D:(j + 1) * D],
                              wst[:, (j % 2) * D:(j % 2 + 1) * D])
            s_v.inc(i)
            wcast_mark[('w1', j)] = s_v.n
            wst_chain[w1_chain_idx[j]] = (s_v, s_v.n)
        em("vector", _vec)

    P2_S0 = s_s.n

    def exp_mark(it, kc):
        return P2_S0 + it * NKC + kc + 1
    if W1_OVERLAP:
        emit_w1dma_p2(0)
        emit_w1dma_p2(1)
    w1_pending = []

    def emit_w1block(base, gate):
        emit_w1cast_p2(base, gate)
        emit_w1cast_p2(base + 1, gate)
        for j in (base + 2, base + 3):
            if j < NHB:
                emit_w1dma_p2(j)
        if base + 2 < NHB:
            emit_w1cast_p2(base + 2, gate)
        if base + 3 < NHB:
            emit_w1cast_p2(base + 3, gate)
        for j in (base + 4, base + 5):
            if j < NHB:
                emit_w1dma_p2(j)
    NG = (H // 2) * NQS * NKC

    def emit_scores_g(g):
        it_, kc = divmod(g, NKC)
        p_, qs_ = divmod(it_, NQS)
        rot = g % 2
        scA = psA[:, rot * 1024: rot * 1024 + 512]
        scB = psA[:, rot * 1024 + 512: rot * 1024 + 1024]

        def _tens(e, g=g, it_=it_, kc=kc, p_=p_, qs_=qs_, scA=scA, scB=scB):
            if g == 0:
                e.wait_ge(s_v.h, ONES_V)
                e.wait_ge(s_f.h, P1_F)
            if g >= 2:
                e.wait_ge(s_s.h, P2_S0 + g - 1)
            nc.tensor.matmul(
                scA, k_bf[0:64, p_ * T + kc * 128: p_ * T + (kc + 1) * 128],
                q_bf[0:64, p_ * T + qs_ * 512: p_ * T + (qs_ + 1) * 512],
                start=True, stop=True, tile_position=(0, 0))
            i = nc.tensor.matmul(
                scB, k_bf[64:128, p_ * T + kc * 128: p_ * T + (kc + 1) * 128],
                q_bf[64:128, p_ * T + qs_ * 512: p_ * T + (qs_ + 1) * 512],
                start=True, stop=True, tile_position=(64, 0))
            s_t.inc(i)
        em("tensor", _tens)
        it_, kc = divmod(g, NKC)
        sc_marks[(it_, kc)] = s_t.n

    emit_scores_g(0)
    emit_scores_g(1)
    for p in range(H // 2):
        for qs in range(NQS):
            it = p * NQS + qs

            def emit_pv(kc, p=p, it=it):
                rot3 = (it * NKC + kc) % 3
                rot = it % 2

                def _tens2(e, kc=kc, rot3=rot3, rot=rot):
                    e.wait_ge(s_s.h, exp_mark(it, kc))
                    if kc == 0 and it >= 2:
                        e.wait_ge(s_v.h, evac_mark[it - 2])
                    for t in range(8):
                        h2, qb = t // 4, t % 4
                        i = nc.tensor.matmul(
                            psB[:, pvoff(rot, t): pvoff(rot, t) + 65],
                            probs[:, rot3 * 1024 + h2 * 512 + qb * 128:
                                  rot3 * 1024 + h2 * 512 + qb * 128 + 128],
                            vt_bf[:, kc * (65 * H) + (2 * p + h2) * 65:
                                  kc * (65 * H) + (2 * p + h2) * 65 + 65],
                            start=(kc == 0 and t % 4 == 0),
                            stop=(kc == NKC - 1 and t % 4 == 3))
                    s_t.inc(i)
                    pv_pair_mark[(it, kc)] = s_t.n
                em("tensor", _tens2)

            if it >= 1:
                emit_norm_vec(it - 1)
            for kc in range(NKC):
                if kc == 2 and it >= 1:
                    emit_trans(it - 1)
                    emit_evac(it - 1)
                if kc == 3 and w1_pending:
                    emit_w1block(*w1_pending.pop(0))
                g = it * NKC + kc + 2
                if g < NG:
                    emit_scores_g(g)
                emit_pv(kc)
            pv_done[it] = s_t.n

            if W1_OVERLAP and qs == NQS - 1:
                w1_pending.append((W1_PER_PAIR * p, sc_marks[(it, NKC - 1)]))
    li = (H // 2) * NQS - 1
    emit_norm_vec(li)
    emit_trans(li)
    emit_evac(li)
    P2_V_CORE = s_v.n
    while w1_pending:
        emit_w1block(*w1_pending.pop(0))

    def _p2_scal(e):
        for it in range((H // 2) * NQS):
            for kc in range(NKC):
                rot2 = kc % 2
                rot3 = (it * NKC + kc) % 3
                e.wait_ge(s_t.h, sc_marks[(it, kc)])
                pit, pkc = (it, kc - 3) if kc >= 3 else (it - 1, NKC - 3 + kc)
                if (pit, pkc) in pv_pair_mark:
                    e.wait_ge(s_t.h, pv_pair_mark[(pit, pkc)])
                i = e.activation(probs[:, rot3 * 1024:(rot3 + 1) * 1024],
                                 psA[:, rot2 * 1024:(rot2 + 1) * 1024],
                                 AF.Exp, bias=-C_SHIFT, scale=SCALE)
                s_s.inc(i)
                assert s_s.n == exp_mark(it, kc)
    em("scalar", _p2_scal)

    P2_V = s_v.n
    P2_T = s_t.n

    esC2.close()
    esC.close()
    esG = ExitStack()
    h_e5 = sb(esG, "h_e5", [128, NHB * 512], F8E5)
    h_bfst = sb(esG, "h_bfst", [128, 4 * 512], BF16)
    y2_bf = sb(esG, "y2_bf", [128, D], BF16)
    y2_f32 = sb(esG, "y2_f32", [128, D], F32)
    mask_st = sb(esG, "mask_st", [128, 2 * D], BF16)
    out_st = sb(esG, "out_st", [128, 2 * D], F32)
    esF = ExitStack()
    if W1_OVERLAP:
        wst2 = sb(esF, "wst2", [128, 2 * D], F32)
        wst2b = wst
    else:
        wst2 = sb(esF, "wst2", [128, 4 * D], F32)
        wst2b = None

    def wst2_buf(i_):
        r = i_ % 4
        if W1_OVERLAP:
            t_ = wst2 if r < 2 else wst2b
            return t_[:, (r % 2) * D:(r % 2 + 1) * D]
        return wst2[:, r * D:(r + 1) * D]
    if W1_OVERLAP:
        assert nc.sbuf_base <= addr["k_bf"][0], (nc.sbuf_base, addr)
    else:
        w1_e5 = sb(esF, "w1_e5", [128, NHB * D], F8E5)
        w2_e5 = sb(esF, "w2_e5", [128, NHB * D], F8E5)

    NSTR = T // 512
    h_cast = {}
    P45_S0 = s_s.n

    def cmark(idx):
        return P45_S0 + idx + 1
    out_done = []
    f2_t_mark = []
    f1_marks = {}
    P2_S_END = s_s.n

    s_gs = [Sem(sems, nc, "s_g%d" % i) for i in range(4)]
    s_g = s_gs[0]
    w_dma_mark = {}

    def emit_w1dma(i_):
        def _sync(e, i_=i_):
            if i_ < 2:
                sem, val = wst_chain[len(wst_chain) - 2 + i_]
                e.wait_ge(sem.h, val)
            else:
                e.wait_ge(s_v.h, wcast_mark[('w1', i_ - 2)])
            s_w.dinc(e.dma_start(out=wst[:, (i_ % 2) * D:(i_ % 2 + 1) * D],
                                 in_=w1_h[i_]))
            w_dma_mark[('w1', i_)] = (s_w, s_w.n)
        em("sync", _sync)

    def emit_w2dma(i_):
        def _gp(e, i_=i_):
            if i_ < 4:
                e.wait_ge(s_t.h, P2_T)
                if W1_OVERLAP and i_ >= 2:
                    e.wait_ge(s_v.h, wcast_mark[('w1', NHB - 2 + (i_ - 2))])
            else:
                e.wait_ge(s_v.h, wcast_mark[('w2', i_ - 4)])
            ch = s_gs[i_ % 4]
            ch.dinc(e.dma_start(out=wst2_buf(i_), in_=w2_h[i_]))
            w_dma_mark[('w2', i_)] = (ch, ch.n)
        em("gpsimd", _gp)

    def emit_wcast(e, kind, i_):
        sem, val = w_dma_mark[(kind, i_)]
        e.wait_ge(sem.h, val)
        if kind == 'w1':
            dst = w1_e5[:, i_ * D:(i_ + 1) * D]
            srcb = wst[:, (i_ % 2) * D:(i_ % 2 + 1) * D]
        else:
            dst = w2_e5[:, (i_) * D:(i_ + 1) * D]
            srcb = wst2_buf(i_)
        s_v.inc(e.tensor_copy(dst, srcb))
        wcast_mark[(kind, i_)] = s_v.n

    for i_ in (0, 1):
        if not W1_OVERLAP:
            emit_w1dma(i_)
    for i_ in range(min(4, NHB)):
        emit_w2dma(i_)

    def _wvec0(e):
        e.wait_ge(s_t.h, P2_T)
        if not W1_OVERLAP:
            emit_wcast(e, 'w1', 0)
            emit_wcast(e, 'w1', 1)
        emit_wcast(e, 'w2', 0)
        emit_wcast(e, 'w2', 1)
    em("vector", _wvec0)

    def _p45_scal(e):
        for ts_ in range(NSTR):
            for hb_ in range(NHB):
                rot_ = hb_ % 4
                idx_ = ts_ * NHB + hb_
                e.wait_ge(s_v.h, h_cast[idx_])
                i = e.activation(h_e5[:, hb_ * 512:(hb_ + 1) * 512],
                                 h_bfst[:, rot_ * 512:(rot_ + 1) * 512], AF.Copy)
                s_s.inc(i)
                assert s_s.n == cmark(idx_)

    last_relu_of_stripe = {}
    for ts in range(NSTR):
        for hb in range(NHB):
            rot = hb % 4
            idx = ts * NHB + hb

            if ts == 0:
                if not W1_OVERLAP and hb + 2 < NHB:
                    emit_w1dma(hb + 2)
                if hb + 4 < NHB:
                    emit_w2dma(hb + 4)

            def _tens(e, ts=ts, hb=hb, rot=rot, idx=idx):
                if idx == 0:
                    e.wait_ge(s_s.h, P2_S_END)
                    e.wait_ge(s_v.h, P2_V_CORE)
                if ts == 0:
                    e.wait_ge(s_v.h, wcast_mark[('w1', hb)])
                if hb == 0 and ts >= 1:
                    e.wait_ge(s_v.h, out_done[-2])
                if hb == 2 and ts >= 1:
                    e.wait_ge(s_v.h, out_done[-1])
                if idx >= 4:
                    e.wait_ge(s_v.h, h_cast[idx - 4])
                for cp in range(NCB // 2):
                    i = nc.tensor.matmul(
                        psA[:, rot * 512:(rot + 1) * 512],
                        w1_e5[:, hb * D + 2 * cp * 128: hb * D + (2 * cp + 2) * 128]
                            .rearrange("p (k m) -> p k m", k=2),
                        xn_e5_v[:, 2 * cp:2 * cp + 2, ts * 512:(ts + 1) * 512],
                        start=(cp == 0), stop=(cp == NCB // 2 - 1),
                        perf_mode=mybir.MatmulPerfMode.DoubleRow)
                    if cp == NCB // 2 - 1:
                        s_t.inc(i)
            em("tensor", _tens)
            f1_marks[idx] = s_t.n

            def _vec(e, ts=ts, hb=hb, rot=rot, idx=idx):
                if ts == 0 and hb + 2 < NHB:
                    if not W1_OVERLAP:
                        emit_wcast(e, 'w1', hb + 2)
                    emit_wcast(e, 'w2', hb + 2)
                e.wait_ge(s_t.h, f1_marks[idx])
                if ts >= 1 and hb == 0:
                    e.wait_ge(s_t.h, f2_t_mark[ts - 1])
                if idx >= 4:
                    e.wait_ge(s_s.h, cmark(idx - 4))
                i = e.tensor_scalar(h_bfst[:, rot * 512:(rot + 1) * 512],
                                    psA[:, rot * 512:(rot + 1) * 512],
                                    b1q[:, hb:hb + 1], 0.0, ALU.add, ALU.max)
                s_v.inc(i)
                h_cast[idx] = s_v.n
            em("vector", _vec)
        H_STRIPE_V = s_v.n
        last_relu_of_stripe[ts] = h_cast[ts * NHB + NHB - 1]

        for tj in range(4):
            tb = ts * 4 + tj
            buf = tb % 2
            pb = tb % 2

            def _sync(e, tb=tb, buf=buf):
                if tb < 2:
                    e.wait_ge(s_v.h, P2_V)
                else:
                    e.wait_ge(s_v.h, out_done[tb - 2])
                s_x.dinc(e.dma_start(out=x_st[:, buf * D:(buf + 1) * D],
                                    in_=x_h[tb * 128:(tb + 1) * 128, :]))
                s_x.dinc(e.dma_start(out=mask_st[:, buf * D:(buf + 1) * D],
                                    in_=mask_h[tb * 128:(tb + 1) * 128, :]))
            em("sync", _sync)

            def _tens(e, ts=ts, tj=tj, tb=tb, pb=pb):
                if tb < 2:
                    e.wait_ge(s_f.h, WPJ_F)
                if tj <= 1:
                    e.wait_ge(s_v.h, last_relu_of_stripe[ts])
                if tb >= 2:
                    e.wait_ge(s_v.h, out_done[tb - 2])
                for jb in range(2):
                    for cb in range(NCB):
                        i = nc.tensor.matmul(
                            psA[:, pb * 1024 + jb * 512: pb * 1024 + (jb + 1) * 512],
                            attn_bf[:, cb * T + tb * 128: cb * T + (tb + 1) * 128],
                            wpj_bf[:, cb * D + jb * 512: cb * D + (jb + 1) * 512],
                            start=(cb == 0), stop=(cb == NCB - 1))
                e.wait_ge(s_s.h, cmark(ts * NHB + NHB - 1))
                h_v = h_e5[:].rearrange("p (h t) -> p h t", t=512)
                w2_v = w2_e5[:].rearrange("p (h j) -> p h j", j=D)
                for ns in range(2):
                    for hp in range(NHB // 2):
                        i = nc.tensor.matmul(
                            psB[:, pb * 1024 + ns * 512: pb * 1024 + (ns + 1) * 512],
                            h_v[:, 2 * hp:2 * hp + 2, tj * 128:(tj + 1) * 128],
                            w2_v[:, 2 * hp:2 * hp + 2, ns * 512:(ns + 1) * 512],
                            start=(hp == 0), stop=(hp == NHB // 2 - 1),
                            perf_mode=mybir.MatmulPerfMode.DoubleRow)
                        if hp == NHB // 2 - 1 and ns == 1:
                            s_t.inc(i)
            em("tensor", _tens)
            f2_mark = s_t.n
            if tj == 3:
                f2_t_mark.append(f2_mark)

            def _vec(e, tb=tb, pb=pb, buf=buf, f2_mark=f2_mark):
                e.wait_ge(s_t.h, f2_mark)
                e.wait_ge(s_x.h, s_x.n)
                if tb >= 2:
                    e.wait_ge(s_o.h, 16 * (tb - 1))
                dst = out_st[:, buf * D:(buf + 1) * D]
                e.tensor_tensor(dst, psA[:, pb * 1024:(pb + 1) * 1024], bp_sb[:], ALU.add)
                e.drain()
                e.tensor_tensor(dst, dst, x_st[:, buf * D:(buf + 1) * D], ALU.add)
                e.tensor_tensor(y2_bf[:], psB[:, pb * 1024:(pb + 1) * 1024], b2q[:], ALU.add)
                e.drain()
                e.tensor_tensor(y2_f32[:], y2_bf[:], mask_st[:, buf * D:(buf + 1) * D],
                                ALU.mult)
                e.drain()
                i = e.tensor_tensor(dst, dst, y2_f32[:], ALU.add)
                s_v.inc(i)
                out_done.append(s_v.n)
            em("vector", _vec)
            st_mark = s_v.n

            def _gp(e, tb=tb, buf=buf, st_mark=st_mark):
                e.wait_ge(s_v.h, st_mark)
                s_o.dinc(e.dma_start(out=out_h[tb * 128:(tb + 1) * 128, :],
                                    in_=out_st[:, buf * D:(buf + 1) * D]))
            em("gpsimd", _gp)


    em("scalar", _p45_scal)

    def _gp_final(e):
        e.wait_ge(s_o.h, s_o.n)
    em("gpsimd", _gp_final)

    for st in (esF, esG, esD, esB):
        st.close()
    es.close()
    sems.close()
    return nc


def _host_mask(T, Dm):
    import jax
    if os.environ.get("BASS_SIM_FP64") == "1":
        jax.config.update("jax_enable_x64", True)
    cpu = jax.devices("cpu")[0]
    with jax.default_device(cpu):
        keep = jax.random.bernoulli(
            jax.random.key(42, impl="threefry2x32"), 0.8, (8, T, Dm))
    keep = np.asarray(keep)
    return np.where(keep, np.float32(1.25), np.float32(0.0)).astype(ml_dtypes.bfloat16)


def _prep_inputs(L0_attn_x, L0_attn_w_qkv, L0_attn_w_proj, L0_attn_b_proj,
                 L0_ffn_W1, L0_ffn_b1, L0_ffn_W2, L0_ffn_b2):
    B, T, _ = L0_attn_x.shape
    w_qkv = np.asarray(L0_attn_w_qkv, dtype=np.float32)
    w_proj = np.asarray(L0_attn_w_proj, dtype=np.float32)
    W1 = np.asarray(L0_ffn_W1, dtype=np.float32)
    W2 = np.asarray(L0_ffn_W2, dtype=np.float32)

    wqk = np.ascontiguousarray(
        w_qkv[:2 * D].reshape(H, 128, NCB, 128).transpose(0, 3, 2, 1)).reshape(H, 128, D)
    wv = np.ascontiguousarray(w_qkv[2 * D:].T).reshape(NCB, 128, D)
    wpj = np.ascontiguousarray(w_proj.T).reshape(NCB, 128, D)
    w1 = np.ascontiguousarray(
        np.ascontiguousarray(W1.T).reshape(NCB, 128, NHB, 128).transpose(2, 1, 0, 3)
    ).reshape(NHB, 128, D)
    w2 = np.ascontiguousarray(W2.T).reshape(NHB, 128, D)
    b1t = np.ascontiguousarray(np.asarray(L0_ffn_b1, dtype=np.float32).reshape(NHB, 128).T)
    b2t = np.ascontiguousarray(
        np.broadcast_to(np.asarray(L0_ffn_b2, dtype=np.float32), (128, D)))
    bpt = np.ascontiguousarray(
        np.broadcast_to(np.asarray(L0_attn_b_proj, dtype=np.float32), (128, D)))
    mask = _host_mask(T, D)

    in_maps = []
    for b in range(B):
        in_maps.append({
            "x": np.ascontiguousarray(np.asarray(L0_attn_x[b], dtype=np.float32)),
            "mask": np.ascontiguousarray(mask[b]),
            "wqk": wqk, "wv": wv, "wpj": wpj, "w1": w1, "w2": w2,
            "b1t": b1t, "b2t": b2t, "bpt": bpt,
        })
    return in_maps, T


def kernel(L0_attn_x, L0_attn_w_qkv, L0_attn_w_proj, L0_attn_b_proj,
           L0_ffn_W1, L0_ffn_b1, L0_ffn_W2, L0_ffn_b2):
    in_maps, T = _prep_inputs(L0_attn_x, L0_attn_w_qkv, L0_attn_w_proj,
                              L0_attn_b_proj, L0_ffn_W1, L0_ffn_b1,
                              L0_ffn_W2, L0_ffn_b2)
    nc = build_nc(T=T)
    res = run_bass_kernel_spmd(nc, in_maps, core_ids=list(range(len(in_maps))))
    return np.stack([np.asarray(r["out"], dtype=np.float32) for r in res.results], axis=0)


# version 33
# speedup vs baseline: 1.0131x; 1.0038x over previous
import os
from contextlib import ExitStack

import numpy as np
import ml_dtypes

import concourse.bass as bass
import concourse.mybir as mybir
from concourse.bass_utils import run_bass_kernel_spmd
from concourse.alu_op_type import AluOpType as ALU

F32 = mybir.dt.float32
BF16 = mybir.dt.bfloat16
F8E5 = mybir.dt.float8e5
AF = mybir.ActivationFunctionType
AX = mybir.AxisListType.X

D = 1024
H = 16
NCB = D // 128
HID = 4096
NHB = HID // 128
C_SHIFT = 72.0
SCALE = 0.125


class Sem:
    def __init__(self, stack, nc, name):
        self.h = stack.enter_context(nc.semaphore(name))
        self.n = 0

    def inc(self, instr, k=1):
        instr.then_inc(self.h, k)
        self.n += k
        return instr

    def dinc(self, instr):
        if self.n > 0:
            instr.wait_op(self.h, self.n, "sem-ge")
        instr.then_inc(self.h, 16)
        self.n += 16
        return instr


def build_nc(T=2048):
    assert T % 512 == 0
    NTB = T // 128
    NQS = T // 512
    NKC = NTB
    nc = bass.Bass("TRN2", target_bir_lowering=False)

    x_h = nc.dram_tensor("x", [T, D], F32, kind="ExternalInput").ap()
    mask_h = nc.dram_tensor("mask", [T, D], BF16, kind="ExternalInput").ap()
    wqk_h = nc.dram_tensor("wqk", [H, 128, D], F32, kind="ExternalInput").ap()
    wv_h = nc.dram_tensor("wv", [NCB, 128, D], F32, kind="ExternalInput").ap()
    wpj_h = nc.dram_tensor("wpj", [NCB, 128, D], F32, kind="ExternalInput").ap()
    w1_h = nc.dram_tensor("w1", [NHB, 128, D], F32, kind="ExternalInput").ap()
    w2_h = nc.dram_tensor("w2", [NHB, 128, D], F32, kind="ExternalInput").ap()
    b1_h = nc.dram_tensor("b1t", [128, NHB], F32, kind="ExternalInput").ap()
    b2_h = nc.dram_tensor("b2t", [128, D], F32, kind="ExternalInput").ap()
    bp_h = nc.dram_tensor("bpt", [128, D], F32, kind="ExternalInput").ap()
    out_h = nc.dram_tensor("out", [T, D], F32, kind="ExternalOutput").ap()
    ident_h = nc.inline_tensor(np.eye(128, dtype=np.float32), name="ident_h").ap()
    ones_h = nc.inline_tensor(np.ones((128, 128), dtype=np.float32), name="ones_h").ap()

    es = ExitStack()
    sems = ExitStack()

    def sb(stack, name, shape, dt, side="left"):
        return stack.enter_context(nc.sbuf_tensor(name, shape, dt, side=side))

    consts = sb(es, "consts", [128, 8], F32)
    CONST_COLS = {}
    for v in (1.0 / D, -1.0 / D, 1e-5, SCALE, -C_SHIFT, 1.0):
        CONST_COLS[v] = len(CONST_COLS)
        nc.const_aps.register(F32, v, consts[:, CONST_COLS[v]:CONST_COLS[v] + 1])

    ident = sb(es, "ident", [128, 128], F32)
    ones_sb = sb(es, "ones_sb", [128, 128], F32)
    b1q = sb(es, "b1q", [128, NHB], F32)
    b1q_e5 = sb(es, "b1q_e5", [128, NHB], F8E5)
    b2q = sb(es, "b2q", [128, D], F32)
    b2q_e5 = sb(es, "b2q_e5", [128, D], F8E5)
    bp_sb = sb(es, "bp_sb", [128, D], F32)
    XST_LO = -(-nc.sbuf_base // 32) * 32
    x_st = sb(es, "x_st", [128, 3 * D], F32)
    wst = sb(es, "wst", [128, 2 * D], F32)
    stat = sb(es, "stat", [128, 18], F32)

    psA = es.enter_context(nc.psum_tensor("psA", [128, 2048], F32))
    psB = es.enter_context(nc.psum_tensor("psB", [128, 2048], F32))

    s_x = Sem(sems, nc, "s_x")
    s_w = Sem(sems, nc, "s_w")
    s_v = Sem(sems, nc, "s_v")
    s_s = Sem(sems, nc, "s_s")
    s_t = Sem(sems, nc, "s_t")
    s_o = Sem(sems, nc, "s_o")
    s_f = Sem(sems, nc, "s_f")

    block = es.enter_context(nc.Block())

    def em(engname, f):
        getattr(block, engname)(f)

    wst_chain = []

    def wst_wait(e, idx):
        if idx >= 2:
            sem, val = wst_chain[idx - 2]
            e.wait_ge(sem.h, val)

    def _setup_sync(e):
        s_w.dinc(e.dma_start(out=ident[:], in_=ident_h[:]))
        s_w.dinc(e.dma_start(out=ones_sb[:], in_=ones_h[:]))
        s_w.dinc(e.dma_start(out=b1q[:, 0:NHB], in_=b1_h[:]))
        s_w.dinc(e.dma_start(out=b2q[:], in_=b2_h[:]))
        s_w.dinc(e.dma_start(out=bp_sb[:], in_=bp_h[:]))
    em("sync", _setup_sync)

    def _setup_vec(e):
        for v, j in CONST_COLS.items():
            e.memset(consts[:, j:j + 1], v)
        e.wait_ge(s_w.h, s_w.n)
        e.tensor_copy(b1q_e5[:, 0:NHB], b1q[:, 0:NHB])
        e.tensor_copy(b2q_e5[:], b2q[:])
        e.drain()
        e.tensor_copy(b1q[:, 0:NHB], b1q_e5[:, 0:NHB])
        s_v.inc(e.tensor_copy(b2q[:], b2q_e5[:]))
    em("vector", _setup_vec)
    SETUP_V = s_v.n

    esB = ExitStack()
    xn_e5 = sb(esB, "xn_e5", [128, NCB * T], F8E5)
    esC = ExitStack()
    addr = {}

    def sb_rec(stack, name, shape, dt, side):
        hi = nc.sbuf_top
        t_ = sb(stack, name, shape, dt, side=side)
        addr[name] = (nc.sbuf_top, hi)
        return t_
    vt_bf = sb_rec(esC, "vt_bf", [128, NTB * (H * 65)], BF16, side="right")
    k_bf = sb_rec(esC, "k_bf", [128, NCB * T], BF16, side="right")
    q_bf = sb_rec(esC, "q_bf", [128, NCB * T], BF16, side="right")
    vt_v = vt_bf[:].rearrange("p (tb h d) -> p tb h d", h=H, d=65)
    W1_BYTES = NHB * D
    W1_OVERLAP = (addr["k_bf"][1] - addr["k_bf"][0] == W1_BYTES and
                  addr["vt_bf"][1] - addr["vt_bf"][0] >= W1_BYTES and
                  NQS >= 1 and (H // 2) > 0 and NHB % (H // 2) == 0)
    if W1_OVERLAP:
        w1_e5 = nc.alloc_sbuf_tensor_at(
            "w1_e5", [128, NHB * D], F8E5, offset=addr["k_bf"][0])
        w2_e5 = nc.alloc_sbuf_tensor_at(
            "w2_e5", [128, NHB * D], F8E5, offset=addr["vt_bf"][0])
    esA = ExitStack()
    xn_f32 = sb(esA, "xn_f32", [128, 3 * D], F32, side="right")
    xn_bf = sb(esA, "xn_bf", [128, NCB * T], BF16, side="right")
    wv_bf = sb(esA, "wv_bf", [128, NCB * D], BF16, side="right")
    wqk_b = sb(esA, "wqk_b", [128, 2 * D], BF16, side="right")
    xn_bf_v = xn_bf[:].rearrange("p (c t) -> p c t", t=T)
    xn_e5_v = xn_e5[:].rearrange("p (c t) -> p c t", t=T)

    evac_done = []
    p0_vec_mark = []
    xn_mark = []
    tr_marks = []
    P0_S0 = s_s.n

    def p0_smark(tb):
        return P0_S0 + tb + 1

    wv_cast = []
    v_free = []
    ONES_V_BOX = []
    KWV = max(1, (NCB + NTB - 1) // NTB)
    wv_sched = {}
    for cb_ in range(NCB):
        wv_sched.setdefault(cb_ // KWV, []).append(cb_)
    last_wv_tb = max(wv_sched)
    v_delay = min(last_wv_tb + 2, NTB - 1) if NTB > 1 else 1
    wv_dma_mark = {}
    wv_chain_idx = {}

    def emit_v(j):
        pb = j % 2

        def _tensV(e, j=j, pb=pb):
            if j < 2:
                e.wait_ge(s_f.h, wv_cast[-1])
            else:
                e.wait_ge(s_f.h, v_free[j - 2])
            e.wait_ge(s_f.h, evac_done[j])
            for cb in range(NCB):
                for vs in range(2):
                    i = nc.tensor.matmul(
                        psA[:, pb * 1024 + vs * 512: pb * 1024 + (vs + 1) * 512],
                        xn_bf[:, cb * T + j * 128: cb * T + (j + 1) * 128],
                        wv_bf[:, cb * D + vs * 512: cb * D + (vs + 1) * 512],
                        start=(cb == 0), stop=(cb == NCB - 1))
                    if cb == NCB - 1 and vs == 1:
                        s_t.inc(i)
        em("tensor", _tensV)

        def _vecV(e, j=j, pb=pb):
            e.wait_ge(s_t.h, s_t.n)
            i = e.tensor_copy(
                vt_v[:, j, :, 0:64],
                psA[:, pb * 1024: pb * 1024 + 1024].rearrange("p (h d) -> p h d", d=64))
            s_f.inc(i)
            v_free.append(s_f.n)
        em("vector", _vecV)

    for tb in range(NTB):
        buf = tb % 3
        nbuf = tb % 3
        xs = x_st[:, buf * D:(buf + 1) * D]
        xn = xn_f32[:, nbuf * D:(nbuf + 1) * D]
        sc = (tb % 3) * 6

        def _sync(e, tb=tb, xs=xs):
            if tb >= 3:
                e.wait_ge(s_v.h, xn_mark[tb - 3])
                e.wait_ge(s_s.h, p0_smark(tb - 3))
            s_x.dinc(e.dma_start(out=xs[:], in_=x_h[tb * 128:(tb + 1) * 128, :]))
            for cb in wv_sched.get(tb, ()):
                sidx = len(wst_chain)
                wst_wait(e, sidx)
                s_w.dinc(e.dma_start(out=wst[:, (cb % 2) * D:(cb % 2 + 1) * D],
                                     in_=wv_h[cb]))
                wv_dma_mark[cb] = s_w.n
                wv_chain_idx[cb] = sidx
                wst_chain.append((s_f, None))
        em("sync", _sync)

        if wv_sched.get(tb):
            def _vec_wv(e, cbs=wv_sched[tb]):
                for cb in cbs:
                    e.wait_ge(s_w.h, wv_dma_mark[cb])
                    i = e.tensor_copy(wv_bf[:, cb * D:(cb + 1) * D],
                                      wst[:, (cb % 2) * D:(cb % 2 + 1) * D])
                    s_f.inc(i)
                    wv_cast.append(s_f.n)
                    wst_chain[wv_chain_idx[cb]] = (s_f, s_f.n)
            em("vector", _vec_wv)
        if tb == last_wv_tb + 1 or (tb == last_wv_tb and tb == NTB - 1):
            def _vec_ones(e):
                s_v.inc(e.memset(vt_v[:, :, :, 64:65], 1.0))
            em("vector", _vec_ones)
            ONES_V_BOX.append(s_v.n)

        def emit_ln_stats(tb_):
            sbuf = tb_ % 3
            sxs = x_st[:, sbuf * D:(sbuf + 1) * D]
            ssc = (tb_ % 3) * 6

            def _vec(e, tb_=tb_, sxs=sxs, ssc=ssc):
                if tb_ == 0:
                    e.wait_ge(s_v.h, SETUP_V)
                if tb_ >= 3:
                    e.wait_ge(s_s.h, p0_smark(tb_ - 3))
                e.wait_ge(s_x.h, 16 * (tb_ + 1))
                e.reduce_sum(stat[:, ssc:ssc + 1], sxs[:], axis=AX)
                e.drain()
                i = e.tensor_scalar(stat[:, ssc + 1:ssc + 2], stat[:, ssc:ssc + 1],
                                    -1.0 / D, None, ALU.mult)
                s_v.inc(i)
                p0_vec_mark.append(s_v.n)
            em("vector", _vec)

        if tb == 0:
            emit_ln_stats(0)

        def _vec2(e, tb=tb, xs=xs, xn=xn, sc=sc):
            e.wait_ge(s_s.h, p0_smark(tb))
            e.reciprocal(stat[:, sc + 4:sc + 5], stat[:, sc + 3:sc + 4])
            e.drain()
            s_v.inc(e.tensor_scalar(xn[:], xs[:], stat[:, sc + 1:sc + 2],
                                    stat[:, sc + 4:sc + 5], ALU.add, ALU.mult))
            xn_mark.append(s_v.n)
        em("vector", _vec2)
        if tb + 1 < NTB:
            emit_ln_stats(tb + 1)

        def _tens(e, tb=tb, xn=xn, xm=xn_mark[tb]):
            e.wait_ge(s_v.h, xm)
            half = tb % 2
            if tb >= 2:
                e.wait_ge(s_f.h, evac_done[tb - 2])
            for cb in range(NCB):
                i = nc.tensor.transpose(
                    psB[:, half * 1024 + cb * 128: half * 1024 + (cb + 1) * 128],
                    xn[:, cb * 128:(cb + 1) * 128], ident[:])
                if cb == NCB - 1:
                    s_t.inc(i)
        em("tensor", _tens)
        tr_marks.append(s_t.n)

        def _vec3(e, tb=tb):
            half = tb % 2
            e.wait_ge(s_t.h, s_t.n)
            src = psB[:, half * 1024: half * 1024 + 1024].rearrange(
                "p (c t) -> p c t", t=128)
            e.tensor_copy(xn_bf_v[:, 0:NCB, tb * 128:(tb + 1) * 128], src)
            e.drain()
            i = e.tensor_copy(xn_e5_v[:, 0:NCB, tb * 128:(tb + 1) * 128], src)
            s_f.inc(i)
            evac_done.append(s_f.n)
        em("vector", _vec3)

        if tb >= v_delay:
            emit_v(tb - v_delay)

    for j in range(max(0, NTB - v_delay), NTB):
        emit_v(j)

    def _p0_scal(e):
        for tb in range(NTB):
            buf = tb % 3
            nbuf = tb % 3
            xs = x_st[:, buf * D:(buf + 1) * D]
            xn = xn_f32[:, nbuf * D:(nbuf + 1) * D]
            sc = (tb % 3) * 6
            e.wait_ge(s_v.h, p0_vec_mark[tb])
            if tb >= 3:
                e.wait_ge(s_t.h, tr_marks[tb - 3])
            e.activation(xn[:], xs[:], AF.Square, bias=stat[:, sc + 1:sc + 2],
                         accum_out=stat[:, sc + 2:sc + 3])
            e.drain()
            s_s.inc(e.activation(stat[:, sc + 3:sc + 4], stat[:, sc + 2:sc + 3],
                                 AF.Sqrt, scale=1.0 / D, bias=1e-5))
            assert s_s.n == p0_smark(tb)
    em("scalar", _p0_scal)

    P0_V = s_v.n
    P0_F = s_f.n

    qk_free = []
    fb_last_evac = {}
    for fb in range(H):
        buf = fb % 2
        sidx = len(wst_chain)

        def _sync(e, fb=fb, buf=buf, sidx=sidx):
            wst_wait(e, sidx)
            if fb >= 2:
                e.wait_ge(s_f.h, fb_last_evac[fb - 2])
            s_w.dinc(e.dma_start(out=wst[:, buf * D:(buf + 1) * D], in_=wqk_h[fb]))
        em("sync", _sync)

        def _vec(e, buf=buf):
            e.wait_ge(s_w.h, s_w.n)
            s_v.inc(e.tensor_copy(wqk_b[:, buf * D:(buf + 1) * D],
                                  wst[:, buf * D:(buf + 1) * D]))
        em("vector", _vec)
        cast_mark = s_v.n
        wst_chain.append((s_v, s_v.n))

        for qs in range(NQS):
            g = NQS * fb + qs

            def _tens(e, fb=fb, buf=buf, qs=qs, g=g, cast_mark=cast_mark):
                if qs == 0:
                    e.wait_ge(s_v.h, cast_mark)
                if fb == 0 and qs == 0:
                    e.wait_ge(s_f.h, P0_F)
                if g >= 4:
                    e.wait_ge(s_f.h, qk_free[g - 4])
                for cb in range(NCB):
                    i = nc.tensor.matmul(
                        psB[:, (g % 4) * 512:(g % 4 + 1) * 512],
                        wqk_b[:, buf * D + cb * 128: buf * D + (cb + 1) * 128],
                        xn_bf[:, cb * T + qs * 512: cb * T + (qs + 1) * 512],
                        start=(cb == 0), stop=(cb == NCB - 1))
                s_t.inc(i)
            em("tensor", _tens)

            def _vec2(e, fb=fb, qs=qs, g=g):
                e.wait_ge(s_t.h, s_t.n)
                dst = q_bf if fb < 8 else k_bf
                c0 = (fb % 8) * T + qs * 512
                i = e.tensor_copy(dst[:, c0:c0 + 512],
                                  psB[:, (g % 4) * 512:(g % 4) * 512 + 512])
                s_f.inc(i)
                qk_free.append(s_f.n)
            em("vector", _vec2)
        fb_last_evac[fb] = s_f.n

    ONES_V = ONES_V_BOX[0]
    P1_V = s_v.n
    P1_F = s_f.n

    esA.close()

    esC2 = ExitStack()
    probs = sb(esC2, "probs", [128, 8 * 512], BF16, side="right")
    x_att_tm = nc.alloc_sbuf_tensor_at("x_att_tm", [128, 1024], F32,
                                       offset=XST_LO)
    rcp = nc.alloc_sbuf_tensor_at("rcp", [128, 16], F32, offset=XST_LO + 4096)
    esD = ExitStack()
    attn_bf = sb(esD, "attn_bf", [128, NCB * T], BF16)
    wpj_bf = sb(esD, "wpj_bf", [128, NCB * D], BF16)

    pv_pair_mark = {}
    sc_marks = {}
    pv_done = {}
    norm_mark = {}
    trans_mark = {}
    evac_mark = {}

    def pvoff(rot, t):
        return (2 * rot + t // 4) * 512 + (t % 4) * 65

    def emit_norm_vec(j):
        rot = j % 2

        def _vec(e, j=j, rot=rot):
            e.wait_ge(s_t.h, pv_done[j])
            for b in range(2):
                bank = (2 * rot + b) * 512
                e.reciprocal(
                    rcp[:, rot * 8 + b * 4: rot * 8 + b * 4 + 4].rearrange(
                        "p (t c) -> p t c", c=1),
                    psB[:, bank: bank + 260].rearrange(
                        "p (t c) -> p t c", c=65)[:, :, 64:65])
            e.drain()
            for t in range(8):
                h2, qb = t // 4, t % 4
                i = e.tensor_scalar(
                    x_att_tm[:, rot * 512 + qb * 128 + h2 * 64:
                             rot * 512 + qb * 128 + h2 * 64 + 64],
                    psB[:, pvoff(rot, t): pvoff(rot, t) + 64],
                    rcp[:, rot * 8 + t: rot * 8 + t + 1], None, ALU.mult)
            s_v.inc(i)
            norm_mark[j] = s_v.n
        em("vector", _vec)

    def emit_trans(j):
        rot = j % 2

        def _tens(e, j=j, rot=rot):
            e.wait_ge(s_v.h, norm_mark[j])
            for qb in range(4):
                dst = (2 * rot + qb // 2) * 512 + (qb % 2) * 128
                i = nc.tensor.transpose(
                    psB[:, dst: dst + 128],
                    x_att_tm[:, rot * 512 + qb * 128: rot * 512 + qb * 128 + 128],
                    ident[:])
            s_t.inc(i)
            trans_mark[j] = s_t.n
        em("tensor", _tens)

    def emit_evac(j):
        rot = j % 2
        pj, qsj = j // NQS, j % NQS

        def _vec(e, j=j, rot=rot, pj=pj, qsj=qsj):
            e.wait_ge(s_t.h, trans_mark[j])
            c0 = pj * T + qsj * 512
            i = e.tensor_copy(
                attn_bf[:, c0:c0 + 512].rearrange("p (b c) -> p b c", c=256),
                psB[:, 2 * rot * 512: 2 * rot * 512 + 1024].rearrange(
                    "p (b c) -> p b c", c=512)[:, :, 0:256])
            s_v.inc(i)
            evac_mark[j] = s_v.n
        em("vector", _vec)

    wpj_cast = []
    for cb in range(NCB):
        buf = cb % 2
        sidx = len(wst_chain)

        def _sync(e, cb=cb, buf=buf, sidx=sidx):
            wst_wait(e, sidx)
            s_w.dinc(e.dma_start(out=wst[:, buf * D:(buf + 1) * D], in_=wpj_h[cb]))
        em("sync", _sync)

        def _vec(e, cb=cb, buf=buf):
            e.wait_ge(s_w.h, s_w.n)
            i = e.tensor_copy(wpj_bf[:, cb * D:(cb + 1) * D],
                              wst[:, buf * D:(buf + 1) * D])
            s_f.inc(i)
            wpj_cast.append(s_f.n)
        em("vector", _vec)
        wst_chain.append((s_f, s_f.n))
    WPJ_F = s_f.n

    wcast_mark = {}
    w1_dma_mark = {}
    W1_PER_PAIR = NHB // (H // 2) if W1_OVERLAP else 0

    w1_chain_idx = {}

    def emit_w1dma_p2(j):
        sidx = len(wst_chain)
        w1_chain_idx[j] = sidx

        def _sync(e, j=j, sidx=sidx):
            wst_wait(e, sidx)
            s_w.dinc(e.dma_start(out=wst[:, (j % 2) * D:(j % 2 + 1) * D],
                                 in_=w1_h[j]))
            w1_dma_mark[j] = s_w.n
        em("sync", _sync)
        wst_chain.append(None)

    def emit_w1cast_p2(j, pair_gate):
        def _vec(e, j=j, pair_gate=pair_gate):
            e.wait_ge(s_w.h, w1_dma_mark[j])
            e.wait_ge(s_t.h, pair_gate)
            i = e.tensor_copy(w1_e5[:, j * D:(j + 1) * D],
                              wst[:, (j % 2) * D:(j % 2 + 1) * D])
            s_v.inc(i)
            wcast_mark[('w1', j)] = s_v.n
            wst_chain[w1_chain_idx[j]] = (s_v, s_v.n)
        em("vector", _vec)

    P2_S0 = s_s.n

    def exp_mark(it, kc):
        return P2_S0 + it * NKC + kc + 1
    if W1_OVERLAP:
        emit_w1dma_p2(0)
        emit_w1dma_p2(1)
    w1_pending = []

    def emit_w1block(base, gate):
        emit_w1cast_p2(base, gate)
        emit_w1cast_p2(base + 1, gate)
        for j in (base + 2, base + 3):
            if j < NHB:
                emit_w1dma_p2(j)
        if base + 2 < NHB:
            emit_w1cast_p2(base + 2, gate)
        if base + 3 < NHB:
            emit_w1cast_p2(base + 3, gate)
        for j in (base + 4, base + 5):
            if j < NHB:
                emit_w1dma_p2(j)
    NG = (H // 2) * NQS * NKC

    def emit_scores_g(g):
        it_, kc = divmod(g, NKC)
        p_, qs_ = divmod(it_, NQS)
        rot = g % 2
        scA = psA[:, rot * 1024: rot * 1024 + 512]
        scB = psA[:, rot * 1024 + 512: rot * 1024 + 1024]

        def _tens(e, g=g, it_=it_, kc=kc, p_=p_, qs_=qs_, scA=scA, scB=scB):
            if g == 0:
                e.wait_ge(s_v.h, ONES_V)
                e.wait_ge(s_f.h, P1_F)
            if g >= 2:
                e.wait_ge(s_s.h, P2_S0 + g - 1)
            nc.tensor.matmul(
                scA, k_bf[0:64, p_ * T + kc * 128: p_ * T + (kc + 1) * 128],
                q_bf[0:64, p_ * T + qs_ * 512: p_ * T + (qs_ + 1) * 512],
                start=True, stop=True, tile_position=(0, 0))
            i = nc.tensor.matmul(
                scB, k_bf[64:128, p_ * T + kc * 128: p_ * T + (kc + 1) * 128],
                q_bf[64:128, p_ * T + qs_ * 512: p_ * T + (qs_ + 1) * 512],
                start=True, stop=True, tile_position=(64, 0))
            s_t.inc(i)
        em("tensor", _tens)
        it_, kc = divmod(g, NKC)
        sc_marks[(it_, kc)] = s_t.n

    emit_scores_g(0)
    emit_scores_g(1)
    for p in range(H // 2):
        for qs in range(NQS):
            it = p * NQS + qs

            def emit_pv(kc, p=p, it=it):
                rot3 = (it * NKC + kc) % 4
                rot = it % 2

                def _tens2(e, kc=kc, rot3=rot3, rot=rot):
                    e.wait_ge(s_s.h, exp_mark(it, kc))
                    if kc == 0 and it >= 2:
                        e.wait_ge(s_v.h, evac_mark[it - 2])
                    for t in range(8):
                        h2, qb = t // 4, t % 4
                        i = nc.tensor.matmul(
                            psB[:, pvoff(rot, t): pvoff(rot, t) + 65],
                            probs[:, rot3 * 1024 + h2 * 512 + qb * 128:
                                  rot3 * 1024 + h2 * 512 + qb * 128 + 128],
                            vt_bf[:, kc * (65 * H) + (2 * p + h2) * 65:
                                  kc * (65 * H) + (2 * p + h2) * 65 + 65],
                            start=(kc == 0 and t % 4 == 0),
                            stop=(kc == NKC - 1 and t % 4 == 3))
                    s_t.inc(i)
                    pv_pair_mark[(it, kc)] = s_t.n
                em("tensor", _tens2)

            if it >= 1:
                emit_norm_vec(it - 1)
            for kc in range(NKC):
                if kc == 2 and it >= 1:
                    emit_trans(it - 1)
                    emit_evac(it - 1)
                if kc == 3 and w1_pending:
                    emit_w1block(*w1_pending.pop(0))
                g = it * NKC + kc + 2
                if g < NG:
                    emit_scores_g(g)
                emit_pv(kc)
            pv_done[it] = s_t.n

            if W1_OVERLAP and qs == NQS - 1:
                w1_pending.append((W1_PER_PAIR * p, sc_marks[(it, NKC - 1)]))
    li = (H // 2) * NQS - 1
    emit_norm_vec(li)
    emit_trans(li)
    emit_evac(li)
    P2_V_CORE = s_v.n
    while w1_pending:
        emit_w1block(*w1_pending.pop(0))

    def _p2_scal(e):
        for it in range((H // 2) * NQS):
            for kc in range(NKC):
                rot2 = kc % 2
                rot3 = (it * NKC + kc) % 4
                e.wait_ge(s_t.h, sc_marks[(it, kc)])
                pit, pkc = (it, kc - 4) if kc >= 4 else (it - 1, NKC - 4 + kc)
                if (pit, pkc) in pv_pair_mark:
                    e.wait_ge(s_t.h, pv_pair_mark[(pit, pkc)])
                i = e.activation(probs[:, rot3 * 1024:(rot3 + 1) * 1024],
                                 psA[:, rot2 * 1024:(rot2 + 1) * 1024],
                                 AF.Exp, bias=-C_SHIFT, scale=SCALE)
                s_s.inc(i)
                assert s_s.n == exp_mark(it, kc)
    em("scalar", _p2_scal)

    P2_V = s_v.n
    P2_T = s_t.n

    esC2.close()
    esC.close()
    esG = ExitStack()
    h_e5 = sb(esG, "h_e5", [128, NHB * 512], F8E5)
    h_bfst = sb(esG, "h_bfst", [128, 4 * 512], BF16)
    y2_bf = sb(esG, "y2_bf", [128, D], BF16)
    y2_f32 = sb(esG, "y2_f32", [128, D], F32)
    mask_st = sb(esG, "mask_st", [128, 2 * D], BF16)
    out_st = sb(esG, "out_st", [128, 2 * D], F32)
    esF = ExitStack()
    if W1_OVERLAP:
        wst2 = sb(esF, "wst2", [128, 2 * D], F32)
        wst2b = wst
    else:
        wst2 = sb(esF, "wst2", [128, 4 * D], F32)
        wst2b = None

    def wst2_buf(i_):
        r = i_ % 4
        if W1_OVERLAP:
            t_ = wst2 if r < 2 else wst2b
            return t_[:, (r % 2) * D:(r % 2 + 1) * D]
        return wst2[:, r * D:(r + 1) * D]
    if W1_OVERLAP:
        assert nc.sbuf_base <= addr["k_bf"][0], (nc.sbuf_base, addr)
    else:
        w1_e5 = sb(esF, "w1_e5", [128, NHB * D], F8E5)
        w2_e5 = sb(esF, "w2_e5", [128, NHB * D], F8E5)

    NSTR = T // 512
    h_cast = {}
    P45_S0 = s_s.n

    def cmark(idx):
        return P45_S0 + idx + 1
    out_done = []
    f2_t_mark = []
    f1_marks = {}
    P2_S_END = s_s.n

    s_gs = [Sem(sems, nc, "s_g%d" % i) for i in range(4)]
    s_g = s_gs[0]
    w_dma_mark = {}

    def emit_w1dma(i_):
        def _sync(e, i_=i_):
            if i_ < 2:
                sem, val = wst_chain[len(wst_chain) - 2 + i_]
                e.wait_ge(sem.h, val)
            else:
                e.wait_ge(s_v.h, wcast_mark[('w1', i_ - 2)])
            s_w.dinc(e.dma_start(out=wst[:, (i_ % 2) * D:(i_ % 2 + 1) * D],
                                 in_=w1_h[i_]))
            w_dma_mark[('w1', i_)] = (s_w, s_w.n)
        em("sync", _sync)

    def emit_w2dma(i_):
        def _gp(e, i_=i_):
            if i_ < 4:
                e.wait_ge(s_t.h, P2_T)
                if W1_OVERLAP and i_ >= 2:
                    e.wait_ge(s_v.h, wcast_mark[('w1', NHB - 2 + (i_ - 2))])
            else:
                e.wait_ge(s_v.h, wcast_mark[('w2', i_ - 4)])
            ch = s_gs[i_ % 4]
            ch.dinc(e.dma_start(out=wst2_buf(i_), in_=w2_h[i_]))
            w_dma_mark[('w2', i_)] = (ch, ch.n)
        em("gpsimd", _gp)

    def emit_wcast(e, kind, i_):
        sem, val = w_dma_mark[(kind, i_)]
        e.wait_ge(sem.h, val)
        if kind == 'w1':
            dst = w1_e5[:, i_ * D:(i_ + 1) * D]
            srcb = wst[:, (i_ % 2) * D:(i_ % 2 + 1) * D]
        else:
            dst = w2_e5[:, (i_) * D:(i_ + 1) * D]
            srcb = wst2_buf(i_)
        s_v.inc(e.tensor_copy(dst, srcb))
        wcast_mark[(kind, i_)] = s_v.n

    for i_ in (0, 1):
        if not W1_OVERLAP:
            emit_w1dma(i_)
    for i_ in range(min(4, NHB)):
        emit_w2dma(i_)

    def _wvec0(e):
        e.wait_ge(s_t.h, P2_T)
        if not W1_OVERLAP:
            emit_wcast(e, 'w1', 0)
            emit_wcast(e, 'w1', 1)
        emit_wcast(e, 'w2', 0)
        emit_wcast(e, 'w2', 1)
    em("vector", _wvec0)

    def _p45_scal(e):
        for ts_ in range(NSTR):
            for hb_ in range(NHB):
                rot_ = hb_ % 4
                idx_ = ts_ * NHB + hb_
                e.wait_ge(s_v.h, h_cast[idx_])
                i = e.activation(h_e5[:, hb_ * 512:(hb_ + 1) * 512],
                                 h_bfst[:, rot_ * 512:(rot_ + 1) * 512], AF.Copy)
                s_s.inc(i)
                assert s_s.n == cmark(idx_)

    last_relu_of_stripe = {}
    for ts in range(NSTR):
        for hb in range(NHB):
            rot = hb % 4
            idx = ts * NHB + hb

            if ts == 0:
                if not W1_OVERLAP and hb + 2 < NHB:
                    emit_w1dma(hb + 2)
                if hb + 4 < NHB:
                    emit_w2dma(hb + 4)

            def _tens(e, ts=ts, hb=hb, rot=rot, idx=idx):
                if idx == 0:
                    e.wait_ge(s_s.h, P2_S_END)
                    e.wait_ge(s_v.h, P2_V_CORE)
                if ts == 0:
                    e.wait_ge(s_v.h, wcast_mark[('w1', hb)])
                if hb == 0 and ts >= 1:
                    e.wait_ge(s_v.h, out_done[-2])
                if hb == 2 and ts >= 1:
                    e.wait_ge(s_v.h, out_done[-1])
                if idx >= 4:
                    e.wait_ge(s_v.h, h_cast[idx - 4])
                for cp in range(NCB // 2):
                    i = nc.tensor.matmul(
                        psA[:, rot * 512:(rot + 1) * 512],
                        w1_e5[:, hb * D + 2 * cp * 128: hb * D + (2 * cp + 2) * 128]
                            .rearrange("p (k m) -> p k m", k=2),
                        xn_e5_v[:, 2 * cp:2 * cp + 2, ts * 512:(ts + 1) * 512],
                        start=(cp == 0), stop=(cp == NCB // 2 - 1),
                        perf_mode=mybir.MatmulPerfMode.DoubleRow)
                    if cp == NCB // 2 - 1:
                        s_t.inc(i)
            em("tensor", _tens)
            f1_marks[idx] = s_t.n

            def _vec(e, ts=ts, hb=hb, rot=rot, idx=idx):
                if ts == 0 and hb + 2 < NHB:
                    if not W1_OVERLAP:
                        emit_wcast(e, 'w1', hb + 2)
                    emit_wcast(e, 'w2', hb + 2)
                e.wait_ge(s_t.h, f1_marks[idx])
                if ts >= 1 and hb == 0:
                    e.wait_ge(s_t.h, f2_t_mark[ts - 1])
                if idx >= 4:
                    e.wait_ge(s_s.h, cmark(idx - 4))
                i = e.tensor_scalar(h_bfst[:, rot * 512:(rot + 1) * 512],
                                    psA[:, rot * 512:(rot + 1) * 512],
                                    b1q[:, hb:hb + 1], 0.0, ALU.add, ALU.max)
                s_v.inc(i)
                h_cast[idx] = s_v.n
            em("vector", _vec)
        H_STRIPE_V = s_v.n
        last_relu_of_stripe[ts] = h_cast[ts * NHB + NHB - 1]

        for tj in range(4):
            tb = ts * 4 + tj
            buf = tb % 2
            pb = tb % 2

            def _sync(e, tb=tb, buf=buf):
                if tb < 2:
                    e.wait_ge(s_v.h, P2_V)
                else:
                    e.wait_ge(s_v.h, out_done[tb - 2])
                s_x.dinc(e.dma_start(out=x_st[:, buf * D:(buf + 1) * D],
                                    in_=x_h[tb * 128:(tb + 1) * 128, :]))
                s_x.dinc(e.dma_start(out=mask_st[:, buf * D:(buf + 1) * D],
                                    in_=mask_h[tb * 128:(tb + 1) * 128, :]))
            em("sync", _sync)

            def _tens(e, ts=ts, tj=tj, tb=tb, pb=pb):
                if tb < 2:
                    e.wait_ge(s_f.h, WPJ_F)
                if tj <= 1:
                    e.wait_ge(s_v.h, last_relu_of_stripe[ts])
                if tb >= 2:
                    e.wait_ge(s_v.h, out_done[tb - 2])
                for jb in range(2):
                    for cb in range(NCB):
                        i = nc.tensor.matmul(
                            psA[:, pb * 1024 + jb * 512: pb * 1024 + (jb + 1) * 512],
                            attn_bf[:, cb * T + tb * 128: cb * T + (tb + 1) * 128],
                            wpj_bf[:, cb * D + jb * 512: cb * D + (jb + 1) * 512],
                            start=(cb == 0), stop=(cb == NCB - 1))
                e.wait_ge(s_s.h, cmark(ts * NHB + NHB - 1))
                h_v = h_e5[:].rearrange("p (h t) -> p h t", t=512)
                w2_v = w2_e5[:].rearrange("p (h j) -> p h j", j=D)
                for ns in range(2):
                    for hp in range(NHB // 2):
                        i = nc.tensor.matmul(
                            psB[:, pb * 1024 + ns * 512: pb * 1024 + (ns + 1) * 512],
                            h_v[:, 2 * hp:2 * hp + 2, tj * 128:(tj + 1) * 128],
                            w2_v[:, 2 * hp:2 * hp + 2, ns * 512:(ns + 1) * 512],
                            start=(hp == 0), stop=(hp == NHB // 2 - 1),
                            perf_mode=mybir.MatmulPerfMode.DoubleRow)
                        if hp == NHB // 2 - 1 and ns == 1:
                            s_t.inc(i)
            em("tensor", _tens)
            f2_mark = s_t.n
            if tj == 3:
                f2_t_mark.append(f2_mark)

            def _vec(e, tb=tb, pb=pb, buf=buf, f2_mark=f2_mark):
                e.wait_ge(s_t.h, f2_mark)
                e.wait_ge(s_x.h, s_x.n)
                if tb >= 2:
                    e.wait_ge(s_o.h, 16 * (tb - 1))
                dst = out_st[:, buf * D:(buf + 1) * D]
                e.tensor_tensor(dst, psA[:, pb * 1024:(pb + 1) * 1024], bp_sb[:], ALU.add)
                e.drain()
                e.tensor_tensor(dst, dst, x_st[:, buf * D:(buf + 1) * D], ALU.add)
                e.tensor_tensor(y2_bf[:], psB[:, pb * 1024:(pb + 1) * 1024], b2q[:], ALU.add)
                e.drain()
                e.tensor_tensor(y2_f32[:], y2_bf[:], mask_st[:, buf * D:(buf + 1) * D],
                                ALU.mult)
                e.drain()
                i = e.tensor_tensor(dst, dst, y2_f32[:], ALU.add)
                s_v.inc(i)
                out_done.append(s_v.n)
            em("vector", _vec)
            st_mark = s_v.n

            def _gp(e, tb=tb, buf=buf, st_mark=st_mark):
                e.wait_ge(s_v.h, st_mark)
                s_o.dinc(e.dma_start(out=out_h[tb * 128:(tb + 1) * 128, :],
                                    in_=out_st[:, buf * D:(buf + 1) * D]))
            em("gpsimd", _gp)


    em("scalar", _p45_scal)

    def _gp_final(e):
        e.wait_ge(s_o.h, s_o.n)
    em("gpsimd", _gp_final)

    for st in (esF, esG, esD, esB):
        st.close()
    es.close()
    sems.close()
    return nc


def _host_mask(T, Dm):
    import jax
    if os.environ.get("BASS_SIM_FP64") == "1":
        jax.config.update("jax_enable_x64", True)
    cpu = jax.devices("cpu")[0]
    with jax.default_device(cpu):
        keep = jax.random.bernoulli(
            jax.random.key(42, impl="threefry2x32"), 0.8, (8, T, Dm))
    keep = np.asarray(keep)
    return np.where(keep, np.float32(1.25), np.float32(0.0)).astype(ml_dtypes.bfloat16)


def _prep_inputs(L0_attn_x, L0_attn_w_qkv, L0_attn_w_proj, L0_attn_b_proj,
                 L0_ffn_W1, L0_ffn_b1, L0_ffn_W2, L0_ffn_b2):
    B, T, _ = L0_attn_x.shape
    w_qkv = np.asarray(L0_attn_w_qkv, dtype=np.float32)
    w_proj = np.asarray(L0_attn_w_proj, dtype=np.float32)
    W1 = np.asarray(L0_ffn_W1, dtype=np.float32)
    W2 = np.asarray(L0_ffn_W2, dtype=np.float32)

    wqk = np.ascontiguousarray(
        w_qkv[:2 * D].reshape(H, 128, NCB, 128).transpose(0, 3, 2, 1)).reshape(H, 128, D)
    wv = np.ascontiguousarray(w_qkv[2 * D:].T).reshape(NCB, 128, D)
    wpj = np.ascontiguousarray(w_proj.T).reshape(NCB, 128, D)
    w1 = np.ascontiguousarray(
        np.ascontiguousarray(W1.T).reshape(NCB, 128, NHB, 128).transpose(2, 1, 0, 3)
    ).reshape(NHB, 128, D)
    w2 = np.ascontiguousarray(W2.T).reshape(NHB, 128, D)
    b1t = np.ascontiguousarray(np.asarray(L0_ffn_b1, dtype=np.float32).reshape(NHB, 128).T)
    b2t = np.ascontiguousarray(
        np.broadcast_to(np.asarray(L0_ffn_b2, dtype=np.float32), (128, D)))
    bpt = np.ascontiguousarray(
        np.broadcast_to(np.asarray(L0_attn_b_proj, dtype=np.float32), (128, D)))
    mask = _host_mask(T, D)

    in_maps = []
    for b in range(B):
        in_maps.append({
            "x": np.ascontiguousarray(np.asarray(L0_attn_x[b], dtype=np.float32)),
            "mask": np.ascontiguousarray(mask[b]),
            "wqk": wqk, "wv": wv, "wpj": wpj, "w1": w1, "w2": w2,
            "b1t": b1t, "b2t": b2t, "bpt": bpt,
        })
    return in_maps, T


def kernel(L0_attn_x, L0_attn_w_qkv, L0_attn_w_proj, L0_attn_b_proj,
           L0_ffn_W1, L0_ffn_b1, L0_ffn_W2, L0_ffn_b2):
    in_maps, T = _prep_inputs(L0_attn_x, L0_attn_w_qkv, L0_attn_w_proj,
                              L0_attn_b_proj, L0_ffn_W1, L0_ffn_b1,
                              L0_ffn_W2, L0_ffn_b2)
    nc = build_nc(T=T)
    res = run_bass_kernel_spmd(nc, in_maps, core_ids=list(range(len(in_maps))))
    return np.stack([np.asarray(r["out"], dtype=np.float32) for r in res.results], axis=0)


# version 37 (final)
# speedup vs baseline: 1.0151x; 1.0020x over previous
import os
from contextlib import ExitStack

import numpy as np
import ml_dtypes

import concourse.bass as bass
import concourse.mybir as mybir
from concourse.bass_utils import run_bass_kernel_spmd
from concourse.alu_op_type import AluOpType as ALU

F32 = mybir.dt.float32
BF16 = mybir.dt.bfloat16
F8E5 = mybir.dt.float8e5
AF = mybir.ActivationFunctionType
AX = mybir.AxisListType.X

D = 1024
H = 16
NCB = D // 128
HID = 4096
NHB = HID // 128
C_SHIFT = 72.0
SCALE = 0.125


class Sem:
    def __init__(self, stack, nc, name):
        self.h = stack.enter_context(nc.semaphore(name))
        self.n = 0

    def inc(self, instr, k=1):
        instr.then_inc(self.h, k)
        self.n += k
        return instr

    def dinc(self, instr):
        if self.n > 0:
            instr.wait_op(self.h, self.n, "sem-ge")
        instr.then_inc(self.h, 16)
        self.n += 16
        return instr


def build_nc(T=2048):
    assert T % 512 == 0
    NTB = T // 128
    NQS = T // 512
    NKC = NTB
    nc = bass.Bass("TRN2", target_bir_lowering=False)

    x_h = nc.dram_tensor("x", [T, D], F32, kind="ExternalInput").ap()
    mask_h = nc.dram_tensor("mask", [T, D], BF16, kind="ExternalInput").ap()
    wqk_h = nc.dram_tensor("wqk", [H, 128, D], F32, kind="ExternalInput").ap()
    wv_h = nc.dram_tensor("wv", [NCB, 128, D], F32, kind="ExternalInput").ap()
    wpj_h = nc.dram_tensor("wpj", [NCB, 128, D], F32, kind="ExternalInput").ap()
    w1_h = nc.dram_tensor("w1", [NHB, 128, D], F32, kind="ExternalInput").ap()
    w2_h = nc.dram_tensor("w2", [NHB, 128, D], F32, kind="ExternalInput").ap()
    b1_h = nc.dram_tensor("b1t", [128, NHB], F32, kind="ExternalInput").ap()
    b2_h = nc.dram_tensor("b2t", [128, D], F32, kind="ExternalInput").ap()
    bp_h = nc.dram_tensor("bpt", [128, D], F32, kind="ExternalInput").ap()
    out_h = nc.dram_tensor("out", [T, D], F32, kind="ExternalOutput").ap()
    ident_h = nc.inline_tensor(np.eye(128, dtype=np.float32), name="ident_h").ap()
    ones_h = nc.inline_tensor(np.ones((128, 128), dtype=np.float32), name="ones_h").ap()

    es = ExitStack()
    sems = ExitStack()

    def sb(stack, name, shape, dt, side="left"):
        return stack.enter_context(nc.sbuf_tensor(name, shape, dt, side=side))

    consts = sb(es, "consts", [128, 8], F32)
    CONST_COLS = {}
    for v in (1.0 / D, -1.0 / D, 1e-5, SCALE, -C_SHIFT, 1.0):
        CONST_COLS[v] = len(CONST_COLS)
        nc.const_aps.register(F32, v, consts[:, CONST_COLS[v]:CONST_COLS[v] + 1])

    ident = sb(es, "ident", [128, 128], F32)
    ones_sb = sb(es, "ones_sb", [128, 128], F32)
    b1q = sb(es, "b1q", [128, NHB], F32)
    b1q_e5 = sb(es, "b1q_e5", [128, NHB], F8E5)
    b2q = sb(es, "b2q", [128, D], F32)
    b2q_e5 = sb(es, "b2q_e5", [128, D], F8E5)
    bp_sb = sb(es, "bp_sb", [128, D], F32)
    XST_LO = -(-nc.sbuf_base // 32) * 32
    x_st = sb(es, "x_st", [128, 3 * D], F32)
    wst = sb(es, "wst", [128, 2 * D], F32)
    stat = sb(es, "stat", [128, 18], F32)

    psA = es.enter_context(nc.psum_tensor("psA", [128, 2048], F32))
    psB = es.enter_context(nc.psum_tensor("psB", [128, 2048], F32))

    s_x = Sem(sems, nc, "s_x")
    s_w = Sem(sems, nc, "s_w")
    s_v = Sem(sems, nc, "s_v")
    s_s = Sem(sems, nc, "s_s")
    s_t = Sem(sems, nc, "s_t")
    s_o = Sem(sems, nc, "s_o")
    s_f = Sem(sems, nc, "s_f")

    block = es.enter_context(nc.Block())

    def em(engname, f):
        getattr(block, engname)(f)

    wst_chain = []

    def wst_wait(e, idx):
        if idx >= 2:
            sem, val = wst_chain[idx - 2]
            e.wait_ge(sem.h, val)

    def _setup_sync(e):
        s_w.dinc(e.dma_start(out=ident[:], in_=ident_h[:]))
        s_w.dinc(e.dma_start(out=ones_sb[:], in_=ones_h[:]))
        s_w.dinc(e.dma_start(out=b1q[:, 0:NHB], in_=b1_h[:]))
        s_w.dinc(e.dma_start(out=b2q[:], in_=b2_h[:]))
        s_w.dinc(e.dma_start(out=bp_sb[:], in_=bp_h[:]))
    em("sync", _setup_sync)

    def _setup_vec(e):
        for v, j in CONST_COLS.items():
            e.memset(consts[:, j:j + 1], v)
        e.wait_ge(s_w.h, s_w.n)
        e.tensor_copy(b1q_e5[:, 0:NHB], b1q[:, 0:NHB])
        e.tensor_copy(b2q_e5[:], b2q[:])
        e.drain()
        e.tensor_copy(b1q[:, 0:NHB], b1q_e5[:, 0:NHB])
        s_v.inc(e.tensor_copy(b2q[:], b2q_e5[:]))
    em("vector", _setup_vec)
    SETUP_V = s_v.n

    esB = ExitStack()
    xn_e5 = sb(esB, "xn_e5", [128, NCB * T], F8E5)
    esC = ExitStack()
    addr = {}

    def sb_rec(stack, name, shape, dt, side):
        hi = nc.sbuf_top
        t_ = sb(stack, name, shape, dt, side=side)
        addr[name] = (nc.sbuf_top, hi)
        return t_
    vt_bf = sb_rec(esC, "vt_bf", [128, NTB * (H * 65)], BF16, side="right")
    k_bf = sb_rec(esC, "k_bf", [128, NCB * T], BF16, side="right")
    q_bf = sb_rec(esC, "q_bf", [128, NCB * T], BF16, side="right")
    vt_v = vt_bf[:].rearrange("p (tb h d) -> p tb h d", h=H, d=65)
    W1_BYTES = NHB * D
    W1_OVERLAP = (addr["k_bf"][1] - addr["k_bf"][0] == W1_BYTES and
                  addr["vt_bf"][1] - addr["vt_bf"][0] >= W1_BYTES and
                  NQS >= 1 and (H // 2) > 0 and NHB % (H // 2) == 0)
    if W1_OVERLAP:
        w1_e5 = nc.alloc_sbuf_tensor_at(
            "w1_e5", [128, NHB * D], F8E5, offset=addr["k_bf"][0])
        w2_e5 = nc.alloc_sbuf_tensor_at(
            "w2_e5", [128, NHB * D], F8E5, offset=addr["vt_bf"][0])
    esA = ExitStack()
    xn_f32 = sb(esA, "xn_f32", [128, 3 * D], F32, side="right")
    xn_bf = sb(esA, "xn_bf", [128, NCB * T], BF16, side="right")
    wv_bf = sb(esA, "wv_bf", [128, NCB * D], BF16, side="right")
    wqk_b = sb(esA, "wqk_b", [128, 2 * D], BF16, side="right")
    xn_bf_v = xn_bf[:].rearrange("p (c t) -> p c t", t=T)
    xn_e5_v = xn_e5[:].rearrange("p (c t) -> p c t", t=T)

    evac_done = []
    p0_vec_mark = []
    xn_mark = []
    tr_marks = []
    P0_S0 = s_s.n

    def p0_smark(tb):
        return P0_S0 + tb + 1

    wv_cast = []
    v_free = []
    ONES_V_BOX = []
    KWV = max(1, (NCB + NTB - 1) // NTB)
    wv_sched = {}
    for cb_ in range(NCB):
        wv_sched.setdefault(cb_ // KWV, []).append(cb_)
    last_wv_tb = max(wv_sched)
    v_delay = min(last_wv_tb + 2, NTB - 1) if NTB > 1 else 1
    wv_dma_mark = {}
    wv_chain_idx = {}

    def emit_v(j):
        pb = j % 2

        def _tensV(e, j=j, pb=pb):
            if j < 2:
                e.wait_ge(s_f.h, wv_cast[-1])
            else:
                e.wait_ge(s_f.h, v_free[j - 2])
            e.wait_ge(s_f.h, evac_done[j])
            for cb in range(NCB):
                for vs in range(2):
                    i = nc.tensor.matmul(
                        psA[:, pb * 1024 + vs * 512: pb * 1024 + (vs + 1) * 512],
                        xn_bf[:, cb * T + j * 128: cb * T + (j + 1) * 128],
                        wv_bf[:, cb * D + vs * 512: cb * D + (vs + 1) * 512],
                        start=(cb == 0), stop=(cb == NCB - 1))
                    if cb == NCB - 1 and vs == 1:
                        s_t.inc(i)
        em("tensor", _tensV)

        def _vecV(e, j=j, pb=pb):
            e.wait_ge(s_t.h, s_t.n)
            i = e.tensor_copy(
                vt_v[:, j, :, 0:64],
                psA[:, pb * 1024: pb * 1024 + 1024].rearrange("p (h d) -> p h d", d=64))
            s_f.inc(i)
            v_free.append(s_f.n)
        em("vector", _vecV)

    for tb in range(NTB):
        buf = tb % 3
        nbuf = tb % 3
        xs = x_st[:, buf * D:(buf + 1) * D]
        xn = xn_f32[:, nbuf * D:(nbuf + 1) * D]
        sc = (tb % 3) * 6

        def _sync(e, tb=tb, xs=xs):
            if tb >= 3:
                e.wait_ge(s_v.h, xn_mark[tb - 3])
                e.wait_ge(s_s.h, p0_smark(tb - 3))
            s_x.dinc(e.dma_start(out=xs[:], in_=x_h[tb * 128:(tb + 1) * 128, :]))
            for cb in wv_sched.get(tb, ()):
                sidx = len(wst_chain)
                wst_wait(e, sidx)
                s_w.dinc(e.dma_start(out=wst[:, (cb % 2) * D:(cb % 2 + 1) * D],
                                     in_=wv_h[cb]))
                wv_dma_mark[cb] = s_w.n
                wv_chain_idx[cb] = sidx
                wst_chain.append((s_f, None))
        em("sync", _sync)

        if wv_sched.get(tb):
            def _vec_wv(e, cbs=wv_sched[tb]):
                for cb in cbs:
                    e.wait_ge(s_w.h, wv_dma_mark[cb])
                    i = e.tensor_copy(wv_bf[:, cb * D:(cb + 1) * D],
                                      wst[:, (cb % 2) * D:(cb % 2 + 1) * D])
                    s_f.inc(i)
                    wv_cast.append(s_f.n)
                    wst_chain[wv_chain_idx[cb]] = (s_f, s_f.n)
            em("vector", _vec_wv)
        if tb == last_wv_tb + 1 or (tb == last_wv_tb and tb == NTB - 1):
            def _vec_ones(e):
                s_v.inc(e.memset(vt_v[:, :, :, 64:65], 1.0))
            em("vector", _vec_ones)
            ONES_V_BOX.append(s_v.n)

        def emit_ln_stats(tb_):
            sbuf = tb_ % 3
            sxs = x_st[:, sbuf * D:(sbuf + 1) * D]
            ssc = (tb_ % 3) * 6

            def _vec(e, tb_=tb_, sxs=sxs, ssc=ssc):
                if tb_ == 0:
                    e.wait_ge(s_v.h, SETUP_V)
                if tb_ >= 3:
                    e.wait_ge(s_s.h, p0_smark(tb_ - 3))
                e.wait_ge(s_x.h, 16 * (tb_ + 1))
                e.reduce_sum(stat[:, ssc:ssc + 1], sxs[:], axis=AX)
                e.drain()
                i = e.tensor_scalar(stat[:, ssc + 1:ssc + 2], stat[:, ssc:ssc + 1],
                                    -1.0 / D, None, ALU.mult)
                s_v.inc(i)
                p0_vec_mark.append(s_v.n)
            em("vector", _vec)

        if tb == 0:
            emit_ln_stats(0)

        def _vec2(e, tb=tb, xs=xs, xn=xn, sc=sc):
            e.wait_ge(s_s.h, p0_smark(tb))
            e.reciprocal(stat[:, sc + 4:sc + 5], stat[:, sc + 3:sc + 4])
            e.drain()
            s_v.inc(e.tensor_scalar(xn[:], xs[:], stat[:, sc + 1:sc + 2],
                                    stat[:, sc + 4:sc + 5], ALU.add, ALU.mult))
            xn_mark.append(s_v.n)
        em("vector", _vec2)
        if tb + 1 < NTB:
            emit_ln_stats(tb + 1)

        def _tens(e, tb=tb, xn=xn, xm=xn_mark[tb]):
            e.wait_ge(s_v.h, xm)
            half = tb % 2
            if tb >= 2:
                e.wait_ge(s_f.h, evac_done[tb - 2])
            for cb in range(NCB):
                i = nc.tensor.transpose(
                    psB[:, half * 1024 + cb * 128: half * 1024 + (cb + 1) * 128],
                    xn[:, cb * 128:(cb + 1) * 128], ident[:])
                if cb == NCB - 1:
                    s_t.inc(i)
        em("tensor", _tens)
        tr_marks.append(s_t.n)

        def _vec3(e, tb=tb):
            half = tb % 2
            e.wait_ge(s_t.h, s_t.n)
            src = psB[:, half * 1024: half * 1024 + 1024].rearrange(
                "p (c t) -> p c t", t=128)
            e.tensor_copy(xn_bf_v[:, 0:NCB, tb * 128:(tb + 1) * 128], src)
            e.drain()
            i = e.tensor_copy(xn_e5_v[:, 0:NCB, tb * 128:(tb + 1) * 128], src)
            s_f.inc(i)
            evac_done.append(s_f.n)
        em("vector", _vec3)

        if tb >= v_delay:
            emit_v(tb - v_delay)

    for j in range(max(0, NTB - v_delay), NTB):
        emit_v(j)

    def _p0_scal(e):
        for tb in range(NTB):
            buf = tb % 3
            nbuf = tb % 3
            xs = x_st[:, buf * D:(buf + 1) * D]
            xn = xn_f32[:, nbuf * D:(nbuf + 1) * D]
            sc = (tb % 3) * 6
            e.wait_ge(s_v.h, p0_vec_mark[tb])
            if tb >= 3:
                e.wait_ge(s_t.h, tr_marks[tb - 3])
            e.activation(xn[:], xs[:], AF.Square, bias=stat[:, sc + 1:sc + 2],
                         accum_out=stat[:, sc + 2:sc + 3])
            e.drain()
            s_s.inc(e.activation(stat[:, sc + 3:sc + 4], stat[:, sc + 2:sc + 3],
                                 AF.Sqrt, scale=1.0 / D, bias=1e-5))
            assert s_s.n == p0_smark(tb)
    em("scalar", _p0_scal)

    P0_V = s_v.n
    P0_F = s_f.n

    qk_free = []
    fb_last_evac = {}
    for fb in range(H):
        buf = fb % 2
        sidx = len(wst_chain)

        def _sync(e, fb=fb, buf=buf, sidx=sidx):
            wst_wait(e, sidx)
            if fb >= 2:
                e.wait_ge(s_f.h, fb_last_evac[fb - 2])
            s_w.dinc(e.dma_start(out=wst[:, buf * D:(buf + 1) * D], in_=wqk_h[fb]))
        em("sync", _sync)

        def _vec(e, buf=buf):
            e.wait_ge(s_w.h, s_w.n)
            s_v.inc(e.tensor_copy(wqk_b[:, buf * D:(buf + 1) * D],
                                  wst[:, buf * D:(buf + 1) * D]))
        em("vector", _vec)
        cast_mark = s_v.n
        wst_chain.append((s_v, s_v.n))

        for qs in range(NQS):
            g = NQS * fb + qs

            def _tens(e, fb=fb, buf=buf, qs=qs, g=g, cast_mark=cast_mark):
                if qs == 0:
                    e.wait_ge(s_v.h, cast_mark)
                if fb == 0 and qs == 0:
                    e.wait_ge(s_f.h, P0_F)
                if g >= 4:
                    e.wait_ge(s_f.h, qk_free[g - 4])
                for cb in range(NCB):
                    i = nc.tensor.matmul(
                        psB[:, (g % 4) * 512:(g % 4 + 1) * 512],
                        wqk_b[:, buf * D + cb * 128: buf * D + (cb + 1) * 128],
                        xn_bf[:, cb * T + qs * 512: cb * T + (qs + 1) * 512],
                        start=(cb == 0), stop=(cb == NCB - 1))
                s_t.inc(i)
            em("tensor", _tens)

            def _vec2(e, fb=fb, qs=qs, g=g):
                e.wait_ge(s_t.h, s_t.n)
                dst = q_bf if fb < 8 else k_bf
                c0 = (fb % 8) * T + qs * 512
                i = e.tensor_copy(dst[:, c0:c0 + 512],
                                  psB[:, (g % 4) * 512:(g % 4) * 512 + 512])
                s_f.inc(i)
                qk_free.append(s_f.n)
            em("vector", _vec2)
        fb_last_evac[fb] = s_f.n

    ONES_V = ONES_V_BOX[0]
    P1_V = s_v.n
    P1_F = s_f.n

    esA.close()

    esC2 = ExitStack()
    probs = sb(esC2, "probs", [128, 8 * 512], BF16, side="right")
    x_att_tm = nc.alloc_sbuf_tensor_at("x_att_tm", [128, 1024], F32,
                                       offset=XST_LO)
    rcp = nc.alloc_sbuf_tensor_at("rcp", [128, 16], F32, offset=XST_LO + 4096)
    esD = ExitStack()
    attn_bf = sb(esD, "attn_bf", [128, NCB * T], BF16)
    wpj_bf = sb(esD, "wpj_bf", [128, NCB * D], BF16)

    pv_pair_mark = {}
    sc_marks = {}
    pv_done = {}
    norm_mark = {}
    trans_mark = {}
    evac_mark = {}

    def pvoff(rot, t):
        return (2 * rot + t // 4) * 512 + (t % 4) * 65

    def emit_norm_vec(j):
        rot = j % 2

        def _vec(e, j=j, rot=rot):
            e.wait_ge(s_t.h, pv_done[j])
            for b in range(2):
                bank = (2 * rot + b) * 512
                e.reciprocal(
                    rcp[:, rot * 8 + b * 4: rot * 8 + b * 4 + 4].rearrange(
                        "p (t c) -> p t c", c=1),
                    psB[:, bank: bank + 260].rearrange(
                        "p (t c) -> p t c", c=65)[:, :, 64:65])
            e.drain()
            for t in range(8):
                h2, qb = t // 4, t % 4
                i = e.tensor_scalar(
                    x_att_tm[:, rot * 512 + qb * 128 + h2 * 64:
                             rot * 512 + qb * 128 + h2 * 64 + 64],
                    psB[:, pvoff(rot, t): pvoff(rot, t) + 64],
                    rcp[:, rot * 8 + t: rot * 8 + t + 1], None, ALU.mult)
            s_v.inc(i)
            norm_mark[j] = s_v.n
        em("vector", _vec)

    def emit_trans(j):
        rot = j % 2

        def _tens(e, j=j, rot=rot):
            e.wait_ge(s_v.h, norm_mark[j])
            for qb in range(4):
                dst = (2 * rot + qb // 2) * 512 + (qb % 2) * 128
                i = nc.tensor.transpose(
                    psB[:, dst: dst + 128],
                    x_att_tm[:, rot * 512 + qb * 128: rot * 512 + qb * 128 + 128],
                    ident[:])
            s_t.inc(i)
            trans_mark[j] = s_t.n
        em("tensor", _tens)

    def emit_evac(j):
        rot = j % 2
        pj, qsj = j // NQS, j % NQS

        def _vec(e, j=j, rot=rot, pj=pj, qsj=qsj):
            e.wait_ge(s_t.h, trans_mark[j])
            c0 = pj * T + qsj * 512
            i = e.tensor_copy(
                attn_bf[:, c0:c0 + 512].rearrange("p (b c) -> p b c", c=256),
                psB[:, 2 * rot * 512: 2 * rot * 512 + 1024].rearrange(
                    "p (b c) -> p b c", c=512)[:, :, 0:256])
            s_v.inc(i)
            evac_mark[j] = s_v.n
        em("vector", _vec)

    wpj_cast = []
    for cb in range(NCB):
        buf = cb % 2
        sidx = len(wst_chain)

        def _sync(e, cb=cb, buf=buf, sidx=sidx):
            wst_wait(e, sidx)
            s_w.dinc(e.dma_start(out=wst[:, buf * D:(buf + 1) * D], in_=wpj_h[cb]))
        em("sync", _sync)

        def _vec(e, cb=cb, buf=buf):
            e.wait_ge(s_w.h, s_w.n)
            i = e.tensor_copy(wpj_bf[:, cb * D:(cb + 1) * D],
                              wst[:, buf * D:(buf + 1) * D])
            s_f.inc(i)
            wpj_cast.append(s_f.n)
        em("vector", _vec)
        wst_chain.append((s_f, s_f.n))
    WPJ_F = s_f.n

    wcast_mark = {}
    w1_dma_mark = {}
    W1_PER_PAIR = NHB // (H // 2) if W1_OVERLAP else 0

    w1_chain_idx = {}

    def emit_w1dma_p2(j):
        sidx = len(wst_chain)
        w1_chain_idx[j] = sidx

        def _sync(e, j=j, sidx=sidx):
            wst_wait(e, sidx)
            s_w.dinc(e.dma_start(out=wst[:, (j % 2) * D:(j % 2 + 1) * D],
                                 in_=w1_h[j]))
            w1_dma_mark[j] = s_w.n
        em("sync", _sync)
        wst_chain.append(None)

    def emit_w1cast_p2(j, pair_gate):
        def _vec(e, j=j, pair_gate=pair_gate):
            e.wait_ge(s_w.h, w1_dma_mark[j])
            e.wait_ge(s_t.h, pair_gate)
            i = e.tensor_copy(w1_e5[:, j * D:(j + 1) * D],
                              wst[:, (j % 2) * D:(j % 2 + 1) * D])
            s_v.inc(i)
            wcast_mark[('w1', j)] = s_v.n
            wst_chain[w1_chain_idx[j]] = (s_v, s_v.n)
        em("vector", _vec)

    P2_S0 = s_s.n

    def exp_mark(it, kc):
        return P2_S0 + it * NKC + kc + 1
    if W1_OVERLAP:
        emit_w1dma_p2(0)
        emit_w1dma_p2(1)
    w1_pending = []

    def emit_w1block(base, gate):
        emit_w1cast_p2(base, gate)
        emit_w1cast_p2(base + 1, gate)
        for j in (base + 2, base + 3):
            if j < NHB:
                emit_w1dma_p2(j)
        if base + 2 < NHB:
            emit_w1cast_p2(base + 2, gate)
        if base + 3 < NHB:
            emit_w1cast_p2(base + 3, gate)
        for j in (base + 4, base + 5):
            if j < NHB:
                emit_w1dma_p2(j)
    NG = (H // 2) * NQS * NKC

    def emit_scores_g(g):
        it_, kc = divmod(g, NKC)
        p_, qs_ = divmod(it_, NQS)
        rot = g % 2
        scA = psA[:, rot * 1024: rot * 1024 + 512]
        scB = psA[:, rot * 1024 + 512: rot * 1024 + 1024]

        def _tens(e, g=g, it_=it_, kc=kc, p_=p_, qs_=qs_, scA=scA, scB=scB):
            if g == 0:
                e.wait_ge(s_v.h, ONES_V)
                e.wait_ge(s_f.h, P1_F)
            if g >= 2:
                e.wait_ge(s_s.h, P2_S0 + g - 1)
            nc.tensor.matmul(
                scA, k_bf[0:64, p_ * T + kc * 128: p_ * T + (kc + 1) * 128],
                q_bf[0:64, p_ * T + qs_ * 512: p_ * T + (qs_ + 1) * 512],
                start=True, stop=True, tile_position=(0, 0))
            i = nc.tensor.matmul(
                scB, k_bf[64:128, p_ * T + kc * 128: p_ * T + (kc + 1) * 128],
                q_bf[64:128, p_ * T + qs_ * 512: p_ * T + (qs_ + 1) * 512],
                start=True, stop=True, tile_position=(64, 0))
            s_t.inc(i)
        em("tensor", _tens)
        it_, kc = divmod(g, NKC)
        sc_marks[(it_, kc)] = s_t.n

    emit_scores_g(0)
    emit_scores_g(1)
    for p in range(H // 2):
        for qs in range(NQS):
            it = p * NQS + qs

            def emit_pv(kc, p=p, it=it):
                rot3 = (it * NKC + kc) % 4
                rot = it % 2

                def _tens2(e, kc=kc, rot3=rot3, rot=rot):
                    e.wait_ge(s_s.h, exp_mark(it, kc))
                    if kc == 0 and it >= 2:
                        e.wait_ge(s_v.h, evac_mark[it - 2])
                    for t in range(8):
                        h2, qb = t // 4, t % 4
                        i = nc.tensor.matmul(
                            psB[:, pvoff(rot, t): pvoff(rot, t) + 65],
                            probs[:, rot3 * 1024 + h2 * 512 + qb * 128:
                                  rot3 * 1024 + h2 * 512 + qb * 128 + 128],
                            vt_bf[:, kc * (65 * H) + (2 * p + h2) * 65:
                                  kc * (65 * H) + (2 * p + h2) * 65 + 65],
                            start=(kc == 0 and t % 4 == 0),
                            stop=(kc == NKC - 1 and t % 4 == 3))
                    s_t.inc(i)
                    pv_pair_mark[(it, kc)] = s_t.n
                em("tensor", _tens2)

            if it >= 1:
                emit_norm_vec(it - 1)
            for kc in range(NKC):
                if kc == 2 and it >= 1:
                    emit_trans(it - 1)
                    emit_evac(it - 1)
                if kc == 3 and w1_pending:
                    emit_w1block(*w1_pending.pop(0))
                g = it * NKC + kc + 2
                if g < NG:
                    emit_scores_g(g)
                emit_pv(kc)
            pv_done[it] = s_t.n

            if W1_OVERLAP and qs == NQS - 1:
                w1_pending.append((W1_PER_PAIR * p, sc_marks[(it, NKC - 1)]))
    li = (H // 2) * NQS - 1
    emit_norm_vec(li)
    emit_trans(li)
    emit_evac(li)
    P2_V_CORE = s_v.n
    while w1_pending:
        emit_w1block(*w1_pending.pop(0))

    def _p2_scal(e):
        for it in range((H // 2) * NQS):
            for kc in range(NKC):
                rot2 = kc % 2
                rot3 = (it * NKC + kc) % 4
                e.wait_ge(s_t.h, sc_marks[(it, kc)])
                pit, pkc = (it, kc - 4) if kc >= 4 else (it - 1, NKC - 4 + kc)
                if (pit, pkc) in pv_pair_mark:
                    e.wait_ge(s_t.h, pv_pair_mark[(pit, pkc)])
                i = e.activation(probs[:, rot3 * 1024:(rot3 + 1) * 1024],
                                 psA[:, rot2 * 1024:(rot2 + 1) * 1024],
                                 AF.Exp, bias=-C_SHIFT, scale=SCALE)
                s_s.inc(i)
                assert s_s.n == exp_mark(it, kc)
    em("scalar", _p2_scal)

    P2_V = s_v.n
    P2_T = s_t.n

    esC2.close()
    esC.close()
    esG = ExitStack()
    h_e5 = sb(esG, "h_e5", [128, NHB * 512], F8E5)
    h_bfst = sb(esG, "h_bfst", [128, 4 * 512], BF16)
    y2_bf = sb(esG, "y2_bf", [128, D], BF16)
    y2_f32 = sb(esG, "y2_f32", [128, D], F32)
    mask_st = sb(esG, "mask_st", [128, 2 * D], BF16)
    out_st = sb(esG, "out_st", [128, 2 * D], F32)
    esF = ExitStack()
    if W1_OVERLAP:
        wst2 = sb(esF, "wst2", [128, 2 * D], F32)
        wst2b = wst
    else:
        wst2 = sb(esF, "wst2", [128, 4 * D], F32)
        wst2b = None

    def wst2_buf(i_):
        r = i_ % 4
        if W1_OVERLAP:
            t_ = wst2 if r < 2 else wst2b
            return t_[:, (r % 2) * D:(r % 2 + 1) * D]
        return wst2[:, r * D:(r + 1) * D]
    if W1_OVERLAP:
        assert nc.sbuf_base <= addr["k_bf"][0], (nc.sbuf_base, addr)
    else:
        w1_e5 = sb(esF, "w1_e5", [128, NHB * D], F8E5)
        w2_e5 = sb(esF, "w2_e5", [128, NHB * D], F8E5)

    NSTR = T // 512
    h_cast = {}
    P45_S0 = s_s.n

    def cmark(idx):
        return P45_S0 + idx + 1
    out_done = []
    f2_t_mark = []
    f1_marks = {}
    P2_S_END = s_s.n

    s_gs = [Sem(sems, nc, "s_g%d" % i) for i in range(4)]
    s_g = s_gs[0]
    w_dma_mark = {}

    def emit_w1dma(i_):
        def _sync(e, i_=i_):
            if i_ < 2:
                sem, val = wst_chain[len(wst_chain) - 2 + i_]
                e.wait_ge(sem.h, val)
            else:
                e.wait_ge(s_v.h, wcast_mark[('w1', i_ - 2)])
            s_w.dinc(e.dma_start(out=wst[:, (i_ % 2) * D:(i_ % 2 + 1) * D],
                                 in_=w1_h[i_]))
            w_dma_mark[('w1', i_)] = (s_w, s_w.n)
        em("sync", _sync)

    def emit_w2dma(i_):
        def _gp(e, i_=i_):
            if i_ < 4:
                e.wait_ge(s_t.h, P2_T)
                if W1_OVERLAP and i_ >= 2:
                    e.wait_ge(s_v.h, wcast_mark[('w1', NHB - 2 + (i_ - 2))])
            else:
                e.wait_ge(s_v.h, wcast_mark[('w2', i_ - 4)])
            ch = s_gs[i_ % 4]
            ch.dinc(e.dma_start(out=wst2_buf(i_), in_=w2_h[i_]))
            w_dma_mark[('w2', i_)] = (ch, ch.n)
        em("gpsimd", _gp)

    def emit_wcast(e, kind, i_):
        sem, val = w_dma_mark[(kind, i_)]
        e.wait_ge(sem.h, val)
        if kind == 'w1':
            dst = w1_e5[:, i_ * D:(i_ + 1) * D]
            srcb = wst[:, (i_ % 2) * D:(i_ % 2 + 1) * D]
        else:
            dst = w2_e5[:, (i_) * D:(i_ + 1) * D]
            srcb = wst2_buf(i_)
        s_v.inc(e.tensor_copy(dst, srcb))
        wcast_mark[(kind, i_)] = s_v.n

    for i_ in (0, 1):
        if not W1_OVERLAP:
            emit_w1dma(i_)
    for i_ in range(min(4, NHB)):
        emit_w2dma(i_)

    def _wvec0(e):
        e.wait_ge(s_t.h, P2_T)
        if not W1_OVERLAP:
            emit_wcast(e, 'w1', 0)
            emit_wcast(e, 'w1', 1)
        emit_wcast(e, 'w2', 0)
        emit_wcast(e, 'w2', 1)
    em("vector", _wvec0)

    def _p45_scal(e):
        for ts_ in range(NSTR):
            for hb_ in range(NHB):
                rot_ = hb_ % 4
                idx_ = ts_ * NHB + hb_
                e.wait_ge(s_v.h, h_cast[idx_])
                i = e.activation(h_e5[:, hb_ * 512:(hb_ + 1) * 512],
                                 h_bfst[:, rot_ * 512:(rot_ + 1) * 512], AF.Copy)
                s_s.inc(i)
                assert s_s.n == cmark(idx_)

    last_relu_of_stripe = {}
    for ts in range(NSTR):
        for hb in range(NHB):
            rot = hb % 4
            idx = ts * NHB + hb

            if ts == 0:
                if not W1_OVERLAP and hb + 2 < NHB:
                    emit_w1dma(hb + 2)
                if hb + 4 < NHB:
                    emit_w2dma(hb + 4)

            def _tens(e, ts=ts, hb=hb, rot=rot, idx=idx):
                if idx == 0:
                    e.wait_ge(s_s.h, P2_S_END)
                    e.wait_ge(s_v.h, P2_V_CORE)
                if ts == 0:
                    e.wait_ge(s_v.h, wcast_mark[('w1', hb)])
                if hb == 0 and ts >= 1:
                    e.wait_ge(s_v.h, out_done[-2])
                if hb == 2 and ts >= 1:
                    e.wait_ge(s_v.h, out_done[-1])
                if idx >= 4:
                    e.wait_ge(s_v.h, h_cast[idx - 4])
                for cp in range(NCB // 2):
                    i = nc.tensor.matmul(
                        psA[:, rot * 512:(rot + 1) * 512],
                        w1_e5[:, hb * D + 2 * cp * 128: hb * D + (2 * cp + 2) * 128]
                            .rearrange("p (k m) -> p k m", k=2),
                        xn_e5_v[:, 2 * cp:2 * cp + 2, ts * 512:(ts + 1) * 512],
                        start=(cp == 0), stop=(cp == NCB // 2 - 1),
                        perf_mode=mybir.MatmulPerfMode.DoubleRow)
                    if cp == NCB // 2 - 1:
                        s_t.inc(i)
            em("tensor", _tens)
            f1_marks[idx] = s_t.n

            def _vec(e, ts=ts, hb=hb, rot=rot, idx=idx):
                e.wait_ge(s_t.h, f1_marks[idx])
                if ts >= 1 and hb == 0:
                    e.wait_ge(s_t.h, f2_t_mark[ts - 1])
                if idx >= 4:
                    e.wait_ge(s_s.h, cmark(idx - 4))
                i = e.tensor_scalar(h_bfst[:, rot * 512:(rot + 1) * 512],
                                    psA[:, rot * 512:(rot + 1) * 512],
                                    b1q[:, hb:hb + 1], 0.0, ALU.add, ALU.max)
                s_v.inc(i)
                h_cast[idx] = s_v.n
                if ts == 0 and hb + 2 < NHB:
                    if not W1_OVERLAP:
                        emit_wcast(e, 'w1', hb + 2)
                    emit_wcast(e, 'w2', hb + 2)
            em("vector", _vec)
        H_STRIPE_V = s_v.n
        last_relu_of_stripe[ts] = h_cast[ts * NHB + NHB - 1]

        for tj in range(4):
            tb = ts * 4 + tj
            buf = tb % 2
            pb = tb % 2

            def _sync(e, tb=tb, buf=buf):
                if tb < 2:
                    e.wait_ge(s_v.h, P2_V)
                else:
                    e.wait_ge(s_v.h, out_done[tb - 2])
                s_x.dinc(e.dma_start(out=x_st[:, buf * D:(buf + 1) * D],
                                    in_=x_h[tb * 128:(tb + 1) * 128, :]))
                s_x.dinc(e.dma_start(out=mask_st[:, buf * D:(buf + 1) * D],
                                    in_=mask_h[tb * 128:(tb + 1) * 128, :]))
            em("sync", _sync)

            def _tens(e, ts=ts, tj=tj, tb=tb, pb=pb):
                if tb < 2:
                    e.wait_ge(s_f.h, WPJ_F)
                if tj <= 1:
                    e.wait_ge(s_v.h, last_relu_of_stripe[ts])
                if tb >= 2:
                    e.wait_ge(s_v.h, out_done[tb - 2])
                for jb in range(2):
                    for cb in range(NCB):
                        i = nc.tensor.matmul(
                            psA[:, pb * 1024 + jb * 512: pb * 1024 + (jb + 1) * 512],
                            attn_bf[:, cb * T + tb * 128: cb * T + (tb + 1) * 128],
                            wpj_bf[:, cb * D + jb * 512: cb * D + (jb + 1) * 512],
                            start=(cb == 0), stop=(cb == NCB - 1))
                e.wait_ge(s_s.h, cmark(ts * NHB + NHB - 1))
                h_v = h_e5[:].rearrange("p (h t) -> p h t", t=512)
                w2_v = w2_e5[:].rearrange("p (h j) -> p h j", j=D)
                for ns in range(2):
                    for hp in range(NHB // 2):
                        i = nc.tensor.matmul(
                            psB[:, pb * 1024 + ns * 512: pb * 1024 + (ns + 1) * 512],
                            h_v[:, 2 * hp:2 * hp + 2, tj * 128:(tj + 1) * 128],
                            w2_v[:, 2 * hp:2 * hp + 2, ns * 512:(ns + 1) * 512],
                            start=(hp == 0), stop=(hp == NHB // 2 - 1),
                            perf_mode=mybir.MatmulPerfMode.DoubleRow)
                        if hp == NHB // 2 - 1 and ns == 1:
                            s_t.inc(i)
            em("tensor", _tens)
            f2_mark = s_t.n
            if tj == 3:
                f2_t_mark.append(f2_mark)

            def _vec(e, tb=tb, pb=pb, buf=buf, f2_mark=f2_mark):
                e.wait_ge(s_t.h, f2_mark)
                e.wait_ge(s_x.h, s_x.n)
                if tb >= 2:
                    e.wait_ge(s_o.h, 16 * (tb - 1))
                dst = out_st[:, buf * D:(buf + 1) * D]
                e.tensor_tensor(dst, psA[:, pb * 1024:(pb + 1) * 1024], bp_sb[:], ALU.add)
                e.drain()
                e.tensor_tensor(dst, dst, x_st[:, buf * D:(buf + 1) * D], ALU.add)
                e.tensor_tensor(y2_bf[:], psB[:, pb * 1024:(pb + 1) * 1024], b2q[:], ALU.add)
                e.drain()
                e.tensor_tensor(y2_f32[:], y2_bf[:], mask_st[:, buf * D:(buf + 1) * D],
                                ALU.mult)
                e.drain()
                i = e.tensor_tensor(dst, dst, y2_f32[:], ALU.add)
                s_v.inc(i)
                out_done.append(s_v.n)
            em("vector", _vec)
            st_mark = s_v.n

            def _gp(e, tb=tb, buf=buf, st_mark=st_mark):
                e.wait_ge(s_v.h, st_mark)
                s_o.dinc(e.dma_start(out=out_h[tb * 128:(tb + 1) * 128, :],
                                    in_=out_st[:, buf * D:(buf + 1) * D]))
            em("gpsimd", _gp)


    em("scalar", _p45_scal)

    def _gp_final(e):
        e.wait_ge(s_o.h, s_o.n)
    em("gpsimd", _gp_final)

    for st in (esF, esG, esD, esB):
        st.close()
    es.close()
    sems.close()
    return nc


def _host_mask(T, Dm):
    import jax
    if os.environ.get("BASS_SIM_FP64") == "1":
        jax.config.update("jax_enable_x64", True)
    cpu = jax.devices("cpu")[0]
    with jax.default_device(cpu):
        keep = jax.random.bernoulli(
            jax.random.key(42, impl="threefry2x32"), 0.8, (8, T, Dm))
    keep = np.asarray(keep)
    return np.where(keep, np.float32(1.25), np.float32(0.0)).astype(ml_dtypes.bfloat16)


def _prep_inputs(L0_attn_x, L0_attn_w_qkv, L0_attn_w_proj, L0_attn_b_proj,
                 L0_ffn_W1, L0_ffn_b1, L0_ffn_W2, L0_ffn_b2):
    B, T, _ = L0_attn_x.shape
    w_qkv = np.asarray(L0_attn_w_qkv, dtype=np.float32)
    w_proj = np.asarray(L0_attn_w_proj, dtype=np.float32)
    W1 = np.asarray(L0_ffn_W1, dtype=np.float32)
    W2 = np.asarray(L0_ffn_W2, dtype=np.float32)

    wqk = np.ascontiguousarray(
        w_qkv[:2 * D].reshape(H, 128, NCB, 128).transpose(0, 3, 2, 1)).reshape(H, 128, D)
    wv = np.ascontiguousarray(w_qkv[2 * D:].T).reshape(NCB, 128, D)
    wpj = np.ascontiguousarray(w_proj.T).reshape(NCB, 128, D)
    w1 = np.ascontiguousarray(
        np.ascontiguousarray(W1.T).reshape(NCB, 128, NHB, 128).transpose(2, 1, 0, 3)
    ).reshape(NHB, 128, D)
    w2 = np.ascontiguousarray(W2.T).reshape(NHB, 128, D)
    b1t = np.ascontiguousarray(np.asarray(L0_ffn_b1, dtype=np.float32).reshape(NHB, 128).T)
    b2t = np.ascontiguousarray(
        np.broadcast_to(np.asarray(L0_ffn_b2, dtype=np.float32), (128, D)))
    bpt = np.ascontiguousarray(
        np.broadcast_to(np.asarray(L0_attn_b_proj, dtype=np.float32), (128, D)))
    mask = _host_mask(T, D)

    in_maps = []
    for b in range(B):
        in_maps.append({
            "x": np.ascontiguousarray(np.asarray(L0_attn_x[b], dtype=np.float32)),
            "mask": np.ascontiguousarray(mask[b]),
            "wqk": wqk, "wv": wv, "wpj": wpj, "w1": w1, "w2": w2,
            "b1t": b1t, "b2t": b2t, "bpt": bpt,
        })
    return in_maps, T


def kernel(L0_attn_x, L0_attn_w_qkv, L0_attn_w_proj, L0_attn_b_proj,
           L0_ffn_W1, L0_ffn_b1, L0_ffn_W2, L0_ffn_b2):
    in_maps, T = _prep_inputs(L0_attn_x, L0_attn_w_qkv, L0_attn_w_proj,
                              L0_attn_b_proj, L0_ffn_W1, L0_ffn_b1,
                              L0_ffn_W2, L0_ffn_b2)
    nc = build_nc(T=T)
    res = run_bass_kernel_spmd(nc, in_maps, core_ids=list(range(len(in_maps))))
    return np.stack([np.asarray(r["out"], dtype=np.float32) for r in res.results], axis=0)


# version 38
# speedup vs baseline: 1.0165x; 1.0014x over previous
import os
from contextlib import ExitStack

import numpy as np
import ml_dtypes

import concourse.bass as bass
import concourse.mybir as mybir
from concourse.bass_utils import run_bass_kernel_spmd
from concourse.alu_op_type import AluOpType as ALU

F32 = mybir.dt.float32
BF16 = mybir.dt.bfloat16
F8E5 = mybir.dt.float8e5
AF = mybir.ActivationFunctionType
AX = mybir.AxisListType.X

D = 1024
H = 16
NCB = D // 128
HID = 4096
NHB = HID // 128
C_SHIFT = 72.0
SCALE = 0.125


class Sem:
    def __init__(self, stack, nc, name):
        self.h = stack.enter_context(nc.semaphore(name))
        self.n = 0

    def inc(self, instr, k=1):
        instr.then_inc(self.h, k)
        self.n += k
        return instr

    def dinc(self, instr):
        if self.n > 0:
            instr.wait_op(self.h, self.n, "sem-ge")
        instr.then_inc(self.h, 16)
        self.n += 16
        return instr


def build_nc(T=2048):
    assert T % 512 == 0
    NTB = T // 128
    NQS = T // 512
    NKC = NTB
    nc = bass.Bass("TRN2", target_bir_lowering=False)

    x_h = nc.dram_tensor("x", [T, D], F32, kind="ExternalInput").ap()
    mask_h = nc.dram_tensor("mask", [T, D], BF16, kind="ExternalInput").ap()
    wqk_h = nc.dram_tensor("wqk", [H, 128, D], F32, kind="ExternalInput").ap()
    wv_h = nc.dram_tensor("wv", [NCB, 128, D], F32, kind="ExternalInput").ap()
    wpj_h = nc.dram_tensor("wpj", [NCB, 128, D], F32, kind="ExternalInput").ap()
    w1_h = nc.dram_tensor("w1", [NHB, 128, D], F32, kind="ExternalInput").ap()
    w2_h = nc.dram_tensor("w2", [NHB, 128, D], F32, kind="ExternalInput").ap()
    b1_h = nc.dram_tensor("b1t", [128, NHB], F32, kind="ExternalInput").ap()
    b2_h = nc.dram_tensor("b2t", [128, D], F32, kind="ExternalInput").ap()
    bp_h = nc.dram_tensor("bpt", [128, D], F32, kind="ExternalInput").ap()
    out_h = nc.dram_tensor("out", [T, D], F32, kind="ExternalOutput").ap()
    ident_h = nc.inline_tensor(np.eye(128, dtype=np.float32), name="ident_h").ap()
    ones_h = nc.inline_tensor(np.ones((128, 128), dtype=np.float32), name="ones_h").ap()

    es = ExitStack()
    sems = ExitStack()

    def sb(stack, name, shape, dt, side="left"):
        return stack.enter_context(nc.sbuf_tensor(name, shape, dt, side=side))

    consts = sb(es, "consts", [128, 8], F32)
    CONST_COLS = {}
    for v in (1.0 / D, -1.0 / D, 1e-5, SCALE, -C_SHIFT, 1.0):
        CONST_COLS[v] = len(CONST_COLS)
        nc.const_aps.register(F32, v, consts[:, CONST_COLS[v]:CONST_COLS[v] + 1])

    ident = sb(es, "ident", [128, 128], F32)
    ones_sb = sb(es, "ones_sb", [128, 128], F32)
    b1q = sb(es, "b1q", [128, NHB], F32)
    b1q_e5 = sb(es, "b1q_e5", [128, NHB], F8E5)
    b2q = sb(es, "b2q", [128, D], F32)
    b2q_e5 = sb(es, "b2q_e5", [128, D], F8E5)
    bp_sb = sb(es, "bp_sb", [128, D], F32)
    XST_LO = -(-nc.sbuf_base // 32) * 32
    x_st = sb(es, "x_st", [128, 3 * D], F32)
    wst = sb(es, "wst", [128, 2 * D], F32)
    stat = sb(es, "stat", [128, 18], F32)

    psA = es.enter_context(nc.psum_tensor("psA", [128, 2048], F32))
    psB = es.enter_context(nc.psum_tensor("psB", [128, 2048], F32))

    s_x = Sem(sems, nc, "s_x")
    s_w = Sem(sems, nc, "s_w")
    s_v = Sem(sems, nc, "s_v")
    s_s = Sem(sems, nc, "s_s")
    s_t = Sem(sems, nc, "s_t")
    s_o = Sem(sems, nc, "s_o")
    s_f = Sem(sems, nc, "s_f")

    block = es.enter_context(nc.Block())

    def em(engname, f):
        getattr(block, engname)(f)

    wst_chain = []

    def wst_wait(e, idx):
        if idx >= 2:
            sem, val = wst_chain[idx - 2]
            e.wait_ge(sem.h, val)

    def _setup_sync(e):
        s_w.dinc(e.dma_start(out=ident[:], in_=ident_h[:]))
        s_w.dinc(e.dma_start(out=ones_sb[:], in_=ones_h[:]))
        s_w.dinc(e.dma_start(out=b1q[:, 0:NHB], in_=b1_h[:]))
    em("sync", _setup_sync)
    SETUP_W = s_w.n

    def _setup_vec(e):
        for v, j in CONST_COLS.items():
            e.memset(consts[:, j:j + 1], v)
        s_v.inc(e.nop())
        e.wait_ge(s_w.h, SETUP_W)
    em("vector", _setup_vec)
    SETUP_V = s_v.n

    esB = ExitStack()
    xn_e5 = sb(esB, "xn_e5", [128, NCB * T], F8E5)
    esC = ExitStack()
    addr = {}

    def sb_rec(stack, name, shape, dt, side):
        hi = nc.sbuf_top
        t_ = sb(stack, name, shape, dt, side=side)
        addr[name] = (nc.sbuf_top, hi)
        return t_
    vt_bf = sb_rec(esC, "vt_bf", [128, NTB * (H * 65)], BF16, side="right")
    k_bf = sb_rec(esC, "k_bf", [128, NCB * T], BF16, side="right")
    q_bf = sb_rec(esC, "q_bf", [128, NCB * T], BF16, side="right")
    vt_v = vt_bf[:].rearrange("p (tb h d) -> p tb h d", h=H, d=65)
    W1_BYTES = NHB * D
    W1_OVERLAP = (addr["k_bf"][1] - addr["k_bf"][0] == W1_BYTES and
                  addr["vt_bf"][1] - addr["vt_bf"][0] >= W1_BYTES and
                  NQS >= 1 and (H // 2) > 0 and NHB % (H // 2) == 0)
    if W1_OVERLAP:
        w1_e5 = nc.alloc_sbuf_tensor_at(
            "w1_e5", [128, NHB * D], F8E5, offset=addr["k_bf"][0])
        w2_e5 = nc.alloc_sbuf_tensor_at(
            "w2_e5", [128, NHB * D], F8E5, offset=addr["vt_bf"][0])
    esA = ExitStack()
    xn_f32 = sb(esA, "xn_f32", [128, 3 * D], F32, side="right")
    xn_bf = sb(esA, "xn_bf", [128, NCB * T], BF16, side="right")
    wv_bf = sb(esA, "wv_bf", [128, NCB * D], BF16, side="right")
    wqk_b = sb(esA, "wqk_b", [128, 2 * D], BF16, side="right")
    xn_bf_v = xn_bf[:].rearrange("p (c t) -> p c t", t=T)
    xn_e5_v = xn_e5[:].rearrange("p (c t) -> p c t", t=T)

    evac_done = []
    p0_vec_mark = []
    xn_mark = []
    tr_marks = []
    P0_S0 = s_s.n

    def p0_smark(tb):
        return P0_S0 + tb + 1

    wv_cast = []
    v_free = []
    ONES_V_BOX = []
    KWV = max(1, (NCB + NTB - 1) // NTB)
    wv_sched = {}
    for cb_ in range(NCB):
        wv_sched.setdefault(cb_ // KWV, []).append(cb_)
    last_wv_tb = max(wv_sched)
    v_delay = min(last_wv_tb + 2, NTB - 1) if NTB > 1 else 1
    wv_dma_mark = {}
    wv_chain_idx = {}

    def emit_v(j):
        pb = j % 2

        def _tensV(e, j=j, pb=pb):
            if j < 2:
                e.wait_ge(s_f.h, wv_cast[-1])
            else:
                e.wait_ge(s_f.h, v_free[j - 2])
            e.wait_ge(s_f.h, evac_done[j])
            for cb in range(NCB):
                for vs in range(2):
                    i = nc.tensor.matmul(
                        psA[:, pb * 1024 + vs * 512: pb * 1024 + (vs + 1) * 512],
                        xn_bf[:, cb * T + j * 128: cb * T + (j + 1) * 128],
                        wv_bf[:, cb * D + vs * 512: cb * D + (vs + 1) * 512],
                        start=(cb == 0), stop=(cb == NCB - 1))
                    if cb == NCB - 1 and vs == 1:
                        s_t.inc(i)
        em("tensor", _tensV)

        def _vecV(e, j=j, pb=pb):
            e.wait_ge(s_t.h, s_t.n)
            i = e.tensor_copy(
                vt_v[:, j, :, 0:64],
                psA[:, pb * 1024: pb * 1024 + 1024].rearrange("p (h d) -> p h d", d=64))
            s_f.inc(i)
            v_free.append(s_f.n)
        em("vector", _vecV)

    for tb in range(NTB):
        buf = tb % 3
        nbuf = tb % 3
        xs = x_st[:, buf * D:(buf + 1) * D]
        xn = xn_f32[:, nbuf * D:(nbuf + 1) * D]
        sc = (tb % 3) * 6

        def _sync(e, tb=tb, xs=xs):
            if tb >= 3:
                e.wait_ge(s_v.h, xn_mark[tb - 3])
                e.wait_ge(s_s.h, p0_smark(tb - 3))
            s_x.dinc(e.dma_start(out=xs[:], in_=x_h[tb * 128:(tb + 1) * 128, :]))
            for cb in wv_sched.get(tb, ()):
                sidx = len(wst_chain)
                wst_wait(e, sidx)
                s_w.dinc(e.dma_start(out=wst[:, (cb % 2) * D:(cb % 2 + 1) * D],
                                     in_=wv_h[cb]))
                wv_dma_mark[cb] = s_w.n
                wv_chain_idx[cb] = sidx
                wst_chain.append((s_f, None))
        em("sync", _sync)

        if wv_sched.get(tb):
            def _vec_wv(e, cbs=wv_sched[tb]):
                for cb in cbs:
                    e.wait_ge(s_w.h, wv_dma_mark[cb])
                    i = e.tensor_copy(wv_bf[:, cb * D:(cb + 1) * D],
                                      wst[:, (cb % 2) * D:(cb % 2 + 1) * D])
                    s_f.inc(i)
                    wv_cast.append(s_f.n)
                    wst_chain[wv_chain_idx[cb]] = (s_f, s_f.n)
            em("vector", _vec_wv)
        if tb == last_wv_tb + 1 or (tb == last_wv_tb and tb == NTB - 1):
            def _vec_ones(e):
                s_v.inc(e.memset(vt_v[:, :, :, 64:65], 1.0))
            em("vector", _vec_ones)
            ONES_V_BOX.append(s_v.n)

        def emit_ln_stats(tb_):
            sbuf = tb_ % 3
            sxs = x_st[:, sbuf * D:(sbuf + 1) * D]
            ssc = (tb_ % 3) * 6

            def _vec(e, tb_=tb_, sxs=sxs, ssc=ssc):
                if tb_ == 0:
                    e.wait_ge(s_v.h, SETUP_V)
                if tb_ >= 3:
                    e.wait_ge(s_s.h, p0_smark(tb_ - 3))
                e.wait_ge(s_x.h, 16 * (tb_ + 1))
                e.reduce_sum(stat[:, ssc:ssc + 1], sxs[:], axis=AX)
                e.drain()
                i = e.tensor_scalar(stat[:, ssc + 1:ssc + 2], stat[:, ssc:ssc + 1],
                                    -1.0 / D, None, ALU.mult)
                s_v.inc(i)
                p0_vec_mark.append(s_v.n)
            em("vector", _vec)

        if tb == 0:
            emit_ln_stats(0)

        def _vec2(e, tb=tb, xs=xs, xn=xn, sc=sc):
            e.wait_ge(s_s.h, p0_smark(tb))
            e.reciprocal(stat[:, sc + 4:sc + 5], stat[:, sc + 3:sc + 4])
            e.drain()
            s_v.inc(e.tensor_scalar(xn[:], xs[:], stat[:, sc + 1:sc + 2],
                                    stat[:, sc + 4:sc + 5], ALU.add, ALU.mult))
            xn_mark.append(s_v.n)
        em("vector", _vec2)
        if tb + 1 < NTB:
            emit_ln_stats(tb + 1)

        def _tens(e, tb=tb, xn=xn, xm=xn_mark[tb]):
            e.wait_ge(s_v.h, xm)
            half = tb % 2
            if tb >= 2:
                e.wait_ge(s_f.h, evac_done[tb - 2])
            for cb in range(NCB):
                i = nc.tensor.transpose(
                    psB[:, half * 1024 + cb * 128: half * 1024 + (cb + 1) * 128],
                    xn[:, cb * 128:(cb + 1) * 128], ident[:])
                if cb == NCB - 1:
                    s_t.inc(i)
        em("tensor", _tens)
        tr_marks.append(s_t.n)

        def _vec3(e, tb=tb):
            half = tb % 2
            e.wait_ge(s_t.h, s_t.n)
            src = psB[:, half * 1024: half * 1024 + 1024].rearrange(
                "p (c t) -> p c t", t=128)
            e.tensor_copy(xn_bf_v[:, 0:NCB, tb * 128:(tb + 1) * 128], src)
            e.drain()
            i = e.tensor_copy(xn_e5_v[:, 0:NCB, tb * 128:(tb + 1) * 128], src)
            s_f.inc(i)
            evac_done.append(s_f.n)
        em("vector", _vec3)

        if tb >= v_delay:
            emit_v(tb - v_delay)

    for j in range(max(0, NTB - v_delay), NTB):
        emit_v(j)

    def _p0_scal(e):
        for tb in range(NTB):
            buf = tb % 3
            nbuf = tb % 3
            xs = x_st[:, buf * D:(buf + 1) * D]
            xn = xn_f32[:, nbuf * D:(nbuf + 1) * D]
            sc = (tb % 3) * 6
            e.wait_ge(s_v.h, p0_vec_mark[tb])
            if tb >= 3:
                e.wait_ge(s_t.h, tr_marks[tb - 3])
            e.activation(xn[:], xs[:], AF.Square, bias=stat[:, sc + 1:sc + 2],
                         accum_out=stat[:, sc + 2:sc + 3])
            e.drain()
            s_s.inc(e.activation(stat[:, sc + 3:sc + 4], stat[:, sc + 2:sc + 3],
                                 AF.Sqrt, scale=1.0 / D, bias=1e-5))
            assert s_s.n == p0_smark(tb)
    em("scalar", _p0_scal)

    P0_V = s_v.n
    P0_F = s_f.n

    def _bias_sync(e):
        s_w.dinc(e.dma_start(out=b2q[:], in_=b2_h[:]))
        s_w.dinc(e.dma_start(out=bp_sb[:], in_=bp_h[:]))
    em("sync", _bias_sync)
    BIAS_W = s_w.n

    def _bias_vec(e):
        e.wait_ge(s_w.h, BIAS_W)
        e.tensor_copy(b1q_e5[:, 0:NHB], b1q[:, 0:NHB])
        e.tensor_copy(b2q_e5[:], b2q[:])
        e.drain()
        e.tensor_copy(b1q[:, 0:NHB], b1q_e5[:, 0:NHB])
        s_v.inc(e.tensor_copy(b2q[:], b2q_e5[:]))
    em("vector", _bias_vec)

    qk_free = []
    fb_last_evac = {}
    for fb in range(H):
        buf = fb % 2
        sidx = len(wst_chain)

        def _sync(e, fb=fb, buf=buf, sidx=sidx):
            wst_wait(e, sidx)
            if fb >= 2:
                e.wait_ge(s_f.h, fb_last_evac[fb - 2])
            s_w.dinc(e.dma_start(out=wst[:, buf * D:(buf + 1) * D], in_=wqk_h[fb]))
        em("sync", _sync)

        def _vec(e, buf=buf):
            e.wait_ge(s_w.h, s_w.n)
            s_v.inc(e.tensor_copy(wqk_b[:, buf * D:(buf + 1) * D],
                                  wst[:, buf * D:(buf + 1) * D]))
        em("vector", _vec)
        cast_mark = s_v.n
        wst_chain.append((s_v, s_v.n))

        for qs in range(NQS):
            g = NQS * fb + qs

            def _tens(e, fb=fb, buf=buf, qs=qs, g=g, cast_mark=cast_mark):
                if qs == 0:
                    e.wait_ge(s_v.h, cast_mark)
                if fb == 0 and qs == 0:
                    e.wait_ge(s_f.h, P0_F)
                if g >= 4:
                    e.wait_ge(s_f.h, qk_free[g - 4])
                for cb in range(NCB):
                    i = nc.tensor.matmul(
                        psB[:, (g % 4) * 512:(g % 4 + 1) * 512],
                        wqk_b[:, buf * D + cb * 128: buf * D + (cb + 1) * 128],
                        xn_bf[:, cb * T + qs * 512: cb * T + (qs + 1) * 512],
                        start=(cb == 0), stop=(cb == NCB - 1))
                s_t.inc(i)
            em("tensor", _tens)

            def _vec2(e, fb=fb, qs=qs, g=g):
                e.wait_ge(s_t.h, s_t.n)
                dst = q_bf if fb < 8 else k_bf
                c0 = (fb % 8) * T + qs * 512
                i = e.tensor_copy(dst[:, c0:c0 + 512],
                                  psB[:, (g % 4) * 512:(g % 4) * 512 + 512])
                s_f.inc(i)
                qk_free.append(s_f.n)
            em("vector", _vec2)
        fb_last_evac[fb] = s_f.n

    ONES_V = ONES_V_BOX[0]
    P1_V = s_v.n
    P1_F = s_f.n

    esA.close()

    esC2 = ExitStack()
    probs = sb(esC2, "probs", [128, 8 * 512], BF16, side="right")
    x_att_tm = nc.alloc_sbuf_tensor_at("x_att_tm", [128, 1024], F32,
                                       offset=XST_LO)
    rcp = nc.alloc_sbuf_tensor_at("rcp", [128, 16], F32, offset=XST_LO + 4096)
    esD = ExitStack()
    attn_bf = sb(esD, "attn_bf", [128, NCB * T], BF16)
    wpj_bf = sb(esD, "wpj_bf", [128, NCB * D], BF16)

    pv_pair_mark = {}
    sc_marks = {}
    pv_done = {}
    norm_mark = {}
    trans_mark = {}
    evac_mark = {}

    def pvoff(rot, t):
        return (2 * rot + t // 4) * 512 + (t % 4) * 65

    def emit_norm_vec(j):
        rot = j % 2

        def _vec(e, j=j, rot=rot):
            e.wait_ge(s_t.h, pv_done[j])
            for b in range(2):
                bank = (2 * rot + b) * 512
                e.reciprocal(
                    rcp[:, rot * 8 + b * 4: rot * 8 + b * 4 + 4].rearrange(
                        "p (t c) -> p t c", c=1),
                    psB[:, bank: bank + 260].rearrange(
                        "p (t c) -> p t c", c=65)[:, :, 64:65])
            e.drain()
            for t in range(8):
                h2, qb = t // 4, t % 4
                i = e.tensor_scalar(
                    x_att_tm[:, rot * 512 + qb * 128 + h2 * 64:
                             rot * 512 + qb * 128 + h2 * 64 + 64],
                    psB[:, pvoff(rot, t): pvoff(rot, t) + 64],
                    rcp[:, rot * 8 + t: rot * 8 + t + 1], None, ALU.mult)
            s_v.inc(i)
            norm_mark[j] = s_v.n
        em("vector", _vec)

    def emit_trans(j):
        rot = j % 2

        def _tens(e, j=j, rot=rot):
            e.wait_ge(s_v.h, norm_mark[j])
            for qb in range(4):
                dst = (2 * rot + qb // 2) * 512 + (qb % 2) * 128
                i = nc.tensor.transpose(
                    psB[:, dst: dst + 128],
                    x_att_tm[:, rot * 512 + qb * 128: rot * 512 + qb * 128 + 128],
                    ident[:])
            s_t.inc(i)
            trans_mark[j] = s_t.n
        em("tensor", _tens)

    def emit_evac(j):
        rot = j % 2
        pj, qsj = j // NQS, j % NQS

        def _vec(e, j=j, rot=rot, pj=pj, qsj=qsj):
            e.wait_ge(s_t.h, trans_mark[j])
            c0 = pj * T + qsj * 512
            i = e.tensor_copy(
                attn_bf[:, c0:c0 + 512].rearrange("p (b c) -> p b c", c=256),
                psB[:, 2 * rot * 512: 2 * rot * 512 + 1024].rearrange(
                    "p (b c) -> p b c", c=512)[:, :, 0:256])
            s_v.inc(i)
            evac_mark[j] = s_v.n
        em("vector", _vec)

    wpj_cast = []
    for cb in range(NCB):
        buf = cb % 2
        sidx = len(wst_chain)

        def _sync(e, cb=cb, buf=buf, sidx=sidx):
            wst_wait(e, sidx)
            s_w.dinc(e.dma_start(out=wst[:, buf * D:(buf + 1) * D], in_=wpj_h[cb]))
        em("sync", _sync)

        def _vec(e, cb=cb, buf=buf):
            e.wait_ge(s_w.h, s_w.n)
            i = e.tensor_copy(wpj_bf[:, cb * D:(cb + 1) * D],
                              wst[:, buf * D:(buf + 1) * D])
            s_f.inc(i)
            wpj_cast.append(s_f.n)
        em("vector", _vec)
        wst_chain.append((s_f, s_f.n))
    WPJ_F = s_f.n

    wcast_mark = {}
    w1_dma_mark = {}
    W1_PER_PAIR = NHB // (H // 2) if W1_OVERLAP else 0

    w1_chain_idx = {}

    def emit_w1dma_p2(j):
        sidx = len(wst_chain)
        w1_chain_idx[j] = sidx

        def _sync(e, j=j, sidx=sidx):
            wst_wait(e, sidx)
            s_w.dinc(e.dma_start(out=wst[:, (j % 2) * D:(j % 2 + 1) * D],
                                 in_=w1_h[j]))
            w1_dma_mark[j] = s_w.n
        em("sync", _sync)
        wst_chain.append(None)

    def emit_w1cast_p2(j, pair_gate):
        def _vec(e, j=j, pair_gate=pair_gate):
            e.wait_ge(s_w.h, w1_dma_mark[j])
            e.wait_ge(s_t.h, pair_gate)
            i = e.tensor_copy(w1_e5[:, j * D:(j + 1) * D],
                              wst[:, (j % 2) * D:(j % 2 + 1) * D])
            s_v.inc(i)
            wcast_mark[('w1', j)] = s_v.n
            wst_chain[w1_chain_idx[j]] = (s_v, s_v.n)
        em("vector", _vec)

    P2_S0 = s_s.n

    def exp_mark(it, kc):
        return P2_S0 + it * NKC + kc + 1
    if W1_OVERLAP:
        emit_w1dma_p2(0)
        emit_w1dma_p2(1)
    w1_pending = []

    def emit_w1block(base, gate):
        emit_w1cast_p2(base, gate)
        emit_w1cast_p2(base + 1, gate)
        for j in (base + 2, base + 3):
            if j < NHB:
                emit_w1dma_p2(j)
        if base + 2 < NHB:
            emit_w1cast_p2(base + 2, gate)
        if base + 3 < NHB:
            emit_w1cast_p2(base + 3, gate)
        for j in (base + 4, base + 5):
            if j < NHB:
                emit_w1dma_p2(j)
    NG = (H // 2) * NQS * NKC

    def emit_scores_g(g):
        it_, kc = divmod(g, NKC)
        p_, qs_ = divmod(it_, NQS)
        rot = g % 2
        scA = psA[:, rot * 1024: rot * 1024 + 512]
        scB = psA[:, rot * 1024 + 512: rot * 1024 + 1024]

        def _tens(e, g=g, it_=it_, kc=kc, p_=p_, qs_=qs_, scA=scA, scB=scB):
            if g == 0:
                e.wait_ge(s_v.h, ONES_V)
                e.wait_ge(s_f.h, P1_F)
            if g >= 2:
                e.wait_ge(s_s.h, P2_S0 + g - 1)
            nc.tensor.matmul(
                scA, k_bf[0:64, p_ * T + kc * 128: p_ * T + (kc + 1) * 128],
                q_bf[0:64, p_ * T + qs_ * 512: p_ * T + (qs_ + 1) * 512],
                start=True, stop=True, tile_position=(0, 0))
            i = nc.tensor.matmul(
                scB, k_bf[64:128, p_ * T + kc * 128: p_ * T + (kc + 1) * 128],
                q_bf[64:128, p_ * T + qs_ * 512: p_ * T + (qs_ + 1) * 512],
                start=True, stop=True, tile_position=(64, 0))
            s_t.inc(i)
        em("tensor", _tens)
        it_, kc = divmod(g, NKC)
        sc_marks[(it_, kc)] = s_t.n

    emit_scores_g(0)
    emit_scores_g(1)
    for p in range(H // 2):
        for qs in range(NQS):
            it = p * NQS + qs

            def emit_pv(kc, p=p, it=it):
                rot3 = (it * NKC + kc) % 4
                rot = it % 2

                def _tens2(e, kc=kc, rot3=rot3, rot=rot):
                    e.wait_ge(s_s.h, exp_mark(it, kc))
                    if kc == 0 and it >= 2:
                        e.wait_ge(s_v.h, evac_mark[it - 2])
                    for t in range(8):
                        h2, qb = t // 4, t % 4
                        i = nc.tensor.matmul(
                            psB[:, pvoff(rot, t): pvoff(rot, t) + 65],
                            probs[:, rot3 * 1024 + h2 * 512 + qb * 128:
                                  rot3 * 1024 + h2 * 512 + qb * 128 + 128],
                            vt_bf[:, kc * (65 * H) + (2 * p + h2) * 65:
                                  kc * (65 * H) + (2 * p + h2) * 65 + 65],
                            start=(kc == 0 and t % 4 == 0),
                            stop=(kc == NKC - 1 and t % 4 == 3))
                    s_t.inc(i)
                    pv_pair_mark[(it, kc)] = s_t.n
                em("tensor", _tens2)

            if it >= 1:
                emit_norm_vec(it - 1)
            for kc in range(NKC):
                if kc == 2 and it >= 1:
                    emit_trans(it - 1)
                    emit_evac(it - 1)
                if kc == 3 and w1_pending:
                    emit_w1block(*w1_pending.pop(0))
                g = it * NKC + kc + 2
                if g < NG:
                    emit_scores_g(g)
                emit_pv(kc)
            pv_done[it] = s_t.n

            if W1_OVERLAP and qs == NQS - 1:
                w1_pending.append((W1_PER_PAIR * p, sc_marks[(it, NKC - 1)]))
    li = (H // 2) * NQS - 1
    emit_norm_vec(li)
    emit_trans(li)
    emit_evac(li)
    P2_V_CORE = s_v.n
    while w1_pending:
        emit_w1block(*w1_pending.pop(0))

    def _p2_scal(e):
        for it in range((H // 2) * NQS):
            for kc in range(NKC):
                rot2 = kc % 2
                rot3 = (it * NKC + kc) % 4
                e.wait_ge(s_t.h, sc_marks[(it, kc)])
                pit, pkc = (it, kc - 4) if kc >= 4 else (it - 1, NKC - 4 + kc)
                if (pit, pkc) in pv_pair_mark:
                    e.wait_ge(s_t.h, pv_pair_mark[(pit, pkc)])
                i = e.activation(probs[:, rot3 * 1024:(rot3 + 1) * 1024],
                                 psA[:, rot2 * 1024:(rot2 + 1) * 1024],
                                 AF.Exp, bias=-C_SHIFT, scale=SCALE)
                s_s.inc(i)
                assert s_s.n == exp_mark(it, kc)
    em("scalar", _p2_scal)

    P2_V = s_v.n
    P2_T = s_t.n

    esC2.close()
    esC.close()
    esG = ExitStack()
    h_e5 = sb(esG, "h_e5", [128, NHB * 512], F8E5)
    h_bfst = sb(esG, "h_bfst", [128, 4 * 512], BF16)
    y2_bf = sb(esG, "y2_bf", [128, D], BF16)
    y2_f32 = sb(esG, "y2_f32", [128, D], F32)
    mask_st = sb(esG, "mask_st", [128, 2 * D], BF16)
    out_st = sb(esG, "out_st", [128, 2 * D], F32)
    esF = ExitStack()
    if W1_OVERLAP:
        wst2 = sb(esF, "wst2", [128, 2 * D], F32)
        wst2b = wst
    else:
        wst2 = sb(esF, "wst2", [128, 4 * D], F32)
        wst2b = None

    def wst2_buf(i_):
        r = i_ % 4
        if W1_OVERLAP:
            t_ = wst2 if r < 2 else wst2b
            return t_[:, (r % 2) * D:(r % 2 + 1) * D]
        return wst2[:, r * D:(r + 1) * D]
    if W1_OVERLAP:
        assert nc.sbuf_base <= addr["k_bf"][0], (nc.sbuf_base, addr)
    else:
        w1_e5 = sb(esF, "w1_e5", [128, NHB * D], F8E5)
        w2_e5 = sb(esF, "w2_e5", [128, NHB * D], F8E5)

    NSTR = T // 512
    h_cast = {}
    P45_S0 = s_s.n

    def cmark(idx):
        return P45_S0 + idx + 1
    out_done = []
    f2_t_mark = []
    f1_marks = {}
    P2_S_END = s_s.n

    s_gs = [Sem(sems, nc, "s_g%d" % i) for i in range(4)]
    s_g = s_gs[0]
    w_dma_mark = {}

    def emit_w1dma(i_):
        def _sync(e, i_=i_):
            if i_ < 2:
                sem, val = wst_chain[len(wst_chain) - 2 + i_]
                e.wait_ge(sem.h, val)
            else:
                e.wait_ge(s_v.h, wcast_mark[('w1', i_ - 2)])
            s_w.dinc(e.dma_start(out=wst[:, (i_ % 2) * D:(i_ % 2 + 1) * D],
                                 in_=w1_h[i_]))
            w_dma_mark[('w1', i_)] = (s_w, s_w.n)
        em("sync", _sync)

    def emit_w2dma(i_):
        def _gp(e, i_=i_):
            if i_ < 4:
                e.wait_ge(s_t.h, P2_T)
                if W1_OVERLAP and i_ >= 2:
                    e.wait_ge(s_v.h, wcast_mark[('w1', NHB - 2 + (i_ - 2))])
            else:
                e.wait_ge(s_v.h, wcast_mark[('w2', i_ - 4)])
            ch = s_gs[i_ % 4]
            ch.dinc(e.dma_start(out=wst2_buf(i_), in_=w2_h[i_]))
            w_dma_mark[('w2', i_)] = (ch, ch.n)
        em("gpsimd", _gp)

    def emit_wcast(e, kind, i_):
        sem, val = w_dma_mark[(kind, i_)]
        e.wait_ge(sem.h, val)
        if kind == 'w1':
            dst = w1_e5[:, i_ * D:(i_ + 1) * D]
            srcb = wst[:, (i_ % 2) * D:(i_ % 2 + 1) * D]
        else:
            dst = w2_e5[:, (i_) * D:(i_ + 1) * D]
            srcb = wst2_buf(i_)
        s_v.inc(e.tensor_copy(dst, srcb))
        wcast_mark[(kind, i_)] = s_v.n

    for i_ in (0, 1):
        if not W1_OVERLAP:
            emit_w1dma(i_)
    for i_ in range(min(4, NHB)):
        emit_w2dma(i_)

    def _wvec0(e):
        e.wait_ge(s_t.h, P2_T)
        if not W1_OVERLAP:
            emit_wcast(e, 'w1', 0)
            emit_wcast(e, 'w1', 1)
        emit_wcast(e, 'w2', 0)
        emit_wcast(e, 'w2', 1)
    em("vector", _wvec0)

    def _p45_scal(e):
        for ts_ in range(NSTR):
            for hb_ in range(NHB):
                rot_ = hb_ % 4
                idx_ = ts_ * NHB + hb_
                e.wait_ge(s_v.h, h_cast[idx_])
                i = e.activation(h_e5[:, hb_ * 512:(hb_ + 1) * 512],
                                 h_bfst[:, rot_ * 512:(rot_ + 1) * 512], AF.Copy)
                s_s.inc(i)
                assert s_s.n == cmark(idx_)

    last_relu_of_stripe = {}
    for ts in range(NSTR):
        for hb in range(NHB):
            rot = hb % 4
            idx = ts * NHB + hb

            if ts == 0:
                if not W1_OVERLAP and hb + 2 < NHB:
                    emit_w1dma(hb + 2)
                if hb + 4 < NHB:
                    emit_w2dma(hb + 4)

            def _tens(e, ts=ts, hb=hb, rot=rot, idx=idx):
                if idx == 0:
                    e.wait_ge(s_s.h, P2_S_END)
                    e.wait_ge(s_v.h, P2_V_CORE)
                if ts == 0:
                    e.wait_ge(s_v.h, wcast_mark[('w1', hb)])
                if hb == 0 and ts >= 1:
                    e.wait_ge(s_v.h, out_done[-2])
                if hb == 2 and ts >= 1:
                    e.wait_ge(s_v.h, out_done[-1])
                if idx >= 4:
                    e.wait_ge(s_v.h, h_cast[idx - 4])
                for cp in range(NCB // 2):
                    i = nc.tensor.matmul(
                        psA[:, rot * 512:(rot + 1) * 512],
                        w1_e5[:, hb * D + 2 * cp * 128: hb * D + (2 * cp + 2) * 128]
                            .rearrange("p (k m) -> p k m", k=2),
                        xn_e5_v[:, 2 * cp:2 * cp + 2, ts * 512:(ts + 1) * 512],
                        start=(cp == 0), stop=(cp == NCB // 2 - 1),
                        perf_mode=mybir.MatmulPerfMode.DoubleRow)
                    if cp == NCB // 2 - 1:
                        s_t.inc(i)
            em("tensor", _tens)
            f1_marks[idx] = s_t.n

            def _vec(e, ts=ts, hb=hb, rot=rot, idx=idx):
                e.wait_ge(s_t.h, f1_marks[idx])
                if ts >= 1 and hb == 0:
                    e.wait_ge(s_t.h, f2_t_mark[ts - 1])
                if idx >= 4:
                    e.wait_ge(s_s.h, cmark(idx - 4))
                i = e.tensor_scalar(h_bfst[:, rot * 512:(rot + 1) * 512],
                                    psA[:, rot * 512:(rot + 1) * 512],
                                    b1q[:, hb:hb + 1], 0.0, ALU.add, ALU.max)
                s_v.inc(i)
                h_cast[idx] = s_v.n
                if ts == 0 and hb + 2 < NHB:
                    if not W1_OVERLAP:
                        emit_wcast(e, 'w1', hb + 2)
                    emit_wcast(e, 'w2', hb + 2)
            em("vector", _vec)
        H_STRIPE_V = s_v.n
        last_relu_of_stripe[ts] = h_cast[ts * NHB + NHB - 1]

        for tj in range(4):
            tb = ts * 4 + tj
            buf = tb % 2
            pb = tb % 2

            def _sync(e, tb=tb, buf=buf):
                if tb < 2:
                    e.wait_ge(s_v.h, P2_V)
                else:
                    e.wait_ge(s_v.h, out_done[tb - 2])
                s_x.dinc(e.dma_start(out=x_st[:, buf * D:(buf + 1) * D],
                                    in_=x_h[tb * 128:(tb + 1) * 128, :]))
                s_x.dinc(e.dma_start(out=mask_st[:, buf * D:(buf + 1) * D],
                                    in_=mask_h[tb * 128:(tb + 1) * 128, :]))
            em("sync", _sync)

            def _tens(e, ts=ts, tj=tj, tb=tb, pb=pb):
                if tb < 2:
                    e.wait_ge(s_f.h, WPJ_F)
                if tj <= 1:
                    e.wait_ge(s_v.h, last_relu_of_stripe[ts])
                if tb >= 2:
                    e.wait_ge(s_v.h, out_done[tb - 2])
                for jb in range(2):
                    for cb in range(NCB):
                        i = nc.tensor.matmul(
                            psA[:, pb * 1024 + jb * 512: pb * 1024 + (jb + 1) * 512],
                            attn_bf[:, cb * T + tb * 128: cb * T + (tb + 1) * 128],
                            wpj_bf[:, cb * D + jb * 512: cb * D + (jb + 1) * 512],
                            start=(cb == 0), stop=(cb == NCB - 1))
                e.wait_ge(s_s.h, cmark(ts * NHB + NHB - 1))
                h_v = h_e5[:].rearrange("p (h t) -> p h t", t=512)
                w2_v = w2_e5[:].rearrange("p (h j) -> p h j", j=D)
                for ns in range(2):
                    for hp in range(NHB // 2):
                        i = nc.tensor.matmul(
                            psB[:, pb * 1024 + ns * 512: pb * 1024 + (ns + 1) * 512],
                            h_v[:, 2 * hp:2 * hp + 2, tj * 128:(tj + 1) * 128],
                            w2_v[:, 2 * hp:2 * hp + 2, ns * 512:(ns + 1) * 512],
                            start=(hp == 0), stop=(hp == NHB // 2 - 1),
                            perf_mode=mybir.MatmulPerfMode.DoubleRow)
                        if hp == NHB // 2 - 1 and ns == 1:
                            s_t.inc(i)
            em("tensor", _tens)
            f2_mark = s_t.n
            if tj == 3:
                f2_t_mark.append(f2_mark)

            def _vec(e, tb=tb, pb=pb, buf=buf, f2_mark=f2_mark):
                e.wait_ge(s_t.h, f2_mark)
                e.wait_ge(s_x.h, s_x.n)
                if tb >= 2:
                    e.wait_ge(s_o.h, 16 * (tb - 1))
                dst = out_st[:, buf * D:(buf + 1) * D]
                e.tensor_tensor(dst, psA[:, pb * 1024:(pb + 1) * 1024], bp_sb[:], ALU.add)
                e.drain()
                e.tensor_tensor(dst, dst, x_st[:, buf * D:(buf + 1) * D], ALU.add)
                e.tensor_tensor(y2_bf[:], psB[:, pb * 1024:(pb + 1) * 1024], b2q[:], ALU.add)
                e.drain()
                e.tensor_tensor(y2_f32[:], y2_bf[:], mask_st[:, buf * D:(buf + 1) * D],
                                ALU.mult)
                e.drain()
                i = e.tensor_tensor(dst, dst, y2_f32[:], ALU.add)
                s_v.inc(i)
                out_done.append(s_v.n)
            em("vector", _vec)
            st_mark = s_v.n

            def _gp(e, tb=tb, buf=buf, st_mark=st_mark):
                e.wait_ge(s_v.h, st_mark)
                s_o.dinc(e.dma_start(out=out_h[tb * 128:(tb + 1) * 128, :],
                                    in_=out_st[:, buf * D:(buf + 1) * D]))
            em("gpsimd", _gp)


    em("scalar", _p45_scal)

    def _gp_final(e):
        e.wait_ge(s_o.h, s_o.n)
    em("gpsimd", _gp_final)

    for st in (esF, esG, esD, esB):
        st.close()
    es.close()
    sems.close()
    return nc


def _host_mask(T, Dm):
    import jax
    if os.environ.get("BASS_SIM_FP64") == "1":
        jax.config.update("jax_enable_x64", True)
    cpu = jax.devices("cpu")[0]
    with jax.default_device(cpu):
        keep = jax.random.bernoulli(
            jax.random.key(42, impl="threefry2x32"), 0.8, (8, T, Dm))
    keep = np.asarray(keep)
    return np.where(keep, np.float32(1.25), np.float32(0.0)).astype(ml_dtypes.bfloat16)


def _prep_inputs(L0_attn_x, L0_attn_w_qkv, L0_attn_w_proj, L0_attn_b_proj,
                 L0_ffn_W1, L0_ffn_b1, L0_ffn_W2, L0_ffn_b2):
    B, T, _ = L0_attn_x.shape
    w_qkv = np.asarray(L0_attn_w_qkv, dtype=np.float32)
    w_proj = np.asarray(L0_attn_w_proj, dtype=np.float32)
    W1 = np.asarray(L0_ffn_W1, dtype=np.float32)
    W2 = np.asarray(L0_ffn_W2, dtype=np.float32)

    wqk = np.ascontiguousarray(
        w_qkv[:2 * D].reshape(H, 128, NCB, 128).transpose(0, 3, 2, 1)).reshape(H, 128, D)
    wv = np.ascontiguousarray(w_qkv[2 * D:].T).reshape(NCB, 128, D)
    wpj = np.ascontiguousarray(w_proj.T).reshape(NCB, 128, D)
    w1 = np.ascontiguousarray(
        np.ascontiguousarray(W1.T).reshape(NCB, 128, NHB, 128).transpose(2, 1, 0, 3)
    ).reshape(NHB, 128, D)
    w2 = np.ascontiguousarray(W2.T).reshape(NHB, 128, D)
    b1t = np.ascontiguousarray(np.asarray(L0_ffn_b1, dtype=np.float32).reshape(NHB, 128).T)
    b2t = np.ascontiguousarray(
        np.broadcast_to(np.asarray(L0_ffn_b2, dtype=np.float32), (128, D)))
    bpt = np.ascontiguousarray(
        np.broadcast_to(np.asarray(L0_attn_b_proj, dtype=np.float32), (128, D)))
    mask = _host_mask(T, D)

    in_maps = []
    for b in range(B):
        in_maps.append({
            "x": np.ascontiguousarray(np.asarray(L0_attn_x[b], dtype=np.float32)),
            "mask": np.ascontiguousarray(mask[b]),
            "wqk": wqk, "wv": wv, "wpj": wpj, "w1": w1, "w2": w2,
            "b1t": b1t, "b2t": b2t, "bpt": bpt,
        })
    return in_maps, T


def kernel(L0_attn_x, L0_attn_w_qkv, L0_attn_w_proj, L0_attn_b_proj,
           L0_ffn_W1, L0_ffn_b1, L0_ffn_W2, L0_ffn_b2):
    in_maps, T = _prep_inputs(L0_attn_x, L0_attn_w_qkv, L0_attn_w_proj,
                              L0_attn_b_proj, L0_ffn_W1, L0_ffn_b1,
                              L0_ffn_W2, L0_ffn_b2)
    nc = build_nc(T=T)
    res = run_bass_kernel_spmd(nc, in_maps, core_ids=list(range(len(in_maps))))
    return np.stack([np.asarray(r["out"], dtype=np.float32) for r in res.results], axis=0)
